# version 23
# baseline (speedup 1.0000x reference)
import numpy as np
import ml_dtypes
from contextlib import ExitStack

import concourse.bass as bass
import concourse.mybir as mybir
from concourse.bass_utils import run_bass_kernel_spmd

F32 = mybir.dt.float32
BF16 = mybir.dt.bfloat16
AF = mybir.ActivationFunctionType
ALU = mybir.AluOpType
AX = mybir.AxisListType

P = 128
S = 4096
C = 1024
H = 4096
NE = 16
R = 2048
NCC = C // P
NKC = H // P
NBB = 4
TCW = 256
NTC = R // TCW
NSC = S // P
A_BOUND = 5.0
EPS = 1e-5

ENGINES = ("sync", "scalar", "vector", "tensor", "gpsimd")


class Sched:

    def __init__(self, nc, ctx):
        self.nc = nc
        self.ctx = ctx
        self.q = {e: [] for e in ENGINES}
        self.sem = {}
        self.cnt = {}
        for name in ("T", "V", "SC"):
            self.add_sem(name)

    def add_sem(self, name):
        self.sem[name] = self.ctx.enter_context(self.nc.semaphore(name))
        self.cnt[name] = 0

    def at(self, eng, builder, waits=(), incs=()):
        incs2 = []
        for s, amt in incs:
            self.cnt[s] += amt
            incs2.append((s, amt))
        self.q[eng].append((builder, tuple(waits), tuple(incs2)))

    def mark(self, s):
        return self.cnt[s]

    def emit(self, block):
        for eng in ENGINES:
            ops = self.q[eng]
            if not ops:
                continue
            waited = {}

            def body(e, ops=ops, waited=waited):
                for builder, waits, incs in ops:
                    for s, v in waits:
                        if v > 0 and waited.get(s, 0) < v:
                            e.wait_ge(self.sem[s], v)
                            waited[s] = v
                    ins = builder(e) if builder is not None else None
                    for s, amt in incs:
                        ins.then_inc(self.sem[s], amt)

            getattr(block, eng)(body)
        self.q = {e: [] for e in ENGINES}


class Chan:

    def __init__(self, bd, name, n=2, serialize=False):
        self.bd = bd
        self.n = n
        self.serialize = serialize
        self.names = []
        for i in range(n):
            nm = f"{name}{i}"
            bd.add_sem(nm)
            self.names.append(nm)
        self.i = 0
        self.last = None

    def issue(self, eng, builder, waits=()):
        return self.issue_group(eng, [builder], waits)

    def issue_group(self, eng, builders, waits=()):
        nm = self.names[self.i % self.n]
        w = list(waits)
        if self.serialize and self.last is not None:
            w.append(self.last)
        for bi, b in enumerate(builders):
            self.bd.at(eng, b, waits=w if bi == 0 else (), incs=[(nm, 16)])
        self.i += 1
        spec = (nm, self.bd.cnt[nm])
        self.last = spec
        return spec

    def done(self):
        return [(nm, self.bd.cnt[nm]) for nm in self.names]


def build_nc(debug=False):
    nc = bass.Bass("TRN2", target_bir_lowering=False)

    x_d = nc.dram_tensor("x_perm", [S, C], F32, kind="ExternalInput").ap()
    wqT_d = nc.dram_tensor("wqT", [C, C], F32, kind="ExternalInput").ap()
    wkT_d = nc.dram_tensor("wkT", [C, C], F32, kind="ExternalInput").ap()
    wvT_d = nc.dram_tensor("wvT", [C, C], F32, kind="ExternalInput").ap()
    w1T_d = nc.dram_tensor("w1T", [C, H], F32, kind="ExternalInput").ap()
    w2T_d = nc.dram_tensor("w2T", [H, C], F32, kind="ExternalInput").ap()
    wr_d = nc.dram_tensor("wr", [C, NE], F32, kind="ExternalInput").ap()
    br_d = nc.dram_tensor("br", [1, NE], F32, kind="ExternalInput").ap()
    b1_d = nc.dram_tensor("b1", [1, H], F32, kind="ExternalInput").ap()
    b2_d = nc.dram_tensor("b2", [1, C], F32, kind="ExternalInput").ap()
    gam_d = nc.dram_tensor("gam", [1, 1], F32, kind="ExternalInput").ap()
    pat_d = nc.dram_tensor("pat", [NE, H], BF16, kind="ExternalInput").ap()
    id_d = nc.dram_tensor("ident", [P, P], BF16, kind="ExternalInput").ap()

    y_d = nc.dram_tensor("y", [R, C], F32, kind="ExternalOutput").ap()

    scr_kind = "ExternalOutput" if debug else "Internal"
    xntok_d = nc.dram_tensor("xntok_scr", [S, C], BF16, kind=scr_kind).ap()
    gate_d = nc.dram_tensor("gate_scr", [R, C], BF16, kind=scr_kind).ap()

    dbg = {}
    if debug:
        for name, shape, dt in [
            ("dbg_crow", [1, R], BF16), ("dbg_f", [P, NCC * S], BF16),
            ("dbg_g", [P, NCC * R], BF16), ("dbg_E", [P, NCC * TCW], BF16),
            ("dbg_scale", [P, 2], F32),
            ("dbg_m", [P, 40], F32), ("dbg_rstd", [P, 20], F32),
            ("dbg_lgs", [P, 64], F32), ("dbg_xnsb", [P, NSC * C], BF16),
        ]:
            dbg[name] = nc.dram_tensor(name, shape, dt, kind="ExternalOutput").ap()

    with ExitStack() as top:
        sb = lambda n, s, d: top.enter_context(nc.sbuf_tensor(n, s, d))
        bd = Sched(nc, top)
        block = top.enter_context(nc.Block())

        T, V, SC = "T", "V", "SC"
        ch_wB = Chan(bd, "cW", 4)
        ch_aux = Chan(bd, "cAX", 2)
        ch_x = Chan(bd, "cX", 4)
        ch_tok = Chan(bd, "cT", 4)
        ch_big = Chan(bd, "cB", 2)
        ch_stgB = Chan(bd, "cSB", 2)
        ch_gate = Chan(bd, "cG", 2)
        ch_stgC = Chan(bd, "cSC", 2)
        ch_xt = Chan(bd, "cXT", 2)
        ch_gt = Chan(bd, "cGT", 2)
        ch_y = Chan(bd, "cY", 2)

        onescol_b = sb("onescol_b", [P, 1], BF16)
        onescol_f = sb("onescol_f", [P, 1], F32)
        ones1_b = sb("ones1_b", [1, P], BF16)
        one11_f = sb("one11_f", [1, 1], F32)
        gam_sb = sb("gam_sb", [1, 1], F32)
        epsb = sb("epsb", [P, 1], F32)
        ms_store = sb("ms_store", [P, 40], F32)
        rstd_store = sb("rstd_store", [P, 20], F32)

        with ExitStack() as sAB:
            sbAB = lambda n, s, d: sAB.enter_context(nc.sbuf_tensor(n, s, d))
            psAB = lambda n, s, d: sAB.enter_context(nc.psum_tensor(n, s, d))
            w1_sb = sbAB("w1_sb", [P, NCC * H], BF16)
            w2_sb = sbAB("w2_sb", [P, NKC * C], BF16)
            ident = sbAB("identc", [P, P], BF16)
            w1aug = sbAB("w1aug", [32, H], BF16)
            wr_sb = sbAB("wr_sb", [P, NCC * NE], BF16)
            br_bf = sbAB("br_bf", [1, NE], BF16)
            stg = [sbAB(f"stgB{i}", [P, NCC * 512], BF16) for i in range(2)]
            h1m = sbAB("h1m", [P, NKC * 512], BF16)
            xaug = [sbAB(f"xaug{i}", [32, 512], BF16) for i in range(2)]
            lgs = sbAB("lgs", [P, 4 * NE], F32)
            t8 = sbAB("t8", [P, 8], F32)
            pack = sbAB("pack", [P, 32], F32)
            pk_b = sbAB("pk_b", [P, 32], BF16)
            h1_ps = [psAB(f"h1_ps{i}", [P, 512], F32) for i in range(3)]
            y_ps = [psAB(f"y_ps{i}", [P, 512], F32) for i in range(2)]
            lg_ps = psAB("lg_ps", [P, 4 * NE], F32)
            aug_ps = psAB("aug_ps", [32, P], BF16)

            x_spec = [None] * 32
            tok_spec = [None] * 32
            v_xn = [0] * 32
            v_bn2 = [0] * 32
            sc_sd = [0] * 32
            g_w1 = [None] * 4
            stg_spec = [None] * NBB
            t_h1 = [[0] * NKC for _ in range(NBB)]
            sc_relu = [[0] * NKC for _ in range(NBB)]
            t_y = [[0] * 8 for _ in range(NBB)]
            sc_silu = [[0] * 8 for _ in range(NBB)]
            gate_spec = [[None] * 4 for _ in range(NBB)]
            t_lg = [0] * NBB
            v_lgs = [0] * NBB
            t_aug = [[0] * 4 for _ in range(NBB)]
            v_xaug = [[0] * 4 for _ in range(NBB)]

            with ExitStack() as sA:
                sbA = lambda n, s, d: sA.enter_context(nc.sbuf_tensor(n, s, d))
                xt = [sbA(f"xt{i}", [P, C], F32) for i in range(4)]
                xnb = [sbA(f"xnb{i}", [P, C], BF16) for i in range(2)]
                bnbuf = sbA("bnbuf", [P, 12], F32)
                sd_t = sbA("sd_t", [P, 4], F32)

                bd.at("scalar", lambda e: e.memzero(w1aug[:]), incs=[(SC, 1)])
                sc_w1aug0 = bd.mark(SC)
                bd.at("vector", lambda e: e.memset(onescol_b[:], 1.0))
                bd.at("vector", lambda e: e.memset(onescol_f[:], 1.0))
                bd.at("vector", lambda e: e.memset(ones1_b[:], 1.0))
                bd.at("vector", lambda e: e.memset(one11_f[:], 1.0))
                bd.at("vector", lambda e: e.memset(epsb[:], EPS))
                bd.at("vector", lambda e: e.memset(pack[:, 0:1], 1.0))
                bd.at("vector", lambda e: e.memset(pack[:, NE + 1:32], 0.0), incs=[(V, 1)])

                sp_id = ch_aux.issue("scalar", lambda e: e.dma_start(out=ident[:], in_=id_d))
                sp_pat = ch_aux.issue("scalar", lambda e: e.dma_start(
                    out=w1aug[1:NE + 1, :], in_=pat_d), waits=[(SC, sc_w1aug0)])
                def w1_issue(q):
                    g_w1[q] = ch_wB.issue("gpsimd", lambda e, q=q: e.dma_start(
                        out=w1_sb[:].rearrange("p (cc h) -> p cc h", h=H)[:, :, 1024 * q:1024 * (q + 1)],
                        in_=w1T_d.rearrange("(cc p) h -> p cc h", p=P)[:, :, 1024 * q:1024 * (q + 1)]))
                w1_issue(0)
                sp_wr = ch_wB.issue("gpsimd", lambda e: e.dma_start(
                    out=wr_sb[:].rearrange("p (cc e) -> p cc e", e=NE),
                    in_=wr_d.rearrange("(cc p) e -> p cc e", p=P)))
                sp_br = ch_wB.issue("gpsimd", lambda e: e.dma_start(out=br_bf[:], in_=br_d))
                g_consts = ch_wB.issue("gpsimd", lambda e: e.dma_start(out=gam_sb[:], in_=gam_d))
                sp_b1 = ch_wB.issue("gpsimd", lambda e: e.dma_start(
                    out=w1aug[0:1, :], in_=b1_d), waits=[(SC, sc_w1aug0)])
                w1_issue(1)
                w1_issue(2)
                w1_issue(3)
                g_w2q = [None] * 4
                for q2 in range(4):
                    g_w2q[q2] = ch_wB.issue("gpsimd", lambda e, q2=q2: e.dma_start(
                        out=w2_sb[:].rearrange("p (kc c) -> p kc c", c=C)[:, 8 * q2:8 * (q2 + 1), :],
                        in_=w2T_d.rearrange("(kc p) c -> p kc c", p=P)[:, 8 * q2:8 * (q2 + 1), :]))

                def emit_stats(j):
                    jb = j % 4
                    xw = [(V, v_xn[j - 4])] if j >= 4 else []
                    xeng = "sync" if j < 22 else "scalar"
                    x_spec[j] = ch_x.issue(xeng, lambda e, j=j, jb=jb: e.dma_start(
                        out=xt[jb][:], in_=x_d[P * j:P * (j + 1), :]), waits=xw)
                    jc = j if j < 16 else 16 + (j % 4)
                    bd.at("vector", lambda e, jb=jb: e.bn_stats(bnbuf[:, 0:6], xt[jb][:, 0:512]),
                          waits=[x_spec[j]])
                    bd.at("vector", lambda e, jb=jb: e.bn_stats(bnbuf[:, 6:12], xt[jb][:, 512:1024]),
                          incs=[(V, 1)])
                    v_bn1 = bd.mark(V)
                    aw = [(V, v_bn1)]
                    if j >= 20:
                        aw.append((SC, sc_sd[j - 4]))
                    bd.at("vector", lambda e, jc=jc: e.bn_aggr(
                        ms_store[:, 2 * jc:2 * jc + 2], bnbuf[:]),
                        waits=aw, incs=[(V, 1)])
                    v_bn2[j] = bd.mark(V)
                    bd.at("scalar", lambda e, jc=jc, j=j: e.activation(
                        sd_t[:, j % 4:j % 4 + 1], ms_store[:, 2 * jc + 1:2 * jc + 2],
                        AF.Sqrt, bias=epsb[:]),
                        waits=[(V, v_bn2[j])], incs=[(SC, 1)])
                    sc_sd[j] = bd.mark(SC)

                def emit_norm(j):
                    jb = j % 4
                    jc = j if j < 16 else 16 + (j % 4)
                    bd.at("vector", lambda e, jc=jc, j=j: e.reciprocal(
                        rstd_store[:, jc:jc + 1], sd_t[:, j % 4:j % 4 + 1]),
                        waits=[(SC, sc_sd[j])], incs=[(V, 1)])
                    v_recip = bd.mark(V)
                    xnw = [(V, v_recip)]
                    if j >= 2:
                        xnw += [tok_spec[j - 2]]
                    bd.at("vector", lambda e, jc=jc, jb=jb, j=j: e.tensor_scalar(
                        xnb[j % 2][:], xt[jb][:], ms_store[:, 2 * jc:2 * jc + 1],
                        rstd_store[:, jc:jc + 1],
                        ALU.subtract, ALU.mult), waits=xnw, incs=[(V, 1)])
                    v_xn[j] = bd.mark(V)
                    tok_spec[j] = ch_tok.issue("scalar", lambda e, j=j: e.dma_start(
                        out=xntok_d[P * j:P * (j + 1), :], in_=xnb[j % 2][:]),
                        waits=[(V, v_xn[j])])

                def emit_stageB(nb):
                    nbb = nb % 2
                    stw = [tok_spec[4 * nb + i] for i in range(4)]
                    if nb >= 2:
                        stw.append((T, t_h1[nb - 2][NKC - 1]))
                    builders = []
                    for cc in range(NCC):
                        builders.append(lambda e, nb=nb, nbb=nbb, cc=cc: e.dma_start(
                            out=stg[nbb][:, 512 * cc:512 * (cc + 1)],
                            in_=xntok_d[512 * nb:512 * (nb + 1), P * cc:P * (cc + 1)],
                            transpose=True))
                    stg_spec[nb] = ch_stgB.issue_group("sync", builders, waits=stw)

                def emit_router(nb):
                    nbb = nb % 2
                    lgw = [stg_spec[nb], g_consts, sp_wr, sp_br]
                    if nb >= 1:
                        lgw.append((V, v_lgs[nb - 1]))
                    for ns in range(4):
                        for cc in range(NCC):
                            bd.at("tensor", lambda e, nbb=nbb, ns=ns, cc=cc: e.matmul(
                                lg_ps[:, NE * ns:NE * (ns + 1)],
                                stg[nbb][:, 512 * cc + P * ns:512 * cc + P * (ns + 1)],
                                wr_sb[:, NE * cc:NE * (cc + 1)],
                                start=(cc == 0), stop=False, skip_group_check=True),
                                waits=lgw if (ns == 0 and cc == 0) else ())
                        bd.at("tensor", lambda e, ns=ns: e.matmul(
                            lg_ps[:, NE * ns:NE * (ns + 1)], ones1_b[:], br_bf[:],
                            start=False, stop=(ns == 3), skip_group_check=True),
                            incs=[(T, 1)] if ns == 3 else ())
                    t_lg[nb] = bd.mark(T)
                    bd.at("vector", lambda e: e.tensor_copy(out=lgs[:], in_=lg_ps[:]),
                          waits=[(T, t_lg[nb])], incs=[(V, 1)])
                    v_lgs[nb] = bd.mark(V)
                    for ns in range(4):
                        pw = []
                        if not (nb == 0 and ns == 0):
                            pnb, pns = (nb, ns - 1) if ns > 0 else (nb - 1, 3)
                            pw = [(T, t_aug[pnb][pns])]
                        bd.at("vector", lambda e, ns=ns: e.max(out=t8[:], in_=lgs[:, NE * ns:NE * (ns + 1)]),
                              waits=[(V, v_lgs[nb])], incs=[(V, 1)])
                        v_max = bd.mark(V)
                        bd.at("vector", lambda e, ns=ns: e.tensor_scalar(
                            pack[:, 1:NE + 1], lgs[:, NE * ns:NE * (ns + 1)], t8[:, 4:5], None, ALU.is_le),
                            waits=[(V, v_max)], incs=[(V, 1)])
                        v_isle = bd.mark(V)
                        bd.at("vector", lambda e: e.tensor_copy(out=pk_b[:], in_=pack[:]),
                              waits=pw + [(V, v_isle)], incs=[(V, 1)])
                        v_pk = bd.mark(V)
                        idw = [sp_id] if (nb == 0 and ns == 0) else []
                        bd.at("tensor", lambda e: e.transpose(aug_ps[:], pk_b[:], ident[:]),
                              waits=idw + [(V, v_pk)], incs=[(T, 1)])
                        t_aug[nb][ns] = bd.mark(T)
                        aw = [(T, t_aug[nb][ns])]
                        if nb >= 2 and ns == 0:
                            aw.append((T, t_h1[nb - 2][NKC - 1]))
                        bd.at("vector", lambda e, ns=ns, nbb=nbb: e.tensor_copy(
                            out=xaug[nbb][:, P * ns:P * (ns + 1)], in_=aug_ps[:]),
                            waits=aw, incs=[(V, 1)])
                        v_xaug[nb][ns] = bd.mark(V)

                def emit_h1(nb, kc):
                    nbb = nb % 2
                    kb = kc % 3
                    hw = [g_w1[kc // 8], (V, v_xaug[nb][3])]
                    if nb == 0 and kc == 0:
                        hw += [sp_b1, sp_pat]
                    if kc >= 3:
                        hw.append((V, sc_relu[nb][kc - 3]))
                    elif nb >= 1:
                        hw.append((V, sc_relu[nb - 1][{0: 30, 1: 31, 2: 29}[kc % 3]]))
                    for cc in range(NCC):
                        bd.at("tensor", lambda e, nbb=nbb, kc=kc, cc=cc, kb=kb: e.matmul(
                            h1_ps[kb][:],
                            w1_sb[:, H * cc + P * kc:H * cc + P * (kc + 1)],
                            stg[nbb][:, 512 * cc:512 * (cc + 1)],
                            start=(cc == 0), stop=False),
                            waits=hw if cc == 0 else ())
                    bd.at("tensor", lambda e, kc=kc, kb=kb, nbb=nbb: e.matmul(
                        h1_ps[kb][:], w1aug[:, P * kc:P * (kc + 1)], xaug[nbb][:],
                        start=False, stop=True), incs=[(T, 1)])
                    t_h1[nb][kc] = bd.mark(T)
                    rw = [(T, t_h1[nb][kc])]
                    if nb >= 1:
                        rw.append((T, t_y[nb - 1][7]))
                    bd.at("vector", lambda e, kc=kc, kb=kb: e.tensor_scalar(
                        h1m[:, 512 * kc:512 * (kc + 1)], h1_ps[kb][:], 0.0, None,
                        ALU.max), waits=rw, incs=[(V, 1)])
                    sc_relu[nb][kc] = bd.mark(V)

                W1T = (2, 8, 14, 20)
                sched = {5: [lambda: emit_stageB(0)],
                         6: [lambda: emit_router(0)],
                         9: [lambda: emit_stageB(1)],
                         22: [lambda: emit_stageB(2)],
                         17: [lambda: emit_router(1)]}
                for kc in range(NKC):
                    t = max(7 + kc // 4, W1T[kc // 8] + 1)
                    sched.setdefault(t, []).append(
                        lambda nb=0, kc=kc: emit_h1(nb, kc))

                emit_stats(0)
                emit_stats(1)
                for j in range(2, 32):
                    emit_norm(j - 2)
                    emit_stats(j)
                    for fthunk in sched.get(j, []):
                        fthunk()
                emit_norm(30)
                emit_norm(31)
                vA_last = bd.mark(V)
                scA_last = bd.mark(SC)
                A_DONE = ([(V, vA_last), (SC, scA_last)]
                          + ch_x.done() + ch_tok.done())
                bd.emit(block)

            with ExitStack() as sB2:
                sbB2 = lambda n, s, d: sB2.enter_context(nc.sbuf_tensor(n, s, d))
                gate_sb = [sbB2(f"gate{i}", [P, C], BF16) for i in range(2)]
                b2_bf = sbB2("b2_bf", [1, C], BF16)
                sp_b2 = ch_wB.issue("gpsimd", lambda e: e.dma_start(
                    out=b2_bf[:], in_=b2_d), waits=A_DONE)

                def emit_y(nb):
                    for pi, (nsp, dh) in enumerate(((0, 0), (0, 1), (1, 0), (1, 1))):
                        for kq in range(4):
                            for kk in range(8):
                                kc = 8 * kq + kk
                                for nsl in range(2):
                                    ns = 2 * nsp + nsl
                                    yw = []
                                    if kk == 0 and nsl == 0:
                                        yw += [g_w2q[kq], (V, sc_relu[nb][8 * kq + 7])]
                                    if kc == 0:
                                        if pi >= 1:
                                            pns = 2 * ((pi - 1) // 2) + nsl
                                            pdh = (pi - 1) % 2
                                            yw.append((SC, sc_silu[nb][pns * 2 + pdh]))
                                        elif nb >= 1:
                                            yw.append((SC, sc_silu[nb - 1][(2 + nsl) * 2 + 1]))
                                    bd.at("tensor", lambda e, ns=ns, dh=dh, kc=kc, nsl=nsl: e.matmul(
                                        y_ps[nsl][:],
                                        h1m[:, 512 * kc + P * ns:512 * kc + P * (ns + 1)],
                                        w2_sb[:, C * kc + 512 * dh:C * kc + 512 * (dh + 1)],
                                        start=(kc == 0), stop=False),
                                        waits=yw)
                        for nsl in range(2):
                            ns = 2 * nsp + nsl
                            gidx = ns * 2 + dh
                            b2w = [sp_b2] if (nb == 0 and pi == 0) else []
                            bd.at("tensor", lambda e, dh=dh, nsl=nsl: e.matmul(
                                y_ps[nsl][:], ones1_b[:], b2_bf[0:1, 512 * dh:512 * (dh + 1)],
                                start=False, stop=True), waits=b2w, incs=[(T, 1)])
                            t_y[nb][gidx] = bd.mark(T)
                            sw = [(T, t_y[nb][gidx])]
                            if nb == 0 and gidx in (0, 2):
                                sw += A_DONE
                            if ns >= 2:
                                sw.append(gate_spec[nb][ns - 2])
                            elif nb >= 1:
                                sw.append(gate_spec[nb - 1][2 + ns])
                            bd.at("scalar", lambda e, ns=ns, dh=dh, nsl=nsl: e.activation(
                                gate_sb[ns % 2][:, 512 * dh:512 * (dh + 1)], y_ps[nsl][:], AF.Silu),
                                waits=sw, incs=[(SC, 1)])
                            sc_silu[nb][gidx] = bd.mark(SC)
                        if dh == 1:
                            for nsl in range(2):
                                ns = 2 * nsp + nsl
                                row = 512 * nb + P * ns
                                gate_spec[nb][ns] = ch_gate.issue("scalar", lambda e, ns=ns, row=row: e.dma_start(
                                    out=gate_d[row:row + P, :], in_=gate_sb[ns % 2][:]),
                                    waits=[(SC, sc_silu[nb][ns * 2 + 1])])

                emit_y(0)
                emit_router(2)
                for kc in range(NKC):
                    emit_h1(1, kc)
                emit_stageB(3)
                emit_y(1)
                emit_router(3)
                for kc in range(NKC):
                    emit_h1(2, kc)
                emit_y(2)
                for kc in range(NKC):
                    emit_h1(3, kc)
                emit_y(3)

                if dbg:
                    ch_wB.issue("gpsimd", lambda e: e.dma_start(out=dbg["dbg_lgs"], in_=lgs[:]),
                               waits=[(V, v_lgs[NBB - 1])])
                    ch_wB.issue("gpsimd", lambda e: e.dma_start(out=dbg["dbg_m"], in_=ms_store[:]))
                    ch_wB.issue("gpsimd", lambda e: e.dma_start(out=dbg["dbg_rstd"], in_=rstd_store[:]))
                tB_last = bd.mark(T)
                vB_last = bd.mark(V)
                scB_last = bd.mark(SC)
                B_DONE = ([(T, tB_last), (V, vB_last), (SC, scB_last)]
                          + ch_stgB.done() + ch_gate.done())
                bd.emit(block)


        with ExitStack() as sCD:
            sbCD = lambda n, s, d: sCD.enter_context(nc.sbuf_tensor(n, s, d))
            crow_bf = sbCD("crow_bf", [1, R], BF16)
            negones_b = sbCD("negones_b", [1, P], BF16)
            f_sb = sbCD("f_sb", [P, NCC * S], BF16)
            g_sb = sbCD("g_sb", [P, NCC * R], BF16)
            wv_sb = sbCD("wv_sb", [P, NCC * C], BF16)

            t_gmm = [[0] * NCC for _ in range(4)]
            v_gev = [[0] * NCC for _ in range(4)]
            sc_gsq = [[0] * NCC for _ in range(4)]
            t_nsq = [0] * 4
            sc_crow = [0] * 4
            t_fmm = [[0] * NCC for _ in range(8)]
            v_fev = [[0] * NCC for _ in range(8)]

            def stage_dmaC(stgc, st_i, readers_mark, first_waits):
                w = list(first_waits)
                if readers_mark:
                    w.append((T, readers_mark))
                builders = []
                for cc in range(NCC):
                    builders.append(lambda e, st_i=st_i, cc=cc, stgc=stgc: e.dma_start(
                        out=stgc[st_i % 2][:, 512 * cc:512 * (cc + 1)],
                        in_=xntok_d[512 * st_i:512 * (st_i + 1), P * cc:P * (cc + 1)],
                        transpose=True))
                return ch_stgC.issue_group("sync", builders, waits=w)

            with ExitStack() as sC1:
                sbC1 = lambda n, s, d: sC1.enter_context(nc.sbuf_tensor(n, s, d))
                psC1 = lambda n, s, d: sC1.enter_context(nc.psum_tensor(n, s, d))
                wk_sb = sbC1("wk_sb", [P, NCC * C], BF16)
                wq_sb = sbC1("wq_sb", [P, NCC * C], BF16)
                stgc = [sbC1(f"stgC{i}", [P, NCC * 512], BF16) for i in range(2)]
                gsq = sbC1("gsq", [P, 1024], F32)
                cps = [psC1(f"cps{i}", [P, 512], F32) for i in range(3)]
                nsq_ps = [psC1(f"nsq_ps{i}", [1, 512], F32) for i in range(2)]

                g_wk = ch_wB.issue("gpsimd", lambda e: e.dma_start(
                    out=wk_sb[:].rearrange("p (cc c) -> p cc c", c=C),
                    in_=wkT_d.rearrange("(cc p) c -> p cc c", p=P)), waits=B_DONE)
                g_wq = ch_wB.issue("gpsimd", lambda e: e.dma_start(
                    out=wq_sb[:].rearrange("p (cc c) -> p cc c", c=C),
                    in_=wqT_d.rearrange("(cc p) c -> p cc c", p=P)), waits=B_DONE)
                g_wv = ch_wB.issue("gpsimd", lambda e: e.dma_start(
                    out=wv_sb[:].rearrange("p (cc c) -> p cc c", c=C),
                    in_=wvT_d.rearrange("(cc p) c -> p cc c", p=P)), waits=B_DONE)

                gsq2 = [gsq[:, 0:512], gsq[:, 512:1024]]

                bd.at("vector", lambda e: e.memset(negones_b[:], -1.0),
                      waits=B_DONE, incs=[(V, 1)])

                def emit_nsq(st_i, dc):
                    nw = [(SC, sc_gsq[st_i][dc])]
                    if st_i >= 2 and dc == 0:
                        nw.append((SC, sc_crow[st_i - 2]))
                    bd.at("tensor", lambda e, st_i=st_i, dc=dc: e.matmul(
                        nsq_ps[st_i % 2][:], onescol_f[:], gsq2[(st_i * NCC + dc) % 2],
                        start=(dc == 0), stop=(dc == NCC - 1)),
                        waits=nw, incs=[(T, 1)] if dc == NCC - 1 else ())
                    if dc == NCC - 1:
                        t_nsq[st_i] = bd.mark(T)
                        bd.at("scalar", lambda e, st_i=st_i: e.activation(
                            crow_bf[0:1, 512 * st_i:512 * (st_i + 1)], nsq_ps[st_i % 2][:],
                            AF.Sqrt, scale=A_BOUND * A_BOUND),
                            waits=[(T, t_nsq[st_i])], incs=[(SC, 1)])
                        sc_crow[st_i] = bd.mark(SC)

                for st_i in range(4):
                    rm = t_gmm[st_i - 2][NCC - 1] if st_i >= 2 else 0
                    st_spec = stage_dmaC(stgc, st_i, rm, B_DONE + ch_tok.done())
                    for dc in range(NCC):
                        db = dc % 3
                        k = st_i * NCC + dc
                        gw = [g_wk, st_spec]
                        if k == 0:
                            gw += B_DONE
                        if k >= 3:
                            pk = k - 3
                            gw += [(V, v_gev[pk // NCC][pk % NCC]), (SC, sc_gsq[pk // NCC][pk % NCC])]
                        for cc in range(NCC):
                            bd.at("tensor", lambda e, st_i=st_i, dc=dc, cc=cc, db=db: e.matmul(
                                cps[db][:],
                                wk_sb[:, C * cc + P * dc:C * cc + P * (dc + 1)],
                                stgc[st_i % 2][:, 512 * cc:512 * (cc + 1)],
                                start=(cc == 0), stop=(cc == NCC - 1)),
                                waits=gw if cc == 0 else (),
                                incs=[(T, 1)] if cc == NCC - 1 else ())
                        t_gmm[st_i][dc] = bd.mark(T)
                        if k >= 1:
                            emit_nsq((k - 1) // NCC, (k - 1) % NCC)
                        bd.at("vector", lambda e, st_i=st_i, dc=dc, db=db: e.tensor_copy(
                            out=g_sb[:, R * dc + 512 * st_i:R * dc + 512 * (st_i + 1)], in_=cps[db][:]),
                            waits=[(T, t_gmm[st_i][dc])], incs=[(V, 1)])
                        v_gev[st_i][dc] = bd.mark(V)
                        bd.at("scalar", lambda e, db=db, k=k: e.activation(
                            gsq2[k % 2], cps[db][:], AF.Square),
                            waits=[(V, v_gev[st_i][dc])], incs=[(SC, 1)])
                        sc_gsq[st_i][dc] = bd.mark(SC)
                emit_nsq(3, NCC - 1)

                stgf = stgc
                fps = cps
                for st_i in range(8):
                    if st_i >= 2:
                        rm = t_fmm[st_i - 2][NCC - 1]
                    else:
                        rm = t_gmm[2 + st_i][NCC - 1]
                    st_spec = stage_dmaC(stgf, st_i, rm, [])
                    for dc in range(NCC):
                        db = dc % 3
                        k = st_i * NCC + dc
                        fw = [g_wq, st_spec]
                        if k >= 3:
                            pk = k - 3
                            fw.append((V, v_fev[pk // NCC][pk % NCC]))
                        else:
                            pk = {0: 30, 1: 31, 2: 29}[k % 3]
                            fw += [(V, v_gev[pk // NCC][pk % NCC]),
                                   (SC, sc_gsq[pk // NCC][pk % NCC])]
                        for cc in range(NCC):
                            bd.at("tensor", lambda e, st_i=st_i, dc=dc, cc=cc, db=db: e.matmul(
                                fps[db][:],
                                wq_sb[:, C * cc + P * dc:C * cc + P * (dc + 1)],
                                stgf[st_i % 2][:, 512 * cc:512 * (cc + 1)],
                                start=(cc == 0), stop=(cc == NCC - 1)),
                                waits=fw if cc == 0 else (),
                                incs=[(T, 1)] if cc == NCC - 1 else ())
                        t_fmm[st_i][dc] = bd.mark(T)
                        bd.at("vector", lambda e, st_i=st_i, dc=dc, db=db: e.tensor_copy(
                            out=f_sb[:, S * dc + 512 * st_i:S * dc + 512 * (st_i + 1)], in_=fps[db][:]),
                            waits=[(T, t_fmm[st_i][dc])], incs=[(V, 1)])
                        v_fev[st_i][dc] = bd.mark(V)

                if dbg:
                    ch_wB.issue("gpsimd", lambda e: e.dma_start(out=dbg["dbg_crow"], in_=crow_bf[:]),
                               waits=[(SC, sc_crow[3])])
                    ch_wB.issue("gpsimd", lambda e: e.dma_start(out=dbg["dbg_f"], in_=f_sb[:]),
                               waits=[(V, v_fev[7][NCC - 1])])
                    ch_wB.issue("gpsimd", lambda e: e.dma_start(out=dbg["dbg_g"], in_=g_sb[:]),
                               waits=[(V, v_gev[3][NCC - 1])])
                tC_last = bd.mark(T)
                vC_last = bd.mark(V)
                scC_last = bd.mark(SC)
                C_DONE = ([(T, tC_last), (V, vC_last), (SC, scC_last)]
                          + ch_stgC.done() + ch_gate.done())
                bd.emit(block)

            with ExitStack() as sD:
                sbD = lambda n, s, d: sD.enter_context(nc.sbuf_tensor(n, s, d))
                psD = lambda n, s, d: sD.enter_context(nc.psum_tensor(n, s, d))
                xn_sb = sbD("xn_sb", [P, NSC * C], BF16)
                e_buf = [sbD(f"e_buf{i}", [P, 512], BF16) for i in range(2)]
                E_sb = sbD("E_sb", [P, NCC * TCW], BF16)
                xtile = [sbD(f"xtile{i}", [P, C], F32) for i in range(2)]
                gate_t = [sbD(f"gate_t{i}", [P, C], BF16) for i in range(2)]
                osb = sbD("osb", [P, C], F32)
                acc = sbD("acc", [P, C], F32)
                scale_sb = sbD("scale_sb", [P, 2], F32)
                zrow = acc[0:1, 0:512]
                sc_ps = [psD(f"sc_ps{i}", [P, 512], F32) for i in range(2)]
                E_ps = psD("E_ps", [P, NCC * TCW], F32)
                z_ps = psD("z_ps", [P, 512], F32)
                o_ps = psD("o_ps", [P, 512], F32)

                xn_spec = []
                for q in range(4):
                    xn_spec.append(ch_big.issue("scalar", lambda e, q=q: e.dma_start(
                        out=xn_sb[:].rearrange("p (si c) -> p si c", c=C)[:, 8 * q:8 * (q + 1), :],
                        in_=xntok_d.rearrange("(si p) c -> p si c", p=P)[:, 8 * q:8 * (q + 1), :]),
                        waits=(C_DONE + ch_tok.done()) if q == 0 else ()))

                t_pair = [[0] * 16 for _ in range(NTC)]
                sc_exp = [[0] * 16 for _ in range(NTC)]
                t_EZ = [[0] * 16 for _ in range(NTC)]
                t_z = [0] * NTC
                v_Eev = [0] * NTC
                t_scmm = [0] * NTC
                v_scev = [0] * NTC
                v_zrow = [0] * NTC
                t_o = [[0] * 4 for _ in range(NTC)]
                sc_oev = [[0] * 4 for _ in range(NTC)]
                v_comb = [[0] * 2 for _ in range(NTC)]
                xt_spec = [[None] * 2 for _ in range(NTC)]
                gt_spec = [[None] * 2 for _ in range(NTC)]
                y_spec = [[None] * 2 for _ in range(NTC)]

                def emit_scores(tc, p):
                    w = [(V, vC_last), (SC, scC_last)]
                    if p >= 2:
                        w.append((SC, sc_exp[tc][p - 2]))
                    elif tc >= 1:
                        w.append((SC, sc_exp[tc - 1][14 + p]))
                        if p == 1:
                            w.append((V, v_scev[tc - 1]))
                    for hi in range(2):
                        si = 2 * p + hi
                        for dc in range(NCC):
                            bd.at("tensor", lambda e, tc=tc, p=p, hi=hi, si=si, dc=dc: e.matmul(
                                sc_ps[p % 2][:, 256 * hi:256 * (hi + 1)],
                                f_sb[:, S * dc + P * si:S * dc + P * (si + 1)],
                                g_sb[:, R * dc + TCW * tc:R * dc + TCW * (tc + 1)],
                                start=(hi == 0 and dc == 0), stop=False,
                                skip_group_check=True),
                                waits=w if (hi == 0 and dc == 0) else ())
                    bd.at("tensor", lambda e, p=p, tc=tc: e.matmul(
                        sc_ps[p % 2][:, 0:512], negones_b[:],
                        crow_bf[0:1, TCW * tc:TCW * (tc + 1)].unsqueeze(1).to_broadcast([1, 2, 256]),
                        start=False, stop=True, skip_group_check=True),
                        incs=[(T, 1)])
                    t_pair[tc][p] = bd.mark(T)

                def emit_exp(tc, p):
                    bd.at("scalar", lambda e, tc=tc, p=p: e.activation(
                        e_buf[p % 2][:], sc_ps[p % 2][:], AF.Exp),
                        waits=[(T, t_pair[tc][p])], incs=[(SC, 1)])
                    sc_exp[tc][p] = bd.mark(SC)

                def emit_EZ(tc, p, last=False):
                    w = [(SC, sc_exp[tc][p]), xn_spec[(2 * p) // 8], xn_spec[(2 * p + 1) // 8]]
                    if p == 0 and tc >= 1:
                        w += [(V, v_Eev[tc - 1])]
                    zw = ()
                    if p == 0 and tc >= 1:
                        zw = [(SC, sc_oev[tc - 1][3])]

                    def z_mm():
                        bd.at("tensor", lambda e, p=p: e.matmul(
                            z_ps[0:1, 0:512], onescol_b[:], e_buf[p % 2][:, 0:512],
                            start=(p == 0), stop=(p == 15), skip_group_check=True),
                            waits=zw, incs=[(T, 1)])
                    if last:
                        bd.at("tensor", None, waits=w)
                        z_mm()
                        t_z[tc] = bd.mark(T)
                    for hi in range(2):
                        si = 2 * p + hi
                        for cc in range(NCC):
                            bd.at("tensor", lambda e, si=si, cc=cc, p=p: e.matmul(
                                E_ps[:, TCW * cc:TCW * (cc + 1)],
                                xn_sb[:, C * si + P * cc:C * si + P * (cc + 1)],
                                e_buf[p % 2][:, 256 * hi:256 * (hi + 1)],
                                start=(si == 0 and cc % 2 == 0), stop=(si == NSC - 1),
                                skip_group_check=True),
                                waits=w if (hi == 0 and cc == 0) else (),
                                incs=[(T, 1)] if (last and hi == 1 and cc % 2 == 1) else ())
                            if last and hi == 1 and cc % 2 == 1:
                                b = cc // 2
                                tm = bd.mark(T)
                                bd.at("vector", lambda e, b=b: e.tensor_copy(
                                    out=E_sb[:, 512 * b:512 * (b + 1)],
                                    in_=E_ps[:, 512 * b:512 * (b + 1)]),
                                    waits=[(T, tm)],
                                    incs=[(V, 1)] if b == 3 else ())
                    if not last:
                        z_mm()
                    else:
                        v_Eev[tc] = bd.mark(V)
                    t_EZ[tc][p] = bd.mark(T)

                for tc in range(NTC):
                    for tsub in range(2):
                        row = TCW * tc + P * tsub
                        xw = [(V, v_comb[tc - 1][tsub])] if tc >= 1 else list(C_DONE)
                        xt_spec[tc][tsub] = ch_xt.issue("sync", lambda e, row=row, tsub=tsub: e.dma_start(
                            out=xtile[tsub][:], in_=x_d[row:row + P, :]), waits=xw)
                        gw = ch_gate.done()
                        if tc >= 1:
                            gw = gw + [(V, v_comb[tc - 1][tsub])]
                        else:
                            gw = gw + C_DONE
                        gt_spec[tc][tsub] = ch_gt.issue("sync", lambda e, row=row, tsub=tsub: e.dma_start(
                            out=gate_t[tsub][:], in_=gate_d[row:row + P, :]), waits=gw)

                    emit_scores(tc, 0)
                    emit_exp(tc, 0)
                    for p in range(1, 16):
                        emit_scores(tc, p)
                        emit_exp(tc, p)
                        emit_EZ(tc, p - 1)
                    emit_EZ(tc, 15, last=True)

                    bd.at("vector", lambda e: e.tensor_copy(out=zrow[:], in_=z_ps[0:1, 0:512]),
                          waits=[(T, t_z[tc])], incs=[(V, 1)])
                    v_zc = bd.mark(V)
                    bd.at("vector", lambda e: e.tensor_tensor(
                        zrow[0:1, 0:256], zrow[0:1, 0:256], zrow[0:1, 256:512], ALU.add),
                        waits=[(V, v_zc)], incs=[(V, 1)])
                    v_zf = bd.mark(V)
                    bd.at("vector", lambda e: e.reciprocal(zrow[0:1, 0:256], zrow[0:1, 0:256]),
                          waits=[(V, v_zf)], incs=[(V, 1)])
                    v_zr = bd.mark(V)
                    bd.at("vector", lambda e: e.tensor_scalar(
                        zrow[0:1, 0:256], zrow[0:1, 0:256], gam_sb[0:1, 0:1], None, ALU.mult),
                        waits=[(V, v_zr)], incs=[(V, 1)])
                    v_zrow[tc] = bd.mark(V)

                    for tsub in range(2):
                        for dh in range(2):
                            gidx = tsub * 2 + dh
                            ob = o_ps[:, 0:512] if gidx % 2 == 0 else z_ps[:, 0:512]
                            ow = [(V, v_Eev[tc]), g_wv]
                            if gidx == 1:
                                ow.append((V, v_zc))
                            if gidx >= 2:
                                ow.append((SC, sc_oev[tc][gidx - 2]))
                            elif tc >= 1:
                                ow.append((SC, sc_oev[tc - 1][gidx + 2]))
                            for cc in range(NCC):
                                bd.at("tensor", lambda e, tsub=tsub, dh=dh, cc=cc, ob=ob: e.matmul(
                                    ob,
                                    E_sb[:, TCW * cc + P * tsub:TCW * cc + P * (tsub + 1)],
                                    wv_sb[:, C * cc + 512 * dh:C * cc + 512 * (dh + 1)],
                                    start=(cc == 0), stop=(cc == NCC - 1),
                                    skip_group_check=True),
                                    waits=ow if cc == 0 else (),
                                    incs=[(T, 1)] if cc == NCC - 1 else ())
                            t_o[tc][gidx] = bd.mark(T)
                            if gidx == 0:
                                bd.at("tensor", lambda e: e.matmul(
                                    sc_ps[1][:, 0:1], zrow[0:1, 0:P], one11_f[:],
                                    start=True, stop=True, skip_group_check=True),
                                    waits=[(V, v_zrow[tc]), (SC, sc_exp[tc][15])])
                                bd.at("tensor", lambda e: e.matmul(
                                    sc_ps[1][:, 1:2], zrow[0:1, P:TCW], one11_f[:],
                                    start=True, stop=True, skip_group_check=True),
                                    incs=[(T, 1)])
                                t_scmm[tc] = bd.mark(T)
                                bd.at("vector", lambda e: e.tensor_copy(
                                    out=scale_sb[:], in_=sc_ps[1][:, 0:2]),
                                    waits=[(T, t_scmm[tc])], incs=[(V, 1)])
                                v_scev[tc] = bd.mark(V)
                            oew = [(T, t_o[tc][gidx]), (V, v_scev[tc])]
                            if dh == 0 and tsub == 0 and tc >= 1:
                                oew.append(y_spec[tc - 1][1])
                            if dh == 0 and tsub == 1:
                                oew.append(y_spec[tc][0])
                            bd.at("scalar", lambda e, tsub=tsub, dh=dh, ob=ob: e.activation(
                                osb[:, 512 * dh:512 * (dh + 1)], ob, AF.Identity,
                                scale=scale_sb[:, tsub:tsub + 1]),
                                waits=oew, incs=[(SC, 1)])
                            sc_oev[tc][gidx] = bd.mark(SC)
                        ti = 2 * tc + tsub
                        bd.at("vector", lambda e, ti=ti, tsub=tsub: e.tensor_scalar(
                            acc[:], xtile[tsub][:], ms_store[:, 2 * ti:2 * ti + 1],
                            rstd_store[:, ti:ti + 1],
                            ALU.subtract, ALU.mult),
                            waits=[xt_spec[tc][tsub]], incs=[(V, 1)])
                        v_c1 = bd.mark(V)
                        bd.at("vector", lambda e: e.tensor_tensor(acc[:], acc[:], osb[:], ALU.add),
                              waits=[(SC, sc_oev[tc][tsub * 2 + 1]), (V, v_c1)], incs=[(V, 1)])
                        v_c2 = bd.mark(V)
                        bd.at("vector", lambda e, tsub=tsub: e.tensor_tensor(
                            acc[:], acc[:], gate_t[tsub][:], ALU.mult),
                            waits=[gt_spec[tc][tsub], (V, v_c2)], incs=[(V, 1)])
                        v_c3 = bd.mark(V)
                        yw = [(V, v_c3)]
                        if tc >= 1:
                            yw.append(y_spec[tc - 1][tsub])
                        bd.at("vector", lambda e, tsub=tsub: e.tensor_tensor(
                            osb[:], acc[:], xtile[tsub][:], ALU.add),
                            waits=yw, incs=[(V, 1)])
                        v_comb[tc][tsub] = bd.mark(V)
                        row = TCW * tc + P * tsub
                        y_spec[tc][tsub] = ch_y.issue("sync", lambda e, row=row: e.dma_start(
                            out=y_d[row:row + P, :], in_=osb[:]),
                            waits=[(V, v_comb[tc][tsub])])

                if dbg:
                    ch_wB.issue("gpsimd", lambda e: e.dma_start(out=dbg["dbg_xnsb"], in_=xn_sb[:]),
                               waits=ch_big.done())
                    ch_wB.issue("gpsimd", lambda e: e.dma_start(out=dbg["dbg_E"], in_=E_sb[:]),
                               waits=[(V, v_Eev[NTC - 1])])
                    ch_wB.issue("gpsimd", lambda e: e.dma_start(out=dbg["dbg_scale"], in_=scale_sb[:]),
                               waits=[(V, v_scev[NTC - 1])])
                    bd.at("gpsimd", None, waits=ch_wB.done())
                bd.at("sync", None, waits=ch_y.done())
                bd.emit(block)

    return nc


_NC_CACHE = {}


def kernel(L0_attn_x, L0_attn_Wq, L0_attn_Wk, L0_attn_Wv, L0_attn_gamma,
           L0_ffn_w_router, L0_ffn_b_router, L0_ffn_w1, L0_ffn_b1,
           L0_ffn_w2, L0_ffn_b2):
    x = np.ascontiguousarray(np.asarray(L0_attn_x, np.float32))
    wqT = np.ascontiguousarray(np.asarray(L0_attn_Wq, np.float32).T)
    wkT = np.ascontiguousarray(np.asarray(L0_attn_Wk, np.float32).T)
    wvT = np.ascontiguousarray(np.asarray(L0_attn_Wv, np.float32).T)
    w1T = np.ascontiguousarray(np.asarray(L0_ffn_w1, np.float32).T)
    w2T = np.ascontiguousarray(np.asarray(L0_ffn_w2, np.float32).T)
    wr = np.ascontiguousarray(np.asarray(L0_ffn_w_router, np.float32))
    br = np.asarray(L0_ffn_b_router, np.float32).reshape(1, NE)
    b1 = np.asarray(L0_ffn_b1, np.float32).reshape(1, H)
    b2 = np.asarray(L0_ffn_b2, np.float32).reshape(1, C)
    gam = np.asarray(L0_attn_gamma, np.float32).reshape(1, 1)
    pat = np.zeros((NE, H), ml_dtypes.bfloat16)
    for e_i in range(NE):
        pat[e_i, 256 * e_i:256 * (e_i + 1)] = -1e30
    ident = np.eye(P).astype(ml_dtypes.bfloat16)

    if "nc" not in _NC_CACHE:
        _NC_CACHE["nc"] = build_nc()
    nc = _NC_CACHE["nc"]

    in_maps = []
    for core in range(8):
        b, half = core // 2, core % 2
        xb = x[b]
        if half == 0:
            x_perm = xb
        else:
            x_perm = np.ascontiguousarray(
                np.concatenate([xb[R:], xb[:R]], axis=0))
        in_maps.append(dict(
            x_perm=x_perm, wqT=wqT, wkT=wkT, wvT=wvT, w1T=w1T, w2T=w2T,
            wr=wr, br=br, b1=b1, b2=b2, gam=gam, pat=pat, ident=ident))

    res = run_bass_kernel_spmd(nc, in_maps, core_ids=list(range(8)))

    out = np.empty((4, S, C), np.float32)
    for core in range(8):
        b, half = core // 2, core % 2
        out[b, R * half:R * (half + 1)] = res.results[core]["y"]
    return out


# version 24
# speedup vs baseline: 1.0011x; 1.0011x over previous
import numpy as np
import ml_dtypes
from contextlib import ExitStack

import concourse.bass as bass
import concourse.mybir as mybir
from concourse.bass_utils import run_bass_kernel_spmd

F32 = mybir.dt.float32
BF16 = mybir.dt.bfloat16
AF = mybir.ActivationFunctionType
ALU = mybir.AluOpType
AX = mybir.AxisListType

P = 128
S = 4096
C = 1024
H = 4096
NE = 16
R = 2048
NCC = C // P
NKC = H // P
NBB = 4
TCW = 256
NTC = R // TCW
NSC = S // P
A_BOUND = 5.0
EPS = 1e-5

ENGINES = ("sync", "scalar", "vector", "tensor", "gpsimd")


class Sched:

    def __init__(self, nc, ctx):
        self.nc = nc
        self.ctx = ctx
        self.q = {e: [] for e in ENGINES}
        self.sem = {}
        self.cnt = {}
        for name in ("T", "V", "SC"):
            self.add_sem(name)

    def add_sem(self, name):
        self.sem[name] = self.ctx.enter_context(self.nc.semaphore(name))
        self.cnt[name] = 0

    def at(self, eng, builder, waits=(), incs=()):
        incs2 = []
        for s, amt in incs:
            self.cnt[s] += amt
            incs2.append((s, amt))
        self.q[eng].append((builder, tuple(waits), tuple(incs2)))

    def mark(self, s):
        return self.cnt[s]

    def emit(self, block):
        for eng in ENGINES:
            ops = self.q[eng]
            if not ops:
                continue
            waited = {}

            def body(e, ops=ops, waited=waited):
                for builder, waits, incs in ops:
                    for s, v in waits:
                        if v > 0 and waited.get(s, 0) < v:
                            e.wait_ge(self.sem[s], v)
                            waited[s] = v
                    ins = builder(e) if builder is not None else None
                    for s, amt in incs:
                        ins.then_inc(self.sem[s], amt)

            getattr(block, eng)(body)
        self.q = {e: [] for e in ENGINES}


class Chan:

    def __init__(self, bd, name, n=2, serialize=False):
        self.bd = bd
        self.n = n
        self.serialize = serialize
        self.names = []
        for i in range(n):
            nm = f"{name}{i}"
            bd.add_sem(nm)
            self.names.append(nm)
        self.i = 0
        self.last = None

    def issue(self, eng, builder, waits=()):
        return self.issue_group(eng, [builder], waits)

    def issue_group(self, eng, builders, waits=()):
        nm = self.names[self.i % self.n]
        w = list(waits)
        if self.serialize and self.last is not None:
            w.append(self.last)
        for bi, b in enumerate(builders):
            self.bd.at(eng, b, waits=w if bi == 0 else (), incs=[(nm, 16)])
        self.i += 1
        spec = (nm, self.bd.cnt[nm])
        self.last = spec
        return spec

    def done(self):
        return [(nm, self.bd.cnt[nm]) for nm in self.names]


def build_nc(debug=False):
    nc = bass.Bass("TRN2", target_bir_lowering=False)

    x_d = nc.dram_tensor("x_perm", [S, C], F32, kind="ExternalInput").ap()
    wqT_d = nc.dram_tensor("wqT", [C, C], F32, kind="ExternalInput").ap()
    wkT_d = nc.dram_tensor("wkT", [C, C], F32, kind="ExternalInput").ap()
    wvT_d = nc.dram_tensor("wvT", [C, C], F32, kind="ExternalInput").ap()
    w1T_d = nc.dram_tensor("w1T", [C, H], F32, kind="ExternalInput").ap()
    w2T_d = nc.dram_tensor("w2T", [H, C], F32, kind="ExternalInput").ap()
    wr_d = nc.dram_tensor("wr", [C, NE], F32, kind="ExternalInput").ap()
    br_d = nc.dram_tensor("br", [1, NE], F32, kind="ExternalInput").ap()
    b1_d = nc.dram_tensor("b1", [1, H], F32, kind="ExternalInput").ap()
    b2_d = nc.dram_tensor("b2", [1, C], F32, kind="ExternalInput").ap()
    gam_d = nc.dram_tensor("gam", [1, 1], F32, kind="ExternalInput").ap()
    pat_d = nc.dram_tensor("pat", [NE, H], BF16, kind="ExternalInput").ap()
    id_d = nc.dram_tensor("ident", [P, P], BF16, kind="ExternalInput").ap()

    y_d = nc.dram_tensor("y", [R, C], F32, kind="ExternalOutput").ap()

    scr_kind = "ExternalOutput" if debug else "Internal"
    xntok_d = nc.dram_tensor("xntok_scr", [S, C], BF16, kind=scr_kind).ap()
    gate_d = nc.dram_tensor("gate_scr", [R, C], BF16, kind=scr_kind).ap()

    dbg = {}
    if debug:
        for name, shape, dt in [
            ("dbg_crow", [1, R], BF16), ("dbg_f", [P, NCC * S], BF16),
            ("dbg_g", [P, NCC * R], BF16), ("dbg_E", [P, NCC * TCW], BF16),
            ("dbg_scale", [P, 2], F32),
            ("dbg_m", [P, 40], F32), ("dbg_rstd", [P, 20], F32),
            ("dbg_lgs", [P, 64], F32), ("dbg_xnsb", [P, NSC * C], BF16),
        ]:
            dbg[name] = nc.dram_tensor(name, shape, dt, kind="ExternalOutput").ap()

    with ExitStack() as top:
        sb = lambda n, s, d: top.enter_context(nc.sbuf_tensor(n, s, d))
        bd = Sched(nc, top)
        block = top.enter_context(nc.Block())

        T, V, SC = "T", "V", "SC"
        ch_wB = Chan(bd, "cW", 4)
        ch_aux = Chan(bd, "cAX", 2)
        ch_x = Chan(bd, "cX", 4)
        ch_tok = Chan(bd, "cT", 4)
        ch_big = Chan(bd, "cB", 2)
        ch_stgB = Chan(bd, "cSB", 2)
        ch_gate = Chan(bd, "cG", 2)
        ch_stgC = Chan(bd, "cSC", 2)
        ch_xt = Chan(bd, "cXT", 2)
        ch_gt = Chan(bd, "cGT", 2)
        ch_y = Chan(bd, "cY", 2)

        onescol_b = sb("onescol_b", [P, 1], BF16)
        onescol_f = sb("onescol_f", [P, 1], F32)
        ones1_b = sb("ones1_b", [1, P], BF16)
        one11_f = sb("one11_f", [1, 1], F32)
        gam_sb = sb("gam_sb", [1, 1], F32)
        epsb = sb("epsb", [P, 1], F32)
        ms_store = sb("ms_store", [P, 40], F32)
        rstd_store = sb("rstd_store", [P, 20], F32)

        with ExitStack() as sAB:
            sbAB = lambda n, s, d: sAB.enter_context(nc.sbuf_tensor(n, s, d))
            psAB = lambda n, s, d: sAB.enter_context(nc.psum_tensor(n, s, d))
            w1_sb = sbAB("w1_sb", [P, NCC * H], BF16)
            w2_sb = sbAB("w2_sb", [P, NKC * C], BF16)
            ident = sbAB("identc", [P, P], BF16)
            w1aug = sbAB("w1aug", [32, H], BF16)
            wr_sb = sbAB("wr_sb", [P, NCC * NE], BF16)
            br_bf = sbAB("br_bf", [1, NE], BF16)
            stg = [sbAB(f"stgB{i}", [P, NCC * 512], BF16) for i in range(2)]
            h1m = sbAB("h1m", [P, NKC * 512], BF16)
            xaug = [sbAB(f"xaug{i}", [32, 512], BF16) for i in range(2)]
            lgs = sbAB("lgs", [P, 4 * NE], F32)
            t8 = sbAB("t8", [P, 8], F32)
            pack = sbAB("pack", [P, 32], F32)
            pk_b = sbAB("pk_b", [P, 32], BF16)
            h1_ps = [psAB(f"h1_ps{i}", [P, 512], F32) for i in range(3)]
            y_ps = [psAB(f"y_ps{i}", [P, 512], F32) for i in range(2)]
            lg_ps = psAB("lg_ps", [P, 4 * NE], F32)
            aug_ps = psAB("aug_ps", [32, P], BF16)

            x_spec = [None] * 32
            tok_spec = [None] * 32
            v_xn = [0] * 32
            v_bn2 = [0] * 32
            sc_sd = [0] * 32
            g_w1 = [None] * 4
            stg_spec = [None] * NBB
            t_h1 = [[0] * NKC for _ in range(NBB)]
            sc_relu = [[0] * NKC for _ in range(NBB)]
            t_y = [[0] * 8 for _ in range(NBB)]
            sc_silu = [[0] * 8 for _ in range(NBB)]
            gate_spec = [[None] * 4 for _ in range(NBB)]
            t_lg = [0] * NBB
            v_lgs = [0] * NBB
            t_aug = [[0] * 4 for _ in range(NBB)]
            v_xaug = [[0] * 4 for _ in range(NBB)]

            with ExitStack() as sA:
                sbA = lambda n, s, d: sA.enter_context(nc.sbuf_tensor(n, s, d))
                xt = [sbA(f"xt{i}", [P, C], F32) for i in range(4)]
                xnb = [sbA(f"xnb{i}", [P, C], BF16) for i in range(2)]
                bnbuf = sbA("bnbuf", [P, 12], F32)
                sd_t = sbA("sd_t", [P, 4], F32)

                bd.at("scalar", lambda e: e.memzero(w1aug[:]), incs=[(SC, 1)])
                sc_w1aug0 = bd.mark(SC)
                bd.at("vector", lambda e: e.memset(onescol_b[:], 1.0))
                bd.at("vector", lambda e: e.memset(onescol_f[:], 1.0))
                bd.at("vector", lambda e: e.memset(ones1_b[:], 1.0))
                bd.at("vector", lambda e: e.memset(one11_f[:], 1.0))
                bd.at("vector", lambda e: e.memset(epsb[:], EPS))
                bd.at("vector", lambda e: e.memset(pack[:, 0:1], 1.0))
                bd.at("vector", lambda e: e.memset(pack[:, NE + 1:32], 0.0), incs=[(V, 1)])

                sp_id = ch_aux.issue("scalar", lambda e: e.dma_start(out=ident[:], in_=id_d))
                sp_pat = ch_aux.issue("scalar", lambda e: e.dma_start(
                    out=w1aug[1:NE + 1, :], in_=pat_d), waits=[(SC, sc_w1aug0)])
                w2f = w2_sb[:].bitcast(F32)
                q0f_spec = ch_aux.issue("scalar", lambda e: e.dma_start(
                    out=w2f[:, 0:8192].rearrange("p (cc h) -> p cc h", h=1024),
                    in_=w1T_d.rearrange("(cc p) h -> p cc h", p=P)[:, :, 0:1024]))
                for cc in range(NCC):
                    bd.at("scalar", lambda e, cc=cc: e.activation(
                        w1_sb[:, H * cc:H * cc + 1024], w2f[:, 1024 * cc:1024 * (cc + 1)],
                        AF.Identity),
                        waits=[q0f_spec] if cc == 0 else (), incs=[(SC, 1)])
                sc_q0 = bd.mark(SC)
                g_w1[0] = (SC, sc_q0)

                def w1_issue(q):
                    g_w1[q] = ch_wB.issue("gpsimd", lambda e, q=q: e.dma_start(
                        out=w1_sb[:].rearrange("p (cc h) -> p cc h", h=H)[:, :, 1024 * q:1024 * (q + 1)],
                        in_=w1T_d.rearrange("(cc p) h -> p cc h", p=P)[:, :, 1024 * q:1024 * (q + 1)]))
                sp_wr = ch_wB.issue("gpsimd", lambda e: e.dma_start(
                    out=wr_sb[:].rearrange("p (cc e) -> p cc e", e=NE),
                    in_=wr_d.rearrange("(cc p) e -> p cc e", p=P)))
                sp_br = ch_wB.issue("gpsimd", lambda e: e.dma_start(out=br_bf[:], in_=br_d))
                g_consts = ch_wB.issue("gpsimd", lambda e: e.dma_start(out=gam_sb[:], in_=gam_d))
                sp_b1 = ch_wB.issue("gpsimd", lambda e: e.dma_start(
                    out=w1aug[0:1, :], in_=b1_d), waits=[(SC, sc_w1aug0)])
                w1_issue(1)
                w1_issue(2)
                w1_issue(3)
                g_w2q = [None] * 4
                for q2 in range(4):
                    w2w = [(SC, sc_q0)] if q2 < 2 else ()
                    g_w2q[q2] = ch_wB.issue("gpsimd", lambda e, q2=q2: e.dma_start(
                        out=w2_sb[:].rearrange("p (kc c) -> p kc c", c=C)[:, 8 * q2:8 * (q2 + 1), :],
                        in_=w2T_d.rearrange("(kc p) c -> p kc c", p=P)[:, 8 * q2:8 * (q2 + 1), :]),
                        waits=w2w)

                def emit_stats(j):
                    jb = j % 4
                    xw = [(V, v_xn[j - 4])] if j >= 4 else []
                    xeng = "sync" if j < 22 else "scalar"
                    x_spec[j] = ch_x.issue(xeng, lambda e, j=j, jb=jb: e.dma_start(
                        out=xt[jb][:], in_=x_d[P * j:P * (j + 1), :]), waits=xw)
                    jc = j if j < 16 else 16 + (j % 4)
                    bd.at("vector", lambda e, jb=jb: e.bn_stats(bnbuf[:, 0:6], xt[jb][:, 0:512]),
                          waits=[x_spec[j]])
                    bd.at("vector", lambda e, jb=jb: e.bn_stats(bnbuf[:, 6:12], xt[jb][:, 512:1024]),
                          incs=[(V, 1)])
                    v_bn1 = bd.mark(V)
                    aw = [(V, v_bn1)]
                    if j >= 20:
                        aw.append((SC, sc_sd[j - 4]))
                    bd.at("vector", lambda e, jc=jc: e.bn_aggr(
                        ms_store[:, 2 * jc:2 * jc + 2], bnbuf[:]),
                        waits=aw, incs=[(V, 1)])
                    v_bn2[j] = bd.mark(V)
                    bd.at("scalar", lambda e, jc=jc, j=j: e.activation(
                        sd_t[:, j % 4:j % 4 + 1], ms_store[:, 2 * jc + 1:2 * jc + 2],
                        AF.Sqrt, bias=epsb[:]),
                        waits=[(V, v_bn2[j])], incs=[(SC, 1)])
                    sc_sd[j] = bd.mark(SC)

                def emit_norm(j):
                    jb = j % 4
                    jc = j if j < 16 else 16 + (j % 4)
                    bd.at("vector", lambda e, jc=jc, j=j: e.reciprocal(
                        rstd_store[:, jc:jc + 1], sd_t[:, j % 4:j % 4 + 1]),
                        waits=[(SC, sc_sd[j])], incs=[(V, 1)])
                    v_recip = bd.mark(V)
                    xnw = [(V, v_recip)]
                    if j >= 2:
                        xnw += [tok_spec[j - 2]]
                    bd.at("vector", lambda e, jc=jc, jb=jb, j=j: e.tensor_scalar(
                        xnb[j % 2][:], xt[jb][:], ms_store[:, 2 * jc:2 * jc + 1],
                        rstd_store[:, jc:jc + 1],
                        ALU.subtract, ALU.mult), waits=xnw, incs=[(V, 1)])
                    v_xn[j] = bd.mark(V)
                    tok_spec[j] = ch_tok.issue("scalar", lambda e, j=j: e.dma_start(
                        out=xntok_d[P * j:P * (j + 1), :], in_=xnb[j % 2][:]),
                        waits=[(V, v_xn[j])])

                def emit_stageB(nb):
                    nbb = nb % 2
                    stw = [tok_spec[4 * nb + i] for i in range(4)]
                    if nb >= 2:
                        stw.append((T, t_h1[nb - 2][NKC - 1]))
                    builders = []
                    for cc in range(NCC):
                        builders.append(lambda e, nb=nb, nbb=nbb, cc=cc: e.dma_start(
                            out=stg[nbb][:, 512 * cc:512 * (cc + 1)],
                            in_=xntok_d[512 * nb:512 * (nb + 1), P * cc:P * (cc + 1)],
                            transpose=True))
                    stg_spec[nb] = ch_stgB.issue_group("sync", builders, waits=stw)

                def emit_router(nb):
                    nbb = nb % 2
                    lgw = [stg_spec[nb], g_consts, sp_wr, sp_br]
                    if nb >= 1:
                        lgw.append((V, v_lgs[nb - 1]))
                    for ns in range(4):
                        for cc in range(NCC):
                            bd.at("tensor", lambda e, nbb=nbb, ns=ns, cc=cc: e.matmul(
                                lg_ps[:, NE * ns:NE * (ns + 1)],
                                stg[nbb][:, 512 * cc + P * ns:512 * cc + P * (ns + 1)],
                                wr_sb[:, NE * cc:NE * (cc + 1)],
                                start=(cc == 0), stop=False, skip_group_check=True),
                                waits=lgw if (ns == 0 and cc == 0) else ())
                        bd.at("tensor", lambda e, ns=ns: e.matmul(
                            lg_ps[:, NE * ns:NE * (ns + 1)], ones1_b[:], br_bf[:],
                            start=False, stop=(ns == 3), skip_group_check=True),
                            incs=[(T, 1)] if ns == 3 else ())
                    t_lg[nb] = bd.mark(T)
                    bd.at("vector", lambda e: e.tensor_copy(out=lgs[:], in_=lg_ps[:]),
                          waits=[(T, t_lg[nb])], incs=[(V, 1)])
                    v_lgs[nb] = bd.mark(V)
                    for ns in range(4):
                        pw = []
                        if not (nb == 0 and ns == 0):
                            pnb, pns = (nb, ns - 1) if ns > 0 else (nb - 1, 3)
                            pw = [(T, t_aug[pnb][pns])]
                        bd.at("vector", lambda e, ns=ns: e.max(out=t8[:], in_=lgs[:, NE * ns:NE * (ns + 1)]),
                              waits=[(V, v_lgs[nb])], incs=[(V, 1)])
                        v_max = bd.mark(V)
                        bd.at("vector", lambda e, ns=ns: e.tensor_scalar(
                            pack[:, 1:NE + 1], lgs[:, NE * ns:NE * (ns + 1)], t8[:, 4:5], None, ALU.is_le),
                            waits=[(V, v_max)], incs=[(V, 1)])
                        v_isle = bd.mark(V)
                        bd.at("vector", lambda e: e.tensor_copy(out=pk_b[:], in_=pack[:]),
                              waits=pw + [(V, v_isle)], incs=[(V, 1)])
                        v_pk = bd.mark(V)
                        idw = [sp_id] if (nb == 0 and ns == 0) else []
                        bd.at("tensor", lambda e: e.transpose(aug_ps[:], pk_b[:], ident[:]),
                              waits=idw + [(V, v_pk)], incs=[(T, 1)])
                        t_aug[nb][ns] = bd.mark(T)
                        aw = [(T, t_aug[nb][ns])]
                        if nb >= 2 and ns == 0:
                            aw.append((T, t_h1[nb - 2][NKC - 1]))
                        bd.at("vector", lambda e, ns=ns, nbb=nbb: e.tensor_copy(
                            out=xaug[nbb][:, P * ns:P * (ns + 1)], in_=aug_ps[:]),
                            waits=aw, incs=[(V, 1)])
                        v_xaug[nb][ns] = bd.mark(V)

                def emit_h1(nb, kc):
                    nbb = nb % 2
                    kb = kc % 3
                    hw = [g_w1[kc // 8], (V, v_xaug[nb][3])]
                    if nb == 0 and kc == 0:
                        hw += [sp_b1, sp_pat]
                    if kc >= 3:
                        hw.append((V, sc_relu[nb][kc - 3]))
                    elif nb >= 1:
                        hw.append((V, sc_relu[nb - 1][{0: 30, 1: 31, 2: 29}[kc % 3]]))
                    for cc in range(NCC):
                        bd.at("tensor", lambda e, nbb=nbb, kc=kc, cc=cc, kb=kb: e.matmul(
                            h1_ps[kb][:],
                            w1_sb[:, H * cc + P * kc:H * cc + P * (kc + 1)],
                            stg[nbb][:, 512 * cc:512 * (cc + 1)],
                            start=(cc == 0), stop=False),
                            waits=hw if cc == 0 else ())
                    bd.at("tensor", lambda e, kc=kc, kb=kb, nbb=nbb: e.matmul(
                        h1_ps[kb][:], w1aug[:, P * kc:P * (kc + 1)], xaug[nbb][:],
                        start=False, stop=True), incs=[(T, 1)])
                    t_h1[nb][kc] = bd.mark(T)
                    rw = [(T, t_h1[nb][kc])]
                    if nb >= 1:
                        rw.append((T, t_y[nb - 1][7]))
                    bd.at("vector", lambda e, kc=kc, kb=kb: e.tensor_scalar(
                        h1m[:, 512 * kc:512 * (kc + 1)], h1_ps[kb][:], 0.0, None,
                        ALU.max), waits=rw, incs=[(V, 1)])
                    sc_relu[nb][kc] = bd.mark(V)

                W1T = (2, 8, 14, 20)
                sched = {5: [lambda: emit_stageB(0)],
                         6: [lambda: emit_router(0)],
                         9: [lambda: emit_stageB(1)],
                         22: [lambda: emit_stageB(2)],
                         17: [lambda: emit_router(1)]}
                for kc in range(NKC):
                    t = max(7 + kc // 4, W1T[kc // 8] + 1)
                    sched.setdefault(t, []).append(
                        lambda nb=0, kc=kc: emit_h1(nb, kc))

                emit_stats(0)
                emit_stats(1)
                for j in range(2, 32):
                    emit_norm(j - 2)
                    emit_stats(j)
                    for fthunk in sched.get(j, []):
                        fthunk()
                emit_norm(30)
                emit_norm(31)
                vA_last = bd.mark(V)
                scA_last = bd.mark(SC)
                A_DONE = ([(V, vA_last), (SC, scA_last)]
                          + ch_x.done() + ch_tok.done())
                bd.emit(block)

            with ExitStack() as sB2:
                sbB2 = lambda n, s, d: sB2.enter_context(nc.sbuf_tensor(n, s, d))
                gate_sb = [sbB2(f"gate{i}", [P, C], BF16) for i in range(2)]
                b2_bf = sbB2("b2_bf", [1, C], BF16)
                sp_b2 = ch_wB.issue("gpsimd", lambda e: e.dma_start(
                    out=b2_bf[:], in_=b2_d), waits=A_DONE)

                def emit_y(nb):
                    for pi, (nsp, dh) in enumerate(((0, 0), (0, 1), (1, 0), (1, 1))):
                        for kq in range(4):
                            for kk in range(8):
                                kc = 8 * kq + kk
                                for nsl in range(2):
                                    ns = 2 * nsp + nsl
                                    yw = []
                                    if kk == 0 and nsl == 0:
                                        yw += [g_w2q[kq], (V, sc_relu[nb][8 * kq + 7])]
                                    if kc == 0:
                                        if pi >= 1:
                                            pns = 2 * ((pi - 1) // 2) + nsl
                                            pdh = (pi - 1) % 2
                                            yw.append((SC, sc_silu[nb][pns * 2 + pdh]))
                                        elif nb >= 1:
                                            yw.append((SC, sc_silu[nb - 1][(2 + nsl) * 2 + 1]))
                                    bd.at("tensor", lambda e, ns=ns, dh=dh, kc=kc, nsl=nsl: e.matmul(
                                        y_ps[nsl][:],
                                        h1m[:, 512 * kc + P * ns:512 * kc + P * (ns + 1)],
                                        w2_sb[:, C * kc + 512 * dh:C * kc + 512 * (dh + 1)],
                                        start=(kc == 0), stop=False),
                                        waits=yw)
                        for nsl in range(2):
                            ns = 2 * nsp + nsl
                            gidx = ns * 2 + dh
                            b2w = [sp_b2] if (nb == 0 and pi == 0) else []
                            bd.at("tensor", lambda e, dh=dh, nsl=nsl: e.matmul(
                                y_ps[nsl][:], ones1_b[:], b2_bf[0:1, 512 * dh:512 * (dh + 1)],
                                start=False, stop=True), waits=b2w, incs=[(T, 1)])
                            t_y[nb][gidx] = bd.mark(T)
                            sw = [(T, t_y[nb][gidx])]
                            if nb == 0 and gidx in (0, 2):
                                sw += A_DONE
                            if ns >= 2:
                                sw.append(gate_spec[nb][ns - 2])
                            elif nb >= 1:
                                sw.append(gate_spec[nb - 1][2 + ns])
                            bd.at("scalar", lambda e, ns=ns, dh=dh, nsl=nsl: e.activation(
                                gate_sb[ns % 2][:, 512 * dh:512 * (dh + 1)], y_ps[nsl][:], AF.Silu),
                                waits=sw, incs=[(SC, 1)])
                            sc_silu[nb][gidx] = bd.mark(SC)
                        if dh == 1:
                            for nsl in range(2):
                                ns = 2 * nsp + nsl
                                row = 512 * nb + P * ns
                                gate_spec[nb][ns] = ch_gate.issue("scalar", lambda e, ns=ns, row=row: e.dma_start(
                                    out=gate_d[row:row + P, :], in_=gate_sb[ns % 2][:]),
                                    waits=[(SC, sc_silu[nb][ns * 2 + 1])])

                emit_y(0)
                emit_router(2)
                for kc in range(NKC):
                    emit_h1(1, kc)
                emit_stageB(3)
                emit_y(1)
                emit_router(3)
                for kc in range(NKC):
                    emit_h1(2, kc)
                emit_y(2)
                for kc in range(NKC):
                    emit_h1(3, kc)
                emit_y(3)

                if dbg:
                    ch_wB.issue("gpsimd", lambda e: e.dma_start(out=dbg["dbg_lgs"], in_=lgs[:]),
                               waits=[(V, v_lgs[NBB - 1])])
                    ch_wB.issue("gpsimd", lambda e: e.dma_start(out=dbg["dbg_m"], in_=ms_store[:]))
                    ch_wB.issue("gpsimd", lambda e: e.dma_start(out=dbg["dbg_rstd"], in_=rstd_store[:]))
                tB_last = bd.mark(T)
                vB_last = bd.mark(V)
                scB_last = bd.mark(SC)
                B_DONE = ([(T, tB_last), (V, vB_last), (SC, scB_last)]
                          + ch_stgB.done() + ch_gate.done())
                bd.emit(block)


        with ExitStack() as sCD:
            sbCD = lambda n, s, d: sCD.enter_context(nc.sbuf_tensor(n, s, d))
            crow_bf = sbCD("crow_bf", [1, R], BF16)
            negones_b = sbCD("negones_b", [1, P], BF16)
            f_sb = sbCD("f_sb", [P, NCC * S], BF16)
            g_sb = sbCD("g_sb", [P, NCC * R], BF16)
            wv_sb = sbCD("wv_sb", [P, NCC * C], BF16)

            t_gmm = [[0] * NCC for _ in range(4)]
            v_gev = [[0] * NCC for _ in range(4)]
            sc_gsq = [[0] * NCC for _ in range(4)]
            t_nsq = [0] * 4
            sc_crow = [0] * 4
            t_fmm = [[0] * NCC for _ in range(8)]
            v_fev = [[0] * NCC for _ in range(8)]

            def stage_dmaC(stgc, st_i, readers_mark, first_waits):
                w = list(first_waits)
                if readers_mark:
                    w.append((T, readers_mark))
                builders = []
                for cc in range(NCC):
                    builders.append(lambda e, st_i=st_i, cc=cc, stgc=stgc: e.dma_start(
                        out=stgc[st_i % 2][:, 512 * cc:512 * (cc + 1)],
                        in_=xntok_d[512 * st_i:512 * (st_i + 1), P * cc:P * (cc + 1)],
                        transpose=True))
                return ch_stgC.issue_group("sync", builders, waits=w)

            with ExitStack() as sC1:
                sbC1 = lambda n, s, d: sC1.enter_context(nc.sbuf_tensor(n, s, d))
                psC1 = lambda n, s, d: sC1.enter_context(nc.psum_tensor(n, s, d))
                wk_sb = sbC1("wk_sb", [P, NCC * C], BF16)
                wq_sb = sbC1("wq_sb", [P, NCC * C], BF16)
                stgc = [sbC1(f"stgC{i}", [P, NCC * 512], BF16) for i in range(2)]
                gsq = sbC1("gsq", [P, 1024], F32)
                cps = [psC1(f"cps{i}", [P, 512], F32) for i in range(3)]
                nsq_ps = [psC1(f"nsq_ps{i}", [1, 512], F32) for i in range(2)]

                g_wk = ch_wB.issue("gpsimd", lambda e: e.dma_start(
                    out=wk_sb[:].rearrange("p (cc c) -> p cc c", c=C),
                    in_=wkT_d.rearrange("(cc p) c -> p cc c", p=P)), waits=B_DONE)
                g_wq = ch_wB.issue("gpsimd", lambda e: e.dma_start(
                    out=wq_sb[:].rearrange("p (cc c) -> p cc c", c=C),
                    in_=wqT_d.rearrange("(cc p) c -> p cc c", p=P)), waits=B_DONE)
                g_wv = ch_wB.issue("gpsimd", lambda e: e.dma_start(
                    out=wv_sb[:].rearrange("p (cc c) -> p cc c", c=C),
                    in_=wvT_d.rearrange("(cc p) c -> p cc c", p=P)), waits=B_DONE)

                gsq2 = [gsq[:, 0:512], gsq[:, 512:1024]]

                bd.at("vector", lambda e: e.memset(negones_b[:], -1.0),
                      waits=B_DONE, incs=[(V, 1)])

                def emit_nsq(st_i, dc):
                    nw = [(SC, sc_gsq[st_i][dc])]
                    if st_i >= 2 and dc == 0:
                        nw.append((SC, sc_crow[st_i - 2]))
                    bd.at("tensor", lambda e, st_i=st_i, dc=dc: e.matmul(
                        nsq_ps[st_i % 2][:], onescol_f[:], gsq2[(st_i * NCC + dc) % 2],
                        start=(dc == 0), stop=(dc == NCC - 1)),
                        waits=nw, incs=[(T, 1)] if dc == NCC - 1 else ())
                    if dc == NCC - 1:
                        t_nsq[st_i] = bd.mark(T)
                        bd.at("scalar", lambda e, st_i=st_i: e.activation(
                            crow_bf[0:1, 512 * st_i:512 * (st_i + 1)], nsq_ps[st_i % 2][:],
                            AF.Sqrt, scale=A_BOUND * A_BOUND),
                            waits=[(T, t_nsq[st_i])], incs=[(SC, 1)])
                        sc_crow[st_i] = bd.mark(SC)

                for st_i in range(4):
                    rm = t_gmm[st_i - 2][NCC - 1] if st_i >= 2 else 0
                    st_spec = stage_dmaC(stgc, st_i, rm, B_DONE + ch_tok.done())
                    for dc in range(NCC):
                        db = dc % 3
                        k = st_i * NCC + dc
                        gw = [g_wk, st_spec]
                        if k == 0:
                            gw += B_DONE
                        if k >= 3:
                            pk = k - 3
                            gw += [(V, v_gev[pk // NCC][pk % NCC]), (SC, sc_gsq[pk // NCC][pk % NCC])]
                        for cc in range(NCC):
                            bd.at("tensor", lambda e, st_i=st_i, dc=dc, cc=cc, db=db: e.matmul(
                                cps[db][:],
                                wk_sb[:, C * cc + P * dc:C * cc + P * (dc + 1)],
                                stgc[st_i % 2][:, 512 * cc:512 * (cc + 1)],
                                start=(cc == 0), stop=(cc == NCC - 1)),
                                waits=gw if cc == 0 else (),
                                incs=[(T, 1)] if cc == NCC - 1 else ())
                        t_gmm[st_i][dc] = bd.mark(T)
                        if k >= 1:
                            emit_nsq((k - 1) // NCC, (k - 1) % NCC)
                        bd.at("vector", lambda e, st_i=st_i, dc=dc, db=db: e.tensor_copy(
                            out=g_sb[:, R * dc + 512 * st_i:R * dc + 512 * (st_i + 1)], in_=cps[db][:]),
                            waits=[(T, t_gmm[st_i][dc])], incs=[(V, 1)])
                        v_gev[st_i][dc] = bd.mark(V)
                        bd.at("scalar", lambda e, db=db, k=k: e.activation(
                            gsq2[k % 2], cps[db][:], AF.Square),
                            waits=[(V, v_gev[st_i][dc])], incs=[(SC, 1)])
                        sc_gsq[st_i][dc] = bd.mark(SC)
                emit_nsq(3, NCC - 1)

                stgf = stgc
                fps = cps
                for st_i in range(8):
                    if st_i >= 2:
                        rm = t_fmm[st_i - 2][NCC - 1]
                    else:
                        rm = t_gmm[2 + st_i][NCC - 1]
                    st_spec = stage_dmaC(stgf, st_i, rm, [])
                    for dc in range(NCC):
                        db = dc % 3
                        k = st_i * NCC + dc
                        fw = [g_wq, st_spec]
                        if k >= 3:
                            pk = k - 3
                            fw.append((V, v_fev[pk // NCC][pk % NCC]))
                        else:
                            pk = {0: 30, 1: 31, 2: 29}[k % 3]
                            fw += [(V, v_gev[pk // NCC][pk % NCC]),
                                   (SC, sc_gsq[pk // NCC][pk % NCC])]
                        for cc in range(NCC):
                            bd.at("tensor", lambda e, st_i=st_i, dc=dc, cc=cc, db=db: e.matmul(
                                fps[db][:],
                                wq_sb[:, C * cc + P * dc:C * cc + P * (dc + 1)],
                                stgf[st_i % 2][:, 512 * cc:512 * (cc + 1)],
                                start=(cc == 0), stop=(cc == NCC - 1)),
                                waits=fw if cc == 0 else (),
                                incs=[(T, 1)] if cc == NCC - 1 else ())
                        t_fmm[st_i][dc] = bd.mark(T)
                        bd.at("vector", lambda e, st_i=st_i, dc=dc, db=db: e.tensor_copy(
                            out=f_sb[:, S * dc + 512 * st_i:S * dc + 512 * (st_i + 1)], in_=fps[db][:]),
                            waits=[(T, t_fmm[st_i][dc])], incs=[(V, 1)])
                        v_fev[st_i][dc] = bd.mark(V)

                if dbg:
                    ch_wB.issue("gpsimd", lambda e: e.dma_start(out=dbg["dbg_crow"], in_=crow_bf[:]),
                               waits=[(SC, sc_crow[3])])
                    ch_wB.issue("gpsimd", lambda e: e.dma_start(out=dbg["dbg_f"], in_=f_sb[:]),
                               waits=[(V, v_fev[7][NCC - 1])])
                    ch_wB.issue("gpsimd", lambda e: e.dma_start(out=dbg["dbg_g"], in_=g_sb[:]),
                               waits=[(V, v_gev[3][NCC - 1])])
                tC_last = bd.mark(T)
                vC_last = bd.mark(V)
                scC_last = bd.mark(SC)
                C_DONE = ([(T, tC_last), (V, vC_last), (SC, scC_last)]
                          + ch_stgC.done() + ch_gate.done())
                bd.emit(block)

            with ExitStack() as sD:
                sbD = lambda n, s, d: sD.enter_context(nc.sbuf_tensor(n, s, d))
                psD = lambda n, s, d: sD.enter_context(nc.psum_tensor(n, s, d))
                xn_sb = sbD("xn_sb", [P, NSC * C], BF16)
                e_buf = [sbD(f"e_buf{i}", [P, 512], BF16) for i in range(2)]
                E_sb = sbD("E_sb", [P, NCC * TCW], BF16)
                xtile = [sbD(f"xtile{i}", [P, C], F32) for i in range(2)]
                gate_t = [sbD(f"gate_t{i}", [P, C], BF16) for i in range(2)]
                osb = sbD("osb", [P, C], F32)
                acc = sbD("acc", [P, C], F32)
                scale_sb = sbD("scale_sb", [P, 2], F32)
                zrow = acc[0:1, 0:512]
                sc_ps = [psD(f"sc_ps{i}", [P, 512], F32) for i in range(2)]
                E_ps = psD("E_ps", [P, NCC * TCW], F32)
                z_ps = psD("z_ps", [P, 512], F32)
                o_ps = psD("o_ps", [P, 512], F32)

                xn_spec = []
                for q in range(4):
                    xn_spec.append(ch_big.issue("scalar", lambda e, q=q: e.dma_start(
                        out=xn_sb[:].rearrange("p (si c) -> p si c", c=C)[:, 8 * q:8 * (q + 1), :],
                        in_=xntok_d.rearrange("(si p) c -> p si c", p=P)[:, 8 * q:8 * (q + 1), :]),
                        waits=(C_DONE + ch_tok.done()) if q == 0 else ()))

                t_pair = [[0] * 16 for _ in range(NTC)]
                sc_exp = [[0] * 16 for _ in range(NTC)]
                t_EZ = [[0] * 16 for _ in range(NTC)]
                t_z = [0] * NTC
                v_Eev = [0] * NTC
                t_scmm = [0] * NTC
                v_scev = [0] * NTC
                v_zrow = [0] * NTC
                t_o = [[0] * 4 for _ in range(NTC)]
                sc_oev = [[0] * 4 for _ in range(NTC)]
                v_comb = [[0] * 2 for _ in range(NTC)]
                xt_spec = [[None] * 2 for _ in range(NTC)]
                gt_spec = [[None] * 2 for _ in range(NTC)]
                y_spec = [[None] * 2 for _ in range(NTC)]

                def emit_scores(tc, p):
                    w = [(V, vC_last), (SC, scC_last)]
                    if p >= 2:
                        w.append((SC, sc_exp[tc][p - 2]))
                    elif tc >= 1:
                        w.append((SC, sc_exp[tc - 1][14 + p]))
                        if p == 1:
                            w.append((V, v_scev[tc - 1]))
                    for hi in range(2):
                        si = 2 * p + hi
                        for dc in range(NCC):
                            bd.at("tensor", lambda e, tc=tc, p=p, hi=hi, si=si, dc=dc: e.matmul(
                                sc_ps[p % 2][:, 256 * hi:256 * (hi + 1)],
                                f_sb[:, S * dc + P * si:S * dc + P * (si + 1)],
                                g_sb[:, R * dc + TCW * tc:R * dc + TCW * (tc + 1)],
                                start=(hi == 0 and dc == 0), stop=False,
                                skip_group_check=True),
                                waits=w if (hi == 0 and dc == 0) else ())
                    bd.at("tensor", lambda e, p=p, tc=tc: e.matmul(
                        sc_ps[p % 2][:, 0:512], negones_b[:],
                        crow_bf[0:1, TCW * tc:TCW * (tc + 1)].unsqueeze(1).to_broadcast([1, 2, 256]),
                        start=False, stop=True, skip_group_check=True),
                        incs=[(T, 1)])
                    t_pair[tc][p] = bd.mark(T)

                def emit_exp(tc, p):
                    bd.at("scalar", lambda e, tc=tc, p=p: e.activation(
                        e_buf[p % 2][:], sc_ps[p % 2][:], AF.Exp),
                        waits=[(T, t_pair[tc][p])], incs=[(SC, 1)])
                    sc_exp[tc][p] = bd.mark(SC)

                def emit_EZ(tc, p, last=False):
                    w = [(SC, sc_exp[tc][p]), xn_spec[(2 * p) // 8], xn_spec[(2 * p + 1) // 8]]
                    if p == 0 and tc >= 1:
                        w += [(V, v_Eev[tc - 1])]
                    zw = ()
                    if p == 0 and tc >= 1:
                        zw = [(SC, sc_oev[tc - 1][3])]

                    def z_mm():
                        bd.at("tensor", lambda e, p=p: e.matmul(
                            z_ps[0:1, 0:512], onescol_b[:], e_buf[p % 2][:, 0:512],
                            start=(p == 0), stop=(p == 15), skip_group_check=True),
                            waits=zw, incs=[(T, 1)])
                    if last:
                        bd.at("tensor", None, waits=w)
                        z_mm()
                        t_z[tc] = bd.mark(T)
                    for hi in range(2):
                        si = 2 * p + hi
                        for cc in range(NCC):
                            bd.at("tensor", lambda e, si=si, cc=cc, p=p: e.matmul(
                                E_ps[:, TCW * cc:TCW * (cc + 1)],
                                xn_sb[:, C * si + P * cc:C * si + P * (cc + 1)],
                                e_buf[p % 2][:, 256 * hi:256 * (hi + 1)],
                                start=(si == 0 and cc % 2 == 0), stop=(si == NSC - 1),
                                skip_group_check=True),
                                waits=w if (hi == 0 and cc == 0) else (),
                                incs=[(T, 1)] if (last and hi == 1 and cc % 2 == 1) else ())
                            if last and hi == 1 and cc % 2 == 1:
                                b = cc // 2
                                tm = bd.mark(T)
                                bd.at("vector", lambda e, b=b: e.tensor_copy(
                                    out=E_sb[:, 512 * b:512 * (b + 1)],
                                    in_=E_ps[:, 512 * b:512 * (b + 1)]),
                                    waits=[(T, tm)],
                                    incs=[(V, 1)] if b == 3 else ())
                    if not last:
                        z_mm()
                    else:
                        v_Eev[tc] = bd.mark(V)
                    t_EZ[tc][p] = bd.mark(T)

                for tc in range(NTC):
                    for tsub in range(2):
                        row = TCW * tc + P * tsub
                        xw = [(V, v_comb[tc - 1][tsub])] if tc >= 1 else list(C_DONE)
                        xt_spec[tc][tsub] = ch_xt.issue("sync", lambda e, row=row, tsub=tsub: e.dma_start(
                            out=xtile[tsub][:], in_=x_d[row:row + P, :]), waits=xw)
                        gw = ch_gate.done()
                        if tc >= 1:
                            gw = gw + [(V, v_comb[tc - 1][tsub])]
                        else:
                            gw = gw + C_DONE
                        gt_spec[tc][tsub] = ch_gt.issue("sync", lambda e, row=row, tsub=tsub: e.dma_start(
                            out=gate_t[tsub][:], in_=gate_d[row:row + P, :]), waits=gw)

                    emit_scores(tc, 0)
                    emit_exp(tc, 0)
                    for p in range(1, 16):
                        emit_scores(tc, p)
                        emit_exp(tc, p)
                        emit_EZ(tc, p - 1)
                    emit_EZ(tc, 15, last=True)

                    bd.at("vector", lambda e: e.tensor_copy(out=zrow[:], in_=z_ps[0:1, 0:512]),
                          waits=[(T, t_z[tc])], incs=[(V, 1)])
                    v_zc = bd.mark(V)
                    bd.at("vector", lambda e: e.tensor_tensor(
                        zrow[0:1, 0:256], zrow[0:1, 0:256], zrow[0:1, 256:512], ALU.add),
                        waits=[(V, v_zc)], incs=[(V, 1)])
                    v_zf = bd.mark(V)
                    bd.at("vector", lambda e: e.reciprocal(zrow[0:1, 0:256], zrow[0:1, 0:256]),
                          waits=[(V, v_zf)], incs=[(V, 1)])
                    v_zr = bd.mark(V)
                    bd.at("vector", lambda e: e.tensor_scalar(
                        zrow[0:1, 0:256], zrow[0:1, 0:256], gam_sb[0:1, 0:1], None, ALU.mult),
                        waits=[(V, v_zr)], incs=[(V, 1)])
                    v_zrow[tc] = bd.mark(V)

                    for tsub in range(2):
                        for dh in range(2):
                            gidx = tsub * 2 + dh
                            ob = o_ps[:, 0:512] if gidx % 2 == 0 else z_ps[:, 0:512]
                            ow = [(V, v_Eev[tc]), g_wv]
                            if gidx == 1:
                                ow.append((V, v_zc))
                            if gidx >= 2:
                                ow.append((SC, sc_oev[tc][gidx - 2]))
                            elif tc >= 1:
                                ow.append((SC, sc_oev[tc - 1][gidx + 2]))
                            for cc in range(NCC):
                                bd.at("tensor", lambda e, tsub=tsub, dh=dh, cc=cc, ob=ob: e.matmul(
                                    ob,
                                    E_sb[:, TCW * cc + P * tsub:TCW * cc + P * (tsub + 1)],
                                    wv_sb[:, C * cc + 512 * dh:C * cc + 512 * (dh + 1)],
                                    start=(cc == 0), stop=(cc == NCC - 1),
                                    skip_group_check=True),
                                    waits=ow if cc == 0 else (),
                                    incs=[(T, 1)] if cc == NCC - 1 else ())
                            t_o[tc][gidx] = bd.mark(T)
                            if gidx == 0:
                                bd.at("tensor", lambda e: e.matmul(
                                    sc_ps[1][:, 0:1], zrow[0:1, 0:P], one11_f[:],
                                    start=True, stop=True, skip_group_check=True),
                                    waits=[(V, v_zrow[tc]), (SC, sc_exp[tc][15])])
                                bd.at("tensor", lambda e: e.matmul(
                                    sc_ps[1][:, 1:2], zrow[0:1, P:TCW], one11_f[:],
                                    start=True, stop=True, skip_group_check=True),
                                    incs=[(T, 1)])
                                t_scmm[tc] = bd.mark(T)
                                bd.at("vector", lambda e: e.tensor_copy(
                                    out=scale_sb[:], in_=sc_ps[1][:, 0:2]),
                                    waits=[(T, t_scmm[tc])], incs=[(V, 1)])
                                v_scev[tc] = bd.mark(V)
                            oew = [(T, t_o[tc][gidx]), (V, v_scev[tc])]
                            if dh == 0 and tsub == 0 and tc >= 1:
                                oew.append(y_spec[tc - 1][1])
                            if dh == 0 and tsub == 1:
                                oew.append(y_spec[tc][0])
                            bd.at("scalar", lambda e, tsub=tsub, dh=dh, ob=ob: e.activation(
                                osb[:, 512 * dh:512 * (dh + 1)], ob, AF.Identity,
                                scale=scale_sb[:, tsub:tsub + 1]),
                                waits=oew, incs=[(SC, 1)])
                            sc_oev[tc][gidx] = bd.mark(SC)
                        ti = 2 * tc + tsub
                        bd.at("vector", lambda e, ti=ti, tsub=tsub: e.tensor_scalar(
                            acc[:], xtile[tsub][:], ms_store[:, 2 * ti:2 * ti + 1],
                            rstd_store[:, ti:ti + 1],
                            ALU.subtract, ALU.mult),
                            waits=[xt_spec[tc][tsub]], incs=[(V, 1)])
                        v_c1 = bd.mark(V)
                        bd.at("vector", lambda e: e.tensor_tensor(acc[:], acc[:], osb[:], ALU.add),
                              waits=[(SC, sc_oev[tc][tsub * 2 + 1]), (V, v_c1)], incs=[(V, 1)])
                        v_c2 = bd.mark(V)
                        bd.at("vector", lambda e, tsub=tsub: e.tensor_tensor(
                            acc[:], acc[:], gate_t[tsub][:], ALU.mult),
                            waits=[gt_spec[tc][tsub], (V, v_c2)], incs=[(V, 1)])
                        v_c3 = bd.mark(V)
                        yw = [(V, v_c3)]
                        if tc >= 1:
                            yw.append(y_spec[tc - 1][tsub])
                        bd.at("vector", lambda e, tsub=tsub: e.tensor_tensor(
                            osb[:], acc[:], xtile[tsub][:], ALU.add),
                            waits=yw, incs=[(V, 1)])
                        v_comb[tc][tsub] = bd.mark(V)
                        row = TCW * tc + P * tsub
                        y_spec[tc][tsub] = ch_y.issue("sync", lambda e, row=row: e.dma_start(
                            out=y_d[row:row + P, :], in_=osb[:]),
                            waits=[(V, v_comb[tc][tsub])])

                if dbg:
                    ch_wB.issue("gpsimd", lambda e: e.dma_start(out=dbg["dbg_xnsb"], in_=xn_sb[:]),
                               waits=ch_big.done())
                    ch_wB.issue("gpsimd", lambda e: e.dma_start(out=dbg["dbg_E"], in_=E_sb[:]),
                               waits=[(V, v_Eev[NTC - 1])])
                    ch_wB.issue("gpsimd", lambda e: e.dma_start(out=dbg["dbg_scale"], in_=scale_sb[:]),
                               waits=[(V, v_scev[NTC - 1])])
                    bd.at("gpsimd", None, waits=ch_wB.done())
                bd.at("sync", None, waits=ch_y.done())
                bd.emit(block)

    return nc


_NC_CACHE = {}


def kernel(L0_attn_x, L0_attn_Wq, L0_attn_Wk, L0_attn_Wv, L0_attn_gamma,
           L0_ffn_w_router, L0_ffn_b_router, L0_ffn_w1, L0_ffn_b1,
           L0_ffn_w2, L0_ffn_b2):
    x = np.ascontiguousarray(np.asarray(L0_attn_x, np.float32))
    wqT = np.ascontiguousarray(np.asarray(L0_attn_Wq, np.float32).T)
    wkT = np.ascontiguousarray(np.asarray(L0_attn_Wk, np.float32).T)
    wvT = np.ascontiguousarray(np.asarray(L0_attn_Wv, np.float32).T)
    w1T = np.ascontiguousarray(np.asarray(L0_ffn_w1, np.float32).T)
    w2T = np.ascontiguousarray(np.asarray(L0_ffn_w2, np.float32).T)
    wr = np.ascontiguousarray(np.asarray(L0_ffn_w_router, np.float32))
    br = np.asarray(L0_ffn_b_router, np.float32).reshape(1, NE)
    b1 = np.asarray(L0_ffn_b1, np.float32).reshape(1, H)
    b2 = np.asarray(L0_ffn_b2, np.float32).reshape(1, C)
    gam = np.asarray(L0_attn_gamma, np.float32).reshape(1, 1)
    pat = np.zeros((NE, H), ml_dtypes.bfloat16)
    for e_i in range(NE):
        pat[e_i, 256 * e_i:256 * (e_i + 1)] = -1e30
    ident = np.eye(P).astype(ml_dtypes.bfloat16)

    if "nc" not in _NC_CACHE:
        _NC_CACHE["nc"] = build_nc()
    nc = _NC_CACHE["nc"]

    in_maps = []
    for core in range(8):
        b, half = core // 2, core % 2
        xb = x[b]
        if half == 0:
            x_perm = xb
        else:
            x_perm = np.ascontiguousarray(
                np.concatenate([xb[R:], xb[:R]], axis=0))
        in_maps.append(dict(
            x_perm=x_perm, wqT=wqT, wkT=wkT, wvT=wvT, w1T=w1T, w2T=w2T,
            wr=wr, br=br, b1=b1, b2=b2, gam=gam, pat=pat, ident=ident))

    res = run_bass_kernel_spmd(nc, in_maps, core_ids=list(range(8)))

    out = np.empty((4, S, C), np.float32)
    for core in range(8):
        b, half = core // 2, core % 2
        out[b, R * half:R * (half + 1)] = res.results[core]["y"]
    return out


# version 25
# speedup vs baseline: 1.0206x; 1.0195x over previous
import numpy as np
import ml_dtypes
from contextlib import ExitStack

import concourse.bass as bass
import concourse.mybir as mybir
from concourse.bass_utils import run_bass_kernel_spmd

F32 = mybir.dt.float32
BF16 = mybir.dt.bfloat16
AF = mybir.ActivationFunctionType
ALU = mybir.AluOpType
AX = mybir.AxisListType

P = 128
S = 4096
C = 1024
H = 4096
NE = 16
R = 2048
NCC = C // P
NKC = H // P
NBB = 4
TCW = 256
NTC = R // TCW
NSC = S // P
A_BOUND = 5.0
EPS = 1e-5

ENGINES = ("sync", "scalar", "vector", "tensor", "gpsimd")


class Sched:

    def __init__(self, nc, ctx):
        self.nc = nc
        self.ctx = ctx
        self.q = {e: [] for e in ENGINES}
        self.sem = {}
        self.cnt = {}
        for name in ("T", "V", "SC"):
            self.add_sem(name)

    def add_sem(self, name):
        self.sem[name] = self.ctx.enter_context(self.nc.semaphore(name))
        self.cnt[name] = 0

    def at(self, eng, builder, waits=(), incs=()):
        incs2 = []
        for s, amt in incs:
            self.cnt[s] += amt
            incs2.append((s, amt))
        self.q[eng].append((builder, tuple(waits), tuple(incs2)))

    def mark(self, s):
        return self.cnt[s]

    def emit(self, block):
        for eng in ENGINES:
            ops = self.q[eng]
            if not ops:
                continue
            waited = {}

            def body(e, ops=ops, waited=waited):
                for builder, waits, incs in ops:
                    for s, v in waits:
                        if v > 0 and waited.get(s, 0) < v:
                            e.wait_ge(self.sem[s], v)
                            waited[s] = v
                    ins = builder(e) if builder is not None else None
                    for s, amt in incs:
                        ins.then_inc(self.sem[s], amt)

            getattr(block, eng)(body)
        self.q = {e: [] for e in ENGINES}


class Chan:

    def __init__(self, bd, name, n=2, serialize=False):
        self.bd = bd
        self.n = n
        self.serialize = serialize
        self.names = []
        for i in range(n):
            nm = f"{name}{i}"
            bd.add_sem(nm)
            self.names.append(nm)
        self.i = 0
        self.last = None

    def issue(self, eng, builder, waits=()):
        return self.issue_group(eng, [builder], waits)

    def issue_group(self, eng, builders, waits=()):
        nm = self.names[self.i % self.n]
        w = list(waits)
        if self.serialize and self.last is not None:
            w.append(self.last)
        for bi, b in enumerate(builders):
            self.bd.at(eng, b, waits=w if bi == 0 else (), incs=[(nm, 16)])
        self.i += 1
        spec = (nm, self.bd.cnt[nm])
        self.last = spec
        return spec

    def done(self):
        return [(nm, self.bd.cnt[nm]) for nm in self.names]


def build_nc(debug=False):
    nc = bass.Bass("TRN2", target_bir_lowering=False)

    x_d = nc.dram_tensor("x_perm", [S, C], F32, kind="ExternalInput").ap()
    wqT_d = nc.dram_tensor("wqT", [C, C], F32, kind="ExternalInput").ap()
    wkT_d = nc.dram_tensor("wkT", [C, C], F32, kind="ExternalInput").ap()
    wvT_d = nc.dram_tensor("wvT", [C, C], F32, kind="ExternalInput").ap()
    w1T_d = nc.dram_tensor("w1T", [C, H], F32, kind="ExternalInput").ap()
    w2T_d = nc.dram_tensor("w2T", [H, C], F32, kind="ExternalInput").ap()
    wr_d = nc.dram_tensor("wr", [C, NE], F32, kind="ExternalInput").ap()
    br_d = nc.dram_tensor("br", [1, NE], F32, kind="ExternalInput").ap()
    b1_d = nc.dram_tensor("b1", [1, H], F32, kind="ExternalInput").ap()
    b2_d = nc.dram_tensor("b2", [1, C], F32, kind="ExternalInput").ap()
    gam_d = nc.dram_tensor("gam", [1, 1], F32, kind="ExternalInput").ap()
    pat_d = nc.dram_tensor("pat", [NE, H], BF16, kind="ExternalInput").ap()
    id_d = nc.dram_tensor("ident", [P, P], BF16, kind="ExternalInput").ap()

    y_d = nc.dram_tensor("y", [R, C], F32, kind="ExternalOutput").ap()

    scr_kind = "ExternalOutput" if debug else "Internal"
    xntok_d = nc.dram_tensor("xntok_scr", [S, C], BF16, kind=scr_kind).ap()
    gate_d = nc.dram_tensor("gate_scr", [R, C], BF16, kind=scr_kind).ap()

    dbg = {}
    if debug:
        for name, shape, dt in [
            ("dbg_crow", [1, R], BF16), ("dbg_f", [P, NCC * S], BF16),
            ("dbg_g", [P, NCC * R], BF16), ("dbg_E", [P, NCC * TCW], BF16),
            ("dbg_scale", [P, 2], F32),
            ("dbg_m", [P, 40], F32), ("dbg_rstd", [P, 20], F32),
            ("dbg_lgs", [P, 64], F32), ("dbg_xnsb", [P, NSC * C], BF16),
        ]:
            dbg[name] = nc.dram_tensor(name, shape, dt, kind="ExternalOutput").ap()

    with ExitStack() as top:
        sb = lambda n, s, d: top.enter_context(nc.sbuf_tensor(n, s, d))
        bd = Sched(nc, top)
        block = top.enter_context(nc.Block())

        T, V, SC = "T", "V", "SC"
        ch_wB = Chan(bd, "cW", 4)
        ch_aux = Chan(bd, "cAX", 2)
        ch_x = Chan(bd, "cX", 4)
        ch_tok = Chan(bd, "cT", 4)
        ch_big = Chan(bd, "cB", 2)
        ch_stgB = Chan(bd, "cSB", 2)
        ch_gate = Chan(bd, "cG", 2)
        ch_stgC = Chan(bd, "cSC", 2)
        ch_xt = Chan(bd, "cXT", 2)
        ch_gt = Chan(bd, "cGT", 2)
        ch_y = Chan(bd, "cY", 2)

        onescol_b = sb("onescol_b", [P, 1], BF16)
        onescol_f = sb("onescol_f", [P, 1], F32)
        ones1_b = sb("ones1_b", [1, P], BF16)
        one11_f = sb("one11_f", [1, 1], F32)
        gam_sb = sb("gam_sb", [1, 1], F32)
        epsb = sb("epsb", [P, 1], F32)
        ms_store = sb("ms_store", [P, 40], F32)
        rstd_store = sb("rstd_store", [P, 20], F32)

        with ExitStack() as sAB:
            sbAB = lambda n, s, d: sAB.enter_context(nc.sbuf_tensor(n, s, d))
            psAB = lambda n, s, d: sAB.enter_context(nc.psum_tensor(n, s, d))
            w1_sb = sbAB("w1_sb", [P, NCC * H], BF16)
            w2_sb = sbAB("w2_sb", [P, NKC * C], BF16)
            ident = sbAB("identc", [P, P], BF16)
            w1aug = sbAB("w1aug", [32, H], BF16)
            wr_sb = sbAB("wr_sb", [P, NCC * NE], BF16)
            br_bf = sbAB("br_bf", [1, NE], BF16)
            stg = [sbAB(f"stgB{i}", [P, NCC * 512], BF16) for i in range(2)]
            h1m = sbAB("h1m", [P, NKC * 512], BF16)
            xaug = [sbAB(f"xaug{i}", [32, 512], BF16) for i in range(2)]
            lgs = sbAB("lgs", [P, 4 * NE], F32)
            t8 = sbAB("t8", [P, 8], F32)
            pack = sbAB("pack", [P, 32], F32)
            pk_b = sbAB("pk_b", [P, 32], BF16)
            h1_ps = [psAB(f"h1_ps{i}", [P, 512], F32) for i in range(3)]
            y_ps = [psAB(f"y_ps{i}", [P, 512], F32) for i in range(2)]
            lg_ps = psAB("lg_ps", [P, 4 * NE], F32)
            aug_ps = psAB("aug_ps", [32, P], BF16)

            x_spec = [None] * 32
            tok_spec = [None] * 32
            sc_xn = [0] * 32
            v_bn2 = [0] * 32
            sc_sd = [0] * 32
            g_w1 = [None] * 4
            stg_spec = [None] * NBB
            t_h1 = [[0] * NKC for _ in range(NBB)]
            sc_relu = [[0] * NKC for _ in range(NBB)]
            t_y = [[0] * 8 for _ in range(NBB)]
            sc_silu = [[0] * 8 for _ in range(NBB)]
            gate_spec = [[None] * 4 for _ in range(NBB)]
            t_lg = [0] * NBB
            v_lgs = [0] * NBB
            t_aug = [[0] * 4 for _ in range(NBB)]
            v_xaug = [[0] * 4 for _ in range(NBB)]

            with ExitStack() as sA:
                sbA = lambda n, s, d: sA.enter_context(nc.sbuf_tensor(n, s, d))
                xt = [sbA(f"xt{i}", [P, C], F32) for i in range(4)]
                xnb = [sbA(f"xnb{i}", [P, C], BF16) for i in range(2)]
                bnbuf = sbA("bnbuf", [P, 12], F32)
                sd_t = sbA("sd_t", [P, 4], F32)
                negmr = sbA("negmr", [P, 20], F32)

                bd.at("scalar", lambda e: e.memzero(w1aug[:]), incs=[(SC, 1)])
                sc_w1aug0 = bd.mark(SC)
                bd.at("vector", lambda e: e.memset(onescol_b[:], 1.0))
                bd.at("vector", lambda e: e.memset(onescol_f[:], 1.0))
                bd.at("vector", lambda e: e.memset(ones1_b[:], 1.0))
                bd.at("vector", lambda e: e.memset(one11_f[:], 1.0))
                bd.at("vector", lambda e: e.memset(epsb[:], EPS))
                bd.at("vector", lambda e: e.memset(pack[:, 0:1], 1.0))
                bd.at("vector", lambda e: e.memset(pack[:, NE + 1:32], 0.0), incs=[(V, 1)])

                sp_id = ch_aux.issue("scalar", lambda e: e.dma_start(out=ident[:], in_=id_d))
                sp_pat = ch_aux.issue("scalar", lambda e: e.dma_start(
                    out=w1aug[1:NE + 1, :], in_=pat_d), waits=[(SC, sc_w1aug0)])
                def w1_issue(q):
                    g_w1[q] = ch_wB.issue("gpsimd", lambda e, q=q: e.dma_start(
                        out=w1_sb[:].rearrange("p (cc h) -> p cc h", h=H)[:, :, 1024 * q:1024 * (q + 1)],
                        in_=w1T_d.rearrange("(cc p) h -> p cc h", p=P)[:, :, 1024 * q:1024 * (q + 1)]))
                w1_issue(0)
                sp_wr = ch_wB.issue("gpsimd", lambda e: e.dma_start(
                    out=wr_sb[:].rearrange("p (cc e) -> p cc e", e=NE),
                    in_=wr_d.rearrange("(cc p) e -> p cc e", p=P)))
                sp_br = ch_wB.issue("gpsimd", lambda e: e.dma_start(out=br_bf[:], in_=br_d))
                g_consts = ch_wB.issue("gpsimd", lambda e: e.dma_start(out=gam_sb[:], in_=gam_d))
                sp_b1 = ch_wB.issue("gpsimd", lambda e: e.dma_start(
                    out=w1aug[0:1, :], in_=b1_d), waits=[(SC, sc_w1aug0)])
                w1_issue(1)
                w1_issue(2)
                w1_issue(3)
                g_w2q = [None] * 4
                for q2 in range(4):
                    g_w2q[q2] = ch_wB.issue("gpsimd", lambda e, q2=q2: e.dma_start(
                        out=w2_sb[:].rearrange("p (kc c) -> p kc c", c=C)[:, 8 * q2:8 * (q2 + 1), :],
                        in_=w2T_d.rearrange("(kc p) c -> p kc c", p=P)[:, 8 * q2:8 * (q2 + 1), :]))

                def emit_stats(j):
                    jb = j % 4
                    xw = [(SC, sc_xn[j - 4])] if j >= 4 else []
                    x_spec[j] = ch_x.issue("sync", lambda e, j=j, jb=jb: e.dma_start(
                        out=xt[jb][:], in_=x_d[P * j:P * (j + 1), :]), waits=xw)
                    jc = j if j < 16 else 16 + (j % 4)
                    bd.at("vector", lambda e, jb=jb: e.bn_stats(bnbuf[:, 0:6], xt[jb][:, 0:512]),
                          waits=[x_spec[j]])
                    bd.at("vector", lambda e, jb=jb: e.bn_stats(bnbuf[:, 6:12], xt[jb][:, 512:1024]),
                          incs=[(V, 1)])
                    v_bn1 = bd.mark(V)
                    aw = [(V, v_bn1)]
                    if j >= 20:
                        aw.append((SC, sc_sd[j - 4]))
                    bd.at("vector", lambda e, jc=jc: e.bn_aggr(
                        ms_store[:, 2 * jc:2 * jc + 2], bnbuf[:]),
                        waits=aw, incs=[(V, 1)])
                    v_bn2[j] = bd.mark(V)
                    bd.at("scalar", lambda e, jc=jc, j=j: e.activation(
                        sd_t[:, j % 4:j % 4 + 1], ms_store[:, 2 * jc + 1:2 * jc + 2],
                        AF.Sqrt, bias=epsb[:]),
                        waits=[(V, v_bn2[j])], incs=[(SC, 1)])
                    sc_sd[j] = bd.mark(SC)

                def emit_norm(j):
                    jb = j % 4
                    jc = j if j < 16 else 16 + (j % 4)
                    rw = [(SC, sc_sd[j])]
                    if j >= 20:
                        rw.append((SC, sc_xn[j - 4]))
                    bd.at("vector", lambda e, jc=jc, j=j: e.reciprocal(
                        rstd_store[:, jc:jc + 1], sd_t[:, j % 4:j % 4 + 1]),
                        waits=rw, incs=[(V, 1)])
                    v_recip = bd.mark(V)
                    bd.at("vector", lambda e, jc=jc: e.tensor_scalar(
                        negmr[:, jc:jc + 1], ms_store[:, 2 * jc:2 * jc + 1],
                        rstd_store[:, jc:jc + 1], -1.0,
                        ALU.mult, ALU.mult), waits=[(V, v_recip)], incs=[(V, 1)])
                    v_negmr = bd.mark(V)
                    xnw = [(V, v_negmr)]
                    if j >= 2:
                        xnw += [tok_spec[j - 2]]
                    bd.at("scalar", lambda e, jc=jc, jb=jb, j=j: e.activation(
                        xnb[j % 2][:], xt[jb][:], AF.Identity,
                        bias=negmr[:, jc:jc + 1], scale=rstd_store[:, jc:jc + 1]),
                        waits=xnw, incs=[(SC, 1)])
                    sc_xn[j] = bd.mark(SC)
                    tok_spec[j] = ch_tok.issue("scalar", lambda e, j=j: e.dma_start(
                        out=xntok_d[P * j:P * (j + 1), :], in_=xnb[j % 2][:]),
                        waits=[(SC, sc_xn[j])])

                def emit_stageB(nb):
                    nbb = nb % 2
                    stw = [tok_spec[4 * nb + i] for i in range(4)]
                    if nb >= 2:
                        stw.append((T, t_h1[nb - 2][NKC - 1]))
                    builders = []
                    for cc in range(NCC):
                        builders.append(lambda e, nb=nb, nbb=nbb, cc=cc: e.dma_start(
                            out=stg[nbb][:, 512 * cc:512 * (cc + 1)],
                            in_=xntok_d[512 * nb:512 * (nb + 1), P * cc:P * (cc + 1)],
                            transpose=True))
                    stg_spec[nb] = ch_stgB.issue_group("sync", builders, waits=stw)

                def emit_router(nb):
                    nbb = nb % 2
                    lgw = [stg_spec[nb], g_consts, sp_wr, sp_br]
                    if nb >= 1:
                        lgw.append((V, v_lgs[nb - 1]))
                    for ns in range(4):
                        for cc in range(NCC):
                            bd.at("tensor", lambda e, nbb=nbb, ns=ns, cc=cc: e.matmul(
                                lg_ps[:, NE * ns:NE * (ns + 1)],
                                stg[nbb][:, 512 * cc + P * ns:512 * cc + P * (ns + 1)],
                                wr_sb[:, NE * cc:NE * (cc + 1)],
                                start=(cc == 0), stop=False, skip_group_check=True),
                                waits=lgw if (ns == 0 and cc == 0) else ())
                        bd.at("tensor", lambda e, ns=ns: e.matmul(
                            lg_ps[:, NE * ns:NE * (ns + 1)], ones1_b[:], br_bf[:],
                            start=False, stop=(ns == 3), skip_group_check=True),
                            incs=[(T, 1)] if ns == 3 else ())
                    t_lg[nb] = bd.mark(T)
                    bd.at("vector", lambda e: e.tensor_copy(out=lgs[:], in_=lg_ps[:]),
                          waits=[(T, t_lg[nb])], incs=[(V, 1)])
                    v_lgs[nb] = bd.mark(V)
                    for ns in range(4):
                        pw = []
                        if not (nb == 0 and ns == 0):
                            pnb, pns = (nb, ns - 1) if ns > 0 else (nb - 1, 3)
                            pw = [(T, t_aug[pnb][pns])]
                        bd.at("vector", lambda e, ns=ns: e.max(out=t8[:], in_=lgs[:, NE * ns:NE * (ns + 1)]),
                              waits=[(V, v_lgs[nb])], incs=[(V, 1)])
                        v_max = bd.mark(V)
                        bd.at("vector", lambda e, ns=ns: e.tensor_scalar(
                            pack[:, 1:NE + 1], lgs[:, NE * ns:NE * (ns + 1)], t8[:, 4:5], None, ALU.is_le),
                            waits=[(V, v_max)], incs=[(V, 1)])
                        v_isle = bd.mark(V)
                        bd.at("vector", lambda e: e.tensor_copy(out=pk_b[:], in_=pack[:]),
                              waits=pw + [(V, v_isle)], incs=[(V, 1)])
                        v_pk = bd.mark(V)
                        idw = [sp_id] if (nb == 0 and ns == 0) else []
                        bd.at("tensor", lambda e: e.transpose(aug_ps[:], pk_b[:], ident[:]),
                              waits=idw + [(V, v_pk)], incs=[(T, 1)])
                        t_aug[nb][ns] = bd.mark(T)
                        aw = [(T, t_aug[nb][ns])]
                        if nb >= 2 and ns == 0:
                            aw.append((T, t_h1[nb - 2][NKC - 1]))
                        bd.at("vector", lambda e, ns=ns, nbb=nbb: e.tensor_copy(
                            out=xaug[nbb][:, P * ns:P * (ns + 1)], in_=aug_ps[:]),
                            waits=aw, incs=[(V, 1)])
                        v_xaug[nb][ns] = bd.mark(V)

                def emit_h1(nb, kc):
                    nbb = nb % 2
                    kb = kc % 3
                    hw = [g_w1[kc // 8], (V, v_xaug[nb][3])]
                    if nb == 0 and kc == 0:
                        hw += [sp_b1, sp_pat]
                    if kc >= 3:
                        hw.append((SC, sc_relu[nb][kc - 3]))
                    elif nb >= 1:
                        hw.append((SC, sc_relu[nb - 1][{0: 30, 1: 31, 2: 29}[kc % 3]]))
                    for cc in range(NCC):
                        bd.at("tensor", lambda e, nbb=nbb, kc=kc, cc=cc, kb=kb: e.matmul(
                            h1_ps[kb][:],
                            w1_sb[:, H * cc + P * kc:H * cc + P * (kc + 1)],
                            stg[nbb][:, 512 * cc:512 * (cc + 1)],
                            start=(cc == 0), stop=False),
                            waits=hw if cc == 0 else ())
                    bd.at("tensor", lambda e, kc=kc, kb=kb, nbb=nbb: e.matmul(
                        h1_ps[kb][:], w1aug[:, P * kc:P * (kc + 1)], xaug[nbb][:],
                        start=False, stop=True), incs=[(T, 1)])
                    t_h1[nb][kc] = bd.mark(T)
                    rw = [(T, t_h1[nb][kc])]
                    if nb >= 1:
                        rw.append((T, t_y[nb - 1][7]))
                    bd.at("scalar", lambda e, kc=kc, kb=kb: e.activation(
                        h1m[:, 512 * kc:512 * (kc + 1)], h1_ps[kb][:], AF.Relu),
                        waits=rw, incs=[(SC, 1)])
                    sc_relu[nb][kc] = bd.mark(SC)

                W1T = (2, 8, 14, 20)
                sched = {5: [lambda: emit_stageB(0)],
                         6: [lambda: emit_router(0)],
                         9: [lambda: emit_stageB(1)],
                         22: [lambda: emit_stageB(2)],
                         17: [lambda: emit_router(1)]}
                for kc in range(NKC):
                    t = max(7 + kc // 4, W1T[kc // 8] + 1)
                    sched.setdefault(t, []).append(
                        lambda nb=0, kc=kc: emit_h1(nb, kc))

                emit_stats(0)
                emit_stats(1)
                for j in range(2, 32):
                    emit_norm(j - 2)
                    emit_stats(j)
                    for fthunk in sched.get(j, []):
                        fthunk()
                emit_norm(30)
                emit_norm(31)
                vA_last = bd.mark(V)
                scA_last = bd.mark(SC)
                A_DONE = ([(V, vA_last), (SC, scA_last)]
                          + ch_x.done() + ch_tok.done())
                bd.emit(block)

            with ExitStack() as sB2:
                sbB2 = lambda n, s, d: sB2.enter_context(nc.sbuf_tensor(n, s, d))
                gate_sb = [sbB2(f"gate{i}", [P, C], BF16) for i in range(2)]
                b2_bf = sbB2("b2_bf", [1, C], BF16)
                sp_b2 = ch_wB.issue("gpsimd", lambda e: e.dma_start(
                    out=b2_bf[:], in_=b2_d), waits=A_DONE)

                def emit_y(nb):
                    for pi, (nsp, dh) in enumerate(((0, 0), (0, 1), (1, 0), (1, 1))):
                        for kq in range(4):
                            for kk in range(8):
                                kc = 8 * kq + kk
                                for nsl in range(2):
                                    ns = 2 * nsp + nsl
                                    yw = []
                                    if kk == 0 and nsl == 0:
                                        yw += [g_w2q[kq], (SC, sc_relu[nb][8 * kq + 7])]
                                    if kc == 0:
                                        if pi >= 1:
                                            pns = 2 * ((pi - 1) // 2) + nsl
                                            pdh = (pi - 1) % 2
                                            yw.append((SC, sc_silu[nb][pns * 2 + pdh]))
                                        elif nb >= 1:
                                            yw.append((SC, sc_silu[nb - 1][(2 + nsl) * 2 + 1]))
                                    bd.at("tensor", lambda e, ns=ns, dh=dh, kc=kc, nsl=nsl: e.matmul(
                                        y_ps[nsl][:],
                                        h1m[:, 512 * kc + P * ns:512 * kc + P * (ns + 1)],
                                        w2_sb[:, C * kc + 512 * dh:C * kc + 512 * (dh + 1)],
                                        start=(kc == 0), stop=False),
                                        waits=yw)
                        for nsl in range(2):
                            ns = 2 * nsp + nsl
                            gidx = ns * 2 + dh
                            b2w = [sp_b2] if (nb == 0 and pi == 0) else []
                            bd.at("tensor", lambda e, dh=dh, nsl=nsl: e.matmul(
                                y_ps[nsl][:], ones1_b[:], b2_bf[0:1, 512 * dh:512 * (dh + 1)],
                                start=False, stop=True), waits=b2w, incs=[(T, 1)])
                            t_y[nb][gidx] = bd.mark(T)
                            sw = [(T, t_y[nb][gidx])]
                            if nb == 0 and gidx in (0, 2):
                                sw += A_DONE
                            if ns >= 2:
                                sw.append(gate_spec[nb][ns - 2])
                            elif nb >= 1:
                                sw.append(gate_spec[nb - 1][2 + ns])
                            bd.at("scalar", lambda e, ns=ns, dh=dh, nsl=nsl: e.activation(
                                gate_sb[ns % 2][:, 512 * dh:512 * (dh + 1)], y_ps[nsl][:], AF.Silu),
                                waits=sw, incs=[(SC, 1)])
                            sc_silu[nb][gidx] = bd.mark(SC)
                        if dh == 1:
                            for nsl in range(2):
                                ns = 2 * nsp + nsl
                                row = 512 * nb + P * ns
                                gate_spec[nb][ns] = ch_gate.issue("scalar", lambda e, ns=ns, row=row: e.dma_start(
                                    out=gate_d[row:row + P, :], in_=gate_sb[ns % 2][:]),
                                    waits=[(SC, sc_silu[nb][ns * 2 + 1])])

                emit_y(0)
                emit_router(2)
                for kc in range(NKC):
                    emit_h1(1, kc)
                emit_stageB(3)
                emit_y(1)
                emit_router(3)
                for kc in range(NKC):
                    emit_h1(2, kc)
                emit_y(2)
                for kc in range(NKC):
                    emit_h1(3, kc)
                emit_y(3)

                if dbg:
                    ch_wB.issue("gpsimd", lambda e: e.dma_start(out=dbg["dbg_lgs"], in_=lgs[:]),
                               waits=[(V, v_lgs[NBB - 1])])
                    ch_wB.issue("gpsimd", lambda e: e.dma_start(out=dbg["dbg_m"], in_=ms_store[:]))
                    ch_wB.issue("gpsimd", lambda e: e.dma_start(out=dbg["dbg_rstd"], in_=rstd_store[:]))
                tB_last = bd.mark(T)
                vB_last = bd.mark(V)
                scB_last = bd.mark(SC)
                B_DONE = ([(T, tB_last), (V, vB_last), (SC, scB_last)]
                          + ch_stgB.done() + ch_gate.done())
                bd.emit(block)


        with ExitStack() as sCD:
            sbCD = lambda n, s, d: sCD.enter_context(nc.sbuf_tensor(n, s, d))
            crow_bf = sbCD("crow_bf", [1, R], BF16)
            negones_b = sbCD("negones_b", [1, P], BF16)
            f_sb = sbCD("f_sb", [P, NCC * S], BF16)
            g_sb = sbCD("g_sb", [P, NCC * R], BF16)
            wv_sb = sbCD("wv_sb", [P, NCC * C], BF16)

            t_gmm = [[0] * NCC for _ in range(4)]
            v_gev = [[0] * NCC for _ in range(4)]
            sc_gsq = [[0] * NCC for _ in range(4)]
            t_nsq = [0] * 4
            sc_crow = [0] * 4
            t_fmm = [[0] * NCC for _ in range(8)]
            v_fev = [[0] * NCC for _ in range(8)]

            def stage_dmaC(stgc, st_i, readers_mark, first_waits):
                w = list(first_waits)
                if readers_mark:
                    w.append((T, readers_mark))
                builders = []
                for cc in range(NCC):
                    builders.append(lambda e, st_i=st_i, cc=cc, stgc=stgc: e.dma_start(
                        out=stgc[st_i % 2][:, 512 * cc:512 * (cc + 1)],
                        in_=xntok_d[512 * st_i:512 * (st_i + 1), P * cc:P * (cc + 1)],
                        transpose=True))
                return ch_stgC.issue_group("sync", builders, waits=w)

            with ExitStack() as sC1:
                sbC1 = lambda n, s, d: sC1.enter_context(nc.sbuf_tensor(n, s, d))
                psC1 = lambda n, s, d: sC1.enter_context(nc.psum_tensor(n, s, d))
                wk_sb = sbC1("wk_sb", [P, NCC * C], BF16)
                wq_sb = sbC1("wq_sb", [P, NCC * C], BF16)
                stgc = [sbC1(f"stgC{i}", [P, NCC * 512], BF16) for i in range(2)]
                gsq = sbC1("gsq", [P, 1024], F32)
                cps = [psC1(f"cps{i}", [P, 512], F32) for i in range(3)]
                nsq_ps = [psC1(f"nsq_ps{i}", [1, 512], F32) for i in range(2)]

                g_wk = ch_wB.issue("gpsimd", lambda e: e.dma_start(
                    out=wk_sb[:].rearrange("p (cc c) -> p cc c", c=C),
                    in_=wkT_d.rearrange("(cc p) c -> p cc c", p=P)), waits=B_DONE)
                g_wq = ch_wB.issue("gpsimd", lambda e: e.dma_start(
                    out=wq_sb[:].rearrange("p (cc c) -> p cc c", c=C),
                    in_=wqT_d.rearrange("(cc p) c -> p cc c", p=P)), waits=B_DONE)
                g_wv = ch_wB.issue("gpsimd", lambda e: e.dma_start(
                    out=wv_sb[:].rearrange("p (cc c) -> p cc c", c=C),
                    in_=wvT_d.rearrange("(cc p) c -> p cc c", p=P)), waits=B_DONE)

                gsq2 = [gsq[:, 0:512], gsq[:, 512:1024]]

                bd.at("vector", lambda e: e.memset(negones_b[:], -1.0),
                      waits=B_DONE, incs=[(V, 1)])

                def emit_nsq(st_i, dc):
                    nw = [(SC, sc_gsq[st_i][dc])]
                    if st_i >= 2 and dc == 0:
                        nw.append((SC, sc_crow[st_i - 2]))
                    bd.at("tensor", lambda e, st_i=st_i, dc=dc: e.matmul(
                        nsq_ps[st_i % 2][:], onescol_f[:], gsq2[(st_i * NCC + dc) % 2],
                        start=(dc == 0), stop=(dc == NCC - 1)),
                        waits=nw, incs=[(T, 1)] if dc == NCC - 1 else ())
                    if dc == NCC - 1:
                        t_nsq[st_i] = bd.mark(T)
                        bd.at("scalar", lambda e, st_i=st_i: e.activation(
                            crow_bf[0:1, 512 * st_i:512 * (st_i + 1)], nsq_ps[st_i % 2][:],
                            AF.Sqrt, scale=A_BOUND * A_BOUND),
                            waits=[(T, t_nsq[st_i])], incs=[(SC, 1)])
                        sc_crow[st_i] = bd.mark(SC)

                for st_i in range(4):
                    rm = t_gmm[st_i - 2][NCC - 1] if st_i >= 2 else 0
                    st_spec = stage_dmaC(stgc, st_i, rm, B_DONE + ch_tok.done())
                    for dc in range(NCC):
                        db = dc % 3
                        k = st_i * NCC + dc
                        gw = [g_wk, st_spec]
                        if k == 0:
                            gw += B_DONE
                        if k >= 3:
                            pk = k - 3
                            gw += [(V, v_gev[pk // NCC][pk % NCC]), (SC, sc_gsq[pk // NCC][pk % NCC])]
                        for cc in range(NCC):
                            bd.at("tensor", lambda e, st_i=st_i, dc=dc, cc=cc, db=db: e.matmul(
                                cps[db][:],
                                wk_sb[:, C * cc + P * dc:C * cc + P * (dc + 1)],
                                stgc[st_i % 2][:, 512 * cc:512 * (cc + 1)],
                                start=(cc == 0), stop=(cc == NCC - 1)),
                                waits=gw if cc == 0 else (),
                                incs=[(T, 1)] if cc == NCC - 1 else ())
                        t_gmm[st_i][dc] = bd.mark(T)
                        if k >= 1:
                            emit_nsq((k - 1) // NCC, (k - 1) % NCC)
                        bd.at("vector", lambda e, st_i=st_i, dc=dc, db=db: e.tensor_copy(
                            out=g_sb[:, R * dc + 512 * st_i:R * dc + 512 * (st_i + 1)], in_=cps[db][:]),
                            waits=[(T, t_gmm[st_i][dc])], incs=[(V, 1)])
                        v_gev[st_i][dc] = bd.mark(V)
                        bd.at("scalar", lambda e, db=db, k=k: e.activation(
                            gsq2[k % 2], cps[db][:], AF.Square),
                            waits=[(V, v_gev[st_i][dc])], incs=[(SC, 1)])
                        sc_gsq[st_i][dc] = bd.mark(SC)
                emit_nsq(3, NCC - 1)

                stgf = stgc
                fps = cps
                for st_i in range(8):
                    if st_i >= 2:
                        rm = t_fmm[st_i - 2][NCC - 1]
                    else:
                        rm = t_gmm[2 + st_i][NCC - 1]
                    st_spec = stage_dmaC(stgf, st_i, rm, [])
                    for dc in range(NCC):
                        db = dc % 3
                        k = st_i * NCC + dc
                        fw = [g_wq, st_spec]
                        if k >= 3:
                            pk = k - 3
                            fw.append((V, v_fev[pk // NCC][pk % NCC]))
                        else:
                            pk = {0: 30, 1: 31, 2: 29}[k % 3]
                            fw += [(V, v_gev[pk // NCC][pk % NCC]),
                                   (SC, sc_gsq[pk // NCC][pk % NCC])]
                        for cc in range(NCC):
                            bd.at("tensor", lambda e, st_i=st_i, dc=dc, cc=cc, db=db: e.matmul(
                                fps[db][:],
                                wq_sb[:, C * cc + P * dc:C * cc + P * (dc + 1)],
                                stgf[st_i % 2][:, 512 * cc:512 * (cc + 1)],
                                start=(cc == 0), stop=(cc == NCC - 1)),
                                waits=fw if cc == 0 else (),
                                incs=[(T, 1)] if cc == NCC - 1 else ())
                        t_fmm[st_i][dc] = bd.mark(T)
                        bd.at("vector", lambda e, st_i=st_i, dc=dc, db=db: e.tensor_copy(
                            out=f_sb[:, S * dc + 512 * st_i:S * dc + 512 * (st_i + 1)], in_=fps[db][:]),
                            waits=[(T, t_fmm[st_i][dc])], incs=[(V, 1)])
                        v_fev[st_i][dc] = bd.mark(V)

                if dbg:
                    ch_wB.issue("gpsimd", lambda e: e.dma_start(out=dbg["dbg_crow"], in_=crow_bf[:]),
                               waits=[(SC, sc_crow[3])])
                    ch_wB.issue("gpsimd", lambda e: e.dma_start(out=dbg["dbg_f"], in_=f_sb[:]),
                               waits=[(V, v_fev[7][NCC - 1])])
                    ch_wB.issue("gpsimd", lambda e: e.dma_start(out=dbg["dbg_g"], in_=g_sb[:]),
                               waits=[(V, v_gev[3][NCC - 1])])
                tC_last = bd.mark(T)
                vC_last = bd.mark(V)
                scC_last = bd.mark(SC)
                C_DONE = ([(T, tC_last), (V, vC_last), (SC, scC_last)]
                          + ch_stgC.done() + ch_gate.done())
                bd.emit(block)

            with ExitStack() as sD:
                sbD = lambda n, s, d: sD.enter_context(nc.sbuf_tensor(n, s, d))
                psD = lambda n, s, d: sD.enter_context(nc.psum_tensor(n, s, d))
                xn_sb = sbD("xn_sb", [P, NSC * C], BF16)
                e_buf = [sbD(f"e_buf{i}", [P, 512], BF16) for i in range(2)]
                E_sb = sbD("E_sb", [P, NCC * TCW], BF16)
                xtile = [sbD(f"xtile{i}", [P, C], F32) for i in range(2)]
                gate_t = [sbD(f"gate_t{i}", [P, C], BF16) for i in range(2)]
                osb = sbD("osb", [P, C], F32)
                acc = sbD("acc", [P, C], F32)
                scale_sb = sbD("scale_sb", [P, 2], F32)
                zrow = acc[0:1, 0:512]
                sc_ps = [psD(f"sc_ps{i}", [P, 512], F32) for i in range(2)]
                E_ps = psD("E_ps", [P, NCC * TCW], F32)
                z_ps = psD("z_ps", [P, 512], F32)
                o_ps = psD("o_ps", [P, 512], F32)

                xn_spec = []
                for q in range(4):
                    xn_spec.append(ch_big.issue("scalar", lambda e, q=q: e.dma_start(
                        out=xn_sb[:].rearrange("p (si c) -> p si c", c=C)[:, 8 * q:8 * (q + 1), :],
                        in_=xntok_d.rearrange("(si p) c -> p si c", p=P)[:, 8 * q:8 * (q + 1), :]),
                        waits=(C_DONE + ch_tok.done()) if q == 0 else ()))

                t_pair = [[0] * 16 for _ in range(NTC)]
                sc_exp = [[0] * 16 for _ in range(NTC)]
                t_EZ = [[0] * 16 for _ in range(NTC)]
                t_z = [0] * NTC
                v_Eev = [0] * NTC
                v_Eevb = [[0] * 4 for _ in range(NTC)]
                t_scmm = [0] * NTC
                v_scev = [0] * NTC
                v_zrow = [0] * NTC
                t_o = [[0] * 4 for _ in range(NTC)]
                sc_oev = [[0] * 4 for _ in range(NTC)]
                v_comb = [[0] * 2 for _ in range(NTC)]
                xt_spec = [[None] * 2 for _ in range(NTC)]
                gt_spec = [[None] * 2 for _ in range(NTC)]
                y_spec = [[None] * 2 for _ in range(NTC)]

                def emit_scores(tc, p):
                    w = [(V, vC_last), (SC, scC_last)]
                    if p >= 2:
                        w.append((SC, sc_exp[tc][p - 2]))
                    elif tc >= 1:
                        w.append((SC, sc_exp[tc - 1][14 + p]))
                        if p == 1:
                            w.append((V, v_scev[tc - 1]))
                    for hi in range(2):
                        si = 2 * p + hi
                        for dc in range(NCC):
                            bd.at("tensor", lambda e, tc=tc, p=p, hi=hi, si=si, dc=dc: e.matmul(
                                sc_ps[p % 2][:, 256 * hi:256 * (hi + 1)],
                                f_sb[:, S * dc + P * si:S * dc + P * (si + 1)],
                                g_sb[:, R * dc + TCW * tc:R * dc + TCW * (tc + 1)],
                                start=(hi == 0 and dc == 0), stop=False,
                                skip_group_check=True),
                                waits=w if (hi == 0 and dc == 0) else ())
                    bd.at("tensor", lambda e, p=p, tc=tc: e.matmul(
                        sc_ps[p % 2][:, 0:512], negones_b[:],
                        crow_bf[0:1, TCW * tc:TCW * (tc + 1)].unsqueeze(1).to_broadcast([1, 2, 256]),
                        start=False, stop=True, skip_group_check=True),
                        incs=[(T, 1)])
                    t_pair[tc][p] = bd.mark(T)

                def emit_exp(tc, p):
                    bd.at("scalar", lambda e, tc=tc, p=p: e.activation(
                        e_buf[p % 2][:], sc_ps[p % 2][:], AF.Exp),
                        waits=[(T, t_pair[tc][p])], incs=[(SC, 1)])
                    sc_exp[tc][p] = bd.mark(SC)

                def emit_EZ(tc, p, last=False):
                    w = [(SC, sc_exp[tc][p]), xn_spec[(2 * p) // 8], xn_spec[(2 * p + 1) // 8]]
                    if p == 0 and tc >= 1:
                        w += [(V, v_Eev[tc - 1])]
                    zw = ()
                    if p == 0 and tc >= 1:
                        zw = [(SC, sc_oev[tc - 1][3])]

                    def z_mm():
                        bd.at("tensor", lambda e, p=p: e.matmul(
                            z_ps[0:1, 0:512], onescol_b[:], e_buf[p % 2][:, 0:512],
                            start=(p == 0), stop=(p == 15), skip_group_check=True),
                            waits=zw, incs=[(T, 1)])
                    if last:
                        bd.at("tensor", None, waits=w)
                        z_mm()
                        t_z[tc] = bd.mark(T)
                    for hi in range(2):
                        si = 2 * p + hi
                        for cc in range(NCC):
                            bd.at("tensor", lambda e, si=si, cc=cc, p=p: e.matmul(
                                E_ps[:, TCW * cc:TCW * (cc + 1)],
                                xn_sb[:, C * si + P * cc:C * si + P * (cc + 1)],
                                e_buf[p % 2][:, 256 * hi:256 * (hi + 1)],
                                start=(si == 0 and cc % 2 == 0), stop=(si == NSC - 1),
                                skip_group_check=True),
                                waits=w if (hi == 0 and cc == 0) else (),
                                incs=[(T, 1)] if (last and hi == 1 and cc % 2 == 1) else ())
                            if last and hi == 1 and cc % 2 == 1:
                                b = cc // 2
                                tm = bd.mark(T)
                                bd.at("vector", lambda e, b=b: e.tensor_copy(
                                    out=E_sb[:, 512 * b:512 * (b + 1)],
                                    in_=E_ps[:, 512 * b:512 * (b + 1)]),
                                    waits=[(T, tm)], incs=[(V, 1)])
                                v_Eevb[tc][b] = bd.mark(V)
                    if not last:
                        z_mm()
                    else:
                        v_Eev[tc] = bd.mark(V)
                    t_EZ[tc][p] = bd.mark(T)

                for tc in range(NTC):
                    for tsub in range(2):
                        row = TCW * tc + P * tsub
                        xw = [(V, v_comb[tc - 1][tsub])] if tc >= 1 else list(C_DONE)
                        xt_spec[tc][tsub] = ch_xt.issue("sync", lambda e, row=row, tsub=tsub: e.dma_start(
                            out=xtile[tsub][:], in_=x_d[row:row + P, :]), waits=xw)
                        gw = ch_gate.done()
                        if tc >= 1:
                            gw = gw + [(V, v_comb[tc - 1][tsub])]
                        else:
                            gw = gw + C_DONE
                        gt_spec[tc][tsub] = ch_gt.issue("sync", lambda e, row=row, tsub=tsub: e.dma_start(
                            out=gate_t[tsub][:], in_=gate_d[row:row + P, :]), waits=gw)

                    emit_scores(tc, 0)
                    emit_exp(tc, 0)
                    for p in range(1, 16):
                        emit_scores(tc, p)
                        emit_exp(tc, p)
                        emit_EZ(tc, p - 1)
                    emit_EZ(tc, 15, last=True)

                    bd.at("vector", lambda e: e.tensor_copy(out=zrow[:], in_=z_ps[0:1, 0:512]),
                          waits=[(T, t_z[tc])], incs=[(V, 1)])
                    v_zc = bd.mark(V)
                    bd.at("vector", lambda e: e.tensor_tensor(
                        zrow[0:1, 0:256], zrow[0:1, 0:256], zrow[0:1, 256:512], ALU.add),
                        waits=[(V, v_zc)], incs=[(V, 1)])
                    v_zf = bd.mark(V)
                    bd.at("vector", lambda e: e.reciprocal(zrow[0:1, 0:256], zrow[0:1, 0:256]),
                          waits=[(V, v_zf)], incs=[(V, 1)])
                    v_zr = bd.mark(V)
                    bd.at("vector", lambda e: e.tensor_scalar(
                        zrow[0:1, 0:256], zrow[0:1, 0:256], gam_sb[0:1, 0:1], None, ALU.mult),
                        waits=[(V, v_zr)], incs=[(V, 1)])
                    v_zrow[tc] = bd.mark(V)

                    for tsub in range(2):
                        for dh in range(2):
                            gidx = tsub * 2 + dh
                            ob = o_ps[:, 0:512] if gidx % 2 == 0 else z_ps[:, 0:512]
                            ow = [g_wv]
                            if gidx == 0:
                                ow.append((V, v_Eevb[tc][0]))
                            else:
                                ow.append((V, v_Eev[tc]))
                            if gidx == 1:
                                ow.append((V, v_zc))
                            if gidx >= 2:
                                ow.append((SC, sc_oev[tc][gidx - 2]))
                            elif tc >= 1:
                                ow.append((SC, sc_oev[tc - 1][gidx + 2]))
                            for cc in range(NCC):
                                cw = ow if cc == 0 else (
                                    [(V, v_Eevb[tc][cc // 2])]
                                    if (gidx == 0 and cc % 2 == 0) else ())
                                bd.at("tensor", lambda e, tsub=tsub, dh=dh, cc=cc, ob=ob: e.matmul(
                                    ob,
                                    E_sb[:, TCW * cc + P * tsub:TCW * cc + P * (tsub + 1)],
                                    wv_sb[:, C * cc + 512 * dh:C * cc + 512 * (dh + 1)],
                                    start=(cc == 0), stop=(cc == NCC - 1),
                                    skip_group_check=True),
                                    waits=cw,
                                    incs=[(T, 1)] if cc == NCC - 1 else ())
                            t_o[tc][gidx] = bd.mark(T)
                            if gidx == 0:
                                bd.at("tensor", lambda e: e.matmul(
                                    sc_ps[1][:, 0:1], zrow[0:1, 0:P], one11_f[:],
                                    start=True, stop=True, skip_group_check=True),
                                    waits=[(V, v_zrow[tc]), (SC, sc_exp[tc][15])])
                                bd.at("tensor", lambda e: e.matmul(
                                    sc_ps[1][:, 1:2], zrow[0:1, P:TCW], one11_f[:],
                                    start=True, stop=True, skip_group_check=True),
                                    incs=[(T, 1)])
                                t_scmm[tc] = bd.mark(T)
                                bd.at("vector", lambda e: e.tensor_copy(
                                    out=scale_sb[:], in_=sc_ps[1][:, 0:2]),
                                    waits=[(T, t_scmm[tc])], incs=[(V, 1)])
                                v_scev[tc] = bd.mark(V)
                            oew = [(T, t_o[tc][gidx]), (V, v_scev[tc])]
                            if dh == 0 and tsub == 0 and tc >= 1:
                                oew.append(y_spec[tc - 1][1])
                            if dh == 0 and tsub == 1:
                                oew.append(y_spec[tc][0])
                            bd.at("scalar", lambda e, tsub=tsub, dh=dh, ob=ob: e.activation(
                                osb[:, 512 * dh:512 * (dh + 1)], ob, AF.Identity,
                                scale=scale_sb[:, tsub:tsub + 1]),
                                waits=oew, incs=[(SC, 1)])
                            sc_oev[tc][gidx] = bd.mark(SC)
                        ti = 2 * tc + tsub
                        bd.at("vector", lambda e, ti=ti, tsub=tsub: e.tensor_scalar(
                            acc[:], xtile[tsub][:], ms_store[:, 2 * ti:2 * ti + 1],
                            rstd_store[:, ti:ti + 1],
                            ALU.subtract, ALU.mult),
                            waits=[xt_spec[tc][tsub]], incs=[(V, 1)])
                        v_c1 = bd.mark(V)
                        bd.at("vector", lambda e: e.tensor_tensor(acc[:], acc[:], osb[:], ALU.add),
                              waits=[(SC, sc_oev[tc][tsub * 2 + 1]), (V, v_c1)], incs=[(V, 1)])
                        v_c2 = bd.mark(V)
                        bd.at("vector", lambda e, tsub=tsub: e.tensor_tensor(
                            acc[:], acc[:], gate_t[tsub][:], ALU.mult),
                            waits=[gt_spec[tc][tsub], (V, v_c2)], incs=[(V, 1)])
                        v_c3 = bd.mark(V)
                        yw = [(V, v_c3)]
                        if tc >= 1:
                            yw.append(y_spec[tc - 1][tsub])
                        bd.at("vector", lambda e, tsub=tsub: e.tensor_tensor(
                            osb[:], acc[:], xtile[tsub][:], ALU.add),
                            waits=yw, incs=[(V, 1)])
                        v_comb[tc][tsub] = bd.mark(V)
                        row = TCW * tc + P * tsub
                        y_spec[tc][tsub] = ch_y.issue("sync", lambda e, row=row: e.dma_start(
                            out=y_d[row:row + P, :], in_=osb[:]),
                            waits=[(V, v_comb[tc][tsub])])

                if dbg:
                    ch_wB.issue("gpsimd", lambda e: e.dma_start(out=dbg["dbg_xnsb"], in_=xn_sb[:]),
                               waits=ch_big.done())
                    ch_wB.issue("gpsimd", lambda e: e.dma_start(out=dbg["dbg_E"], in_=E_sb[:]),
                               waits=[(V, v_Eev[NTC - 1])])
                    ch_wB.issue("gpsimd", lambda e: e.dma_start(out=dbg["dbg_scale"], in_=scale_sb[:]),
                               waits=[(V, v_scev[NTC - 1])])
                    bd.at("gpsimd", None, waits=ch_wB.done())
                bd.at("sync", None, waits=ch_y.done())
                bd.emit(block)

    return nc


_NC_CACHE = {}


def kernel(L0_attn_x, L0_attn_Wq, L0_attn_Wk, L0_attn_Wv, L0_attn_gamma,
           L0_ffn_w_router, L0_ffn_b_router, L0_ffn_w1, L0_ffn_b1,
           L0_ffn_w2, L0_ffn_b2):
    x = np.ascontiguousarray(np.asarray(L0_attn_x, np.float32))
    wqT = np.ascontiguousarray(np.asarray(L0_attn_Wq, np.float32).T)
    wkT = np.ascontiguousarray(np.asarray(L0_attn_Wk, np.float32).T)
    wvT = np.ascontiguousarray(np.asarray(L0_attn_Wv, np.float32).T)
    w1T = np.ascontiguousarray(np.asarray(L0_ffn_w1, np.float32).T)
    w2T = np.ascontiguousarray(np.asarray(L0_ffn_w2, np.float32).T)
    wr = np.ascontiguousarray(np.asarray(L0_ffn_w_router, np.float32))
    br = np.asarray(L0_ffn_b_router, np.float32).reshape(1, NE)
    b1 = np.asarray(L0_ffn_b1, np.float32).reshape(1, H)
    b2 = np.asarray(L0_ffn_b2, np.float32).reshape(1, C)
    gam = np.asarray(L0_attn_gamma, np.float32).reshape(1, 1)
    pat = np.zeros((NE, H), ml_dtypes.bfloat16)
    for e_i in range(NE):
        pat[e_i, 256 * e_i:256 * (e_i + 1)] = -1e30
    ident = np.eye(P).astype(ml_dtypes.bfloat16)

    if "nc" not in _NC_CACHE:
        _NC_CACHE["nc"] = build_nc()
    nc = _NC_CACHE["nc"]

    in_maps = []
    for core in range(8):
        b, half = core // 2, core % 2
        xb = x[b]
        if half == 0:
            x_perm = xb
        else:
            x_perm = np.ascontiguousarray(
                np.concatenate([xb[R:], xb[:R]], axis=0))
        in_maps.append(dict(
            x_perm=x_perm, wqT=wqT, wkT=wkT, wvT=wvT, w1T=w1T, w2T=w2T,
            wr=wr, br=br, b1=b1, b2=b2, gam=gam, pat=pat, ident=ident))

    res = run_bass_kernel_spmd(nc, in_maps, core_ids=list(range(8)))

    out = np.empty((4, S, C), np.float32)
    for core in range(8):
        b, half = core // 2, core % 2
        out[b, R * half:R * (half + 1)] = res.results[core]["y"]
    return out


# version 26 (final)
# speedup vs baseline: 1.0619x; 1.0405x over previous
import numpy as np
import ml_dtypes
from contextlib import ExitStack

import concourse.bass as bass
import concourse.mybir as mybir
from concourse.bass_utils import run_bass_kernel_spmd

F32 = mybir.dt.float32
BF16 = mybir.dt.bfloat16
AF = mybir.ActivationFunctionType
ALU = mybir.AluOpType
AX = mybir.AxisListType

P = 128
S = 4096
C = 1024
H = 4096
NE = 16
R = 2048
NCC = C // P
NKC = H // P
NBB = 4
TCW = 256
NTC = R // TCW
NSC = S // P
A_BOUND = 5.0
EPS = 1e-5

ENGINES = ("sync", "scalar", "vector", "tensor", "gpsimd")


class Sched:

    def __init__(self, nc, ctx):
        self.nc = nc
        self.ctx = ctx
        self.q = {e: [] for e in ENGINES}
        self.sem = {}
        self.cnt = {}
        for name in ("T", "V", "SC"):
            self.add_sem(name)

    def add_sem(self, name):
        self.sem[name] = self.ctx.enter_context(self.nc.semaphore(name))
        self.cnt[name] = 0

    def at(self, eng, builder, waits=(), incs=()):
        incs2 = []
        for s, amt in incs:
            self.cnt[s] += amt
            incs2.append((s, amt))
        self.q[eng].append((builder, tuple(waits), tuple(incs2)))

    def mark(self, s):
        return self.cnt[s]

    def emit(self, block):
        for eng in ENGINES:
            ops = self.q[eng]
            if not ops:
                continue
            waited = {}

            def body(e, ops=ops, waited=waited):
                for builder, waits, incs in ops:
                    for s, v in waits:
                        if v > 0 and waited.get(s, 0) < v:
                            e.wait_ge(self.sem[s], v)
                            waited[s] = v
                    ins = builder(e) if builder is not None else None
                    for s, amt in incs:
                        ins.then_inc(self.sem[s], amt)

            getattr(block, eng)(body)
        self.q = {e: [] for e in ENGINES}


class Chan:

    def __init__(self, bd, name, n=2, serialize=False):
        self.bd = bd
        self.n = n
        self.serialize = serialize
        self.names = []
        for i in range(n):
            nm = f"{name}{i}"
            bd.add_sem(nm)
            self.names.append(nm)
        self.i = 0
        self.last = None

    def issue(self, eng, builder, waits=()):
        return self.issue_group(eng, [builder], waits)

    def issue_group(self, eng, builders, waits=()):
        nm = self.names[self.i % self.n]
        w = list(waits)
        if self.serialize and self.last is not None:
            w.append(self.last)
        for bi, b in enumerate(builders):
            self.bd.at(eng, b, waits=w if bi == 0 else (), incs=[(nm, 16)])
        self.i += 1
        spec = (nm, self.bd.cnt[nm])
        self.last = spec
        return spec

    def done(self):
        return [(nm, self.bd.cnt[nm]) for nm in self.names]


def build_nc(debug=False):
    nc = bass.Bass("TRN2", target_bir_lowering=False)

    x_d = nc.dram_tensor("x_perm", [S, C], F32, kind="ExternalInput").ap()
    wqT_d = nc.dram_tensor("wqT", [C, C], F32, kind="ExternalInput").ap()
    wkT_d = nc.dram_tensor("wkT", [C, C], F32, kind="ExternalInput").ap()
    wvT_d = nc.dram_tensor("wvT", [C, C], F32, kind="ExternalInput").ap()
    w1T_d = nc.dram_tensor("w1T", [C, H], F32, kind="ExternalInput").ap()
    w2T_d = nc.dram_tensor("w2T", [H, C], F32, kind="ExternalInput").ap()
    wr_d = nc.dram_tensor("wr", [C, NE], F32, kind="ExternalInput").ap()
    br_d = nc.dram_tensor("br", [1, NE], F32, kind="ExternalInput").ap()
    b1_d = nc.dram_tensor("b1", [1, H], F32, kind="ExternalInput").ap()
    b2_d = nc.dram_tensor("b2", [1, C], F32, kind="ExternalInput").ap()
    gam_d = nc.dram_tensor("gam", [1, 1], F32, kind="ExternalInput").ap()
    pat_d = nc.dram_tensor("pat", [NE, H], BF16, kind="ExternalInput").ap()
    id_d = nc.dram_tensor("ident", [P, P], BF16, kind="ExternalInput").ap()

    y_d = nc.dram_tensor("y", [R, C], F32, kind="ExternalOutput").ap()

    scr_kind = "ExternalOutput" if debug else "Internal"
    xntok_d = nc.dram_tensor("xntok_scr", [S, C], BF16, kind=scr_kind).ap()
    gate_d = nc.dram_tensor("gate_scr", [R, C], BF16, kind=scr_kind).ap()

    dbg = {}
    if debug:
        for name, shape, dt in [
            ("dbg_crow", [1, R], BF16), ("dbg_f", [P, NCC * S], BF16),
            ("dbg_g", [P, NCC * R], BF16), ("dbg_E", [P, NCC * TCW], BF16),
            ("dbg_scale", [P, 2], F32),
            ("dbg_m", [P, 40], F32), ("dbg_rstd", [P, 20], F32),
            ("dbg_lgs", [P, 64], F32), ("dbg_xnsb", [P, NSC * C], BF16),
            ("dbg_bb", [P, 1], F32),
        ]:
            dbg[name] = nc.dram_tensor(name, shape, dt, kind="ExternalOutput").ap()

    with ExitStack() as top:
        sb = lambda n, s, d: top.enter_context(nc.sbuf_tensor(n, s, d))
        bd = Sched(nc, top)
        block = top.enter_context(nc.Block())

        T, V, SC = "T", "V", "SC"
        ch_wB = Chan(bd, "cW", 4)
        ch_aux = Chan(bd, "cAX", 2)
        ch_x = Chan(bd, "cX", 4)
        ch_tok = Chan(bd, "cT", 4)
        ch_big = Chan(bd, "cB", 2)
        ch_stgB = Chan(bd, "cSB", 2)
        ch_gate = Chan(bd, "cG", 2)
        ch_stgC = Chan(bd, "cSC", 2)
        ch_xt = Chan(bd, "cXT", 2)
        ch_gt = Chan(bd, "cGT", 2)
        ch_y = Chan(bd, "cY", 2)

        onescol_b = sb("onescol_b", [P, 1], BF16)
        onescol_f = sb("onescol_f", [P, 1], F32)
        ones1_b = sb("ones1_b", [1, P], BF16)
        one11_f = sb("one11_f", [1, 1], F32)
        gam_sb = sb("gam_sb", [1, 1], F32)
        epsb = sb("epsb", [P, 1], F32)
        ms_store = sb("ms_store", [P, 40], F32)
        rstd_store = sb("rstd_store", [P, 20], F32)

        with ExitStack() as sAB:
            sbAB = lambda n, s, d: sAB.enter_context(nc.sbuf_tensor(n, s, d))
            psAB = lambda n, s, d: sAB.enter_context(nc.psum_tensor(n, s, d))
            w1_sb = sbAB("w1_sb", [P, NCC * H], BF16)
            w2_sb = sbAB("w2_sb", [P, NKC * C], BF16)
            ident = sbAB("identc", [P, P], BF16)
            w1aug = sbAB("w1aug", [32, H], BF16)
            wr_sb = sbAB("wr_sb", [P, NCC * NE], BF16)
            br_bf = sbAB("br_bf", [1, NE], BF16)
            stg = [sbAB(f"stgB{i}", [P, NCC * 512], BF16) for i in range(2)]
            h1m = sbAB("h1m", [P, NKC * 512], BF16)
            xaug = [sbAB(f"xaug{i}", [32, 512], BF16) for i in range(2)]
            lgs = sbAB("lgs", [P, 4 * NE], F32)
            t8 = sbAB("t8", [P, 8], F32)
            pack = sbAB("pack", [P, 32], F32)
            pk_b = sbAB("pk_b", [P, 32], BF16)
            h1_ps = [psAB(f"h1_ps{i}", [P, 512], F32) for i in range(3)]
            y_ps = [psAB(f"y_ps{i}", [P, 512], F32) for i in range(2)]
            lg_ps = psAB("lg_ps", [P, 4 * NE], F32)
            aug_ps = psAB("aug_ps", [32, P], BF16)

            x_spec = [None] * 32
            tok_spec = [None] * 32
            sc_xn = [0] * 32
            v_bn2 = [0] * 32
            sc_sd = [0] * 32
            g_w1 = [None] * 4
            stg_spec = [None] * NBB
            t_h1 = [[0] * NKC for _ in range(NBB)]
            sc_relu = [[0] * NKC for _ in range(NBB)]
            t_y = [[0] * 8 for _ in range(NBB)]
            sc_silu = [[0] * 8 for _ in range(NBB)]
            gate_spec = [[None] * 4 for _ in range(NBB)]
            t_lg = [0] * NBB
            v_lgs = [0] * NBB
            t_aug = [[0] * 4 for _ in range(NBB)]
            v_xaug = [[0] * 4 for _ in range(NBB)]

            with ExitStack() as sA:
                sbA = lambda n, s, d: sA.enter_context(nc.sbuf_tensor(n, s, d))
                xt = [sbA(f"xt{i}", [P, C], F32) for i in range(4)]
                xnb = [sbA(f"xnb{i}", [P, C], BF16) for i in range(2)]
                bnbuf = sbA("bnbuf", [P, 12], F32)
                sd_t = sbA("sd_t", [P, 4], F32)
                negmr = sbA("negmr", [P, 20], F32)

                bd.at("scalar", lambda e: e.memzero(w1aug[:]), incs=[(SC, 1)])
                sc_w1aug0 = bd.mark(SC)
                bd.at("vector", lambda e: e.memset(onescol_b[:], 1.0))
                bd.at("vector", lambda e: e.memset(onescol_f[:], 1.0))
                bd.at("vector", lambda e: e.memset(ones1_b[:], 1.0))
                bd.at("vector", lambda e: e.memset(one11_f[:], 1.0))
                bd.at("vector", lambda e: e.memset(epsb[:], EPS))
                bd.at("vector", lambda e: e.memset(pack[:, 0:1], 1.0))
                bd.at("vector", lambda e: e.memset(pack[:, NE + 1:32], 0.0), incs=[(V, 1)])

                sp_id = ch_aux.issue("scalar", lambda e: e.dma_start(out=ident[:], in_=id_d))
                sp_pat = ch_aux.issue("scalar", lambda e: e.dma_start(
                    out=w1aug[1:NE + 1, :], in_=pat_d), waits=[(SC, sc_w1aug0)])
                def w1_issue(q):
                    g_w1[q] = ch_wB.issue("gpsimd", lambda e, q=q: e.dma_start(
                        out=w1_sb[:].rearrange("p (cc h) -> p cc h", h=H)[:, :, 1024 * q:1024 * (q + 1)],
                        in_=w1T_d.rearrange("(cc p) h -> p cc h", p=P)[:, :, 1024 * q:1024 * (q + 1)]))
                w1_issue(0)
                sp_wr = ch_wB.issue("gpsimd", lambda e: e.dma_start(
                    out=wr_sb[:].rearrange("p (cc e) -> p cc e", e=NE),
                    in_=wr_d.rearrange("(cc p) e -> p cc e", p=P)))
                sp_br = ch_wB.issue("gpsimd", lambda e: e.dma_start(out=br_bf[:], in_=br_d))
                g_consts = ch_wB.issue("gpsimd", lambda e: e.dma_start(out=gam_sb[:], in_=gam_d))
                sp_b1 = ch_wB.issue("gpsimd", lambda e: e.dma_start(
                    out=w1aug[0:1, :], in_=b1_d), waits=[(SC, sc_w1aug0)])
                w1_issue(1)
                w1_issue(2)
                w1_issue(3)
                g_w2q = [None] * 4
                for q2 in range(4):
                    g_w2q[q2] = ch_wB.issue("gpsimd", lambda e, q2=q2: e.dma_start(
                        out=w2_sb[:].rearrange("p (kc c) -> p kc c", c=C)[:, 8 * q2:8 * (q2 + 1), :],
                        in_=w2T_d.rearrange("(kc p) c -> p kc c", p=P)[:, 8 * q2:8 * (q2 + 1), :]))

                def emit_stats(j):
                    jb = j % 4
                    xw = [(SC, sc_xn[j - 4])] if j >= 4 else []
                    x_spec[j] = ch_x.issue("sync", lambda e, j=j, jb=jb: e.dma_start(
                        out=xt[jb][:], in_=x_d[P * j:P * (j + 1), :]), waits=xw)
                    jc = j if j < 16 else 16 + (j % 4)
                    bd.at("vector", lambda e, jb=jb: e.bn_stats(bnbuf[:, 0:6], xt[jb][:, 0:512]),
                          waits=[x_spec[j]])
                    bd.at("vector", lambda e, jb=jb: e.bn_stats(bnbuf[:, 6:12], xt[jb][:, 512:1024]),
                          incs=[(V, 1)])
                    v_bn1 = bd.mark(V)
                    aw = [(V, v_bn1)]
                    if j >= 20:
                        aw.append((SC, sc_sd[j - 4]))
                    bd.at("vector", lambda e, jc=jc: e.bn_aggr(
                        ms_store[:, 2 * jc:2 * jc + 2], bnbuf[:]),
                        waits=aw, incs=[(V, 1)])
                    v_bn2[j] = bd.mark(V)
                    bd.at("scalar", lambda e, jc=jc, j=j: e.activation(
                        sd_t[:, j % 4:j % 4 + 1], ms_store[:, 2 * jc + 1:2 * jc + 2],
                        AF.Sqrt, bias=epsb[:]),
                        waits=[(V, v_bn2[j])], incs=[(SC, 1)])
                    sc_sd[j] = bd.mark(SC)

                def emit_norm(j):
                    jb = j % 4
                    jc = j if j < 16 else 16 + (j % 4)
                    rw = [(SC, sc_sd[j])]
                    if j >= 20:
                        rw.append((SC, sc_xn[j - 4]))
                    bd.at("vector", lambda e, jc=jc, j=j: e.reciprocal(
                        rstd_store[:, jc:jc + 1], sd_t[:, j % 4:j % 4 + 1]),
                        waits=rw, incs=[(V, 1)])
                    v_recip = bd.mark(V)
                    bd.at("vector", lambda e, jc=jc: e.tensor_scalar(
                        negmr[:, jc:jc + 1], ms_store[:, 2 * jc:2 * jc + 1],
                        rstd_store[:, jc:jc + 1], -1.0,
                        ALU.mult, ALU.mult), waits=[(V, v_recip)], incs=[(V, 1)])
                    v_negmr = bd.mark(V)
                    xnw = [(V, v_negmr)]
                    if j >= 2:
                        xnw += [tok_spec[j - 2]]
                    bd.at("scalar", lambda e, jc=jc, jb=jb, j=j: e.activation(
                        xnb[j % 2][:], xt[jb][:], AF.Identity,
                        bias=negmr[:, jc:jc + 1], scale=rstd_store[:, jc:jc + 1]),
                        waits=xnw, incs=[(SC, 1)])
                    sc_xn[j] = bd.mark(SC)
                    tok_spec[j] = ch_tok.issue("scalar", lambda e, j=j: e.dma_start(
                        out=xntok_d[P * j:P * (j + 1), :], in_=xnb[j % 2][:]),
                        waits=[(SC, sc_xn[j])])

                def emit_stageB(nb):
                    nbb = nb % 2
                    stw = [tok_spec[4 * nb + i] for i in range(4)]
                    if nb >= 2:
                        stw.append((T, t_h1[nb - 2][NKC - 1]))
                    builders = []
                    for cc in range(NCC):
                        builders.append(lambda e, nb=nb, nbb=nbb, cc=cc: e.dma_start(
                            out=stg[nbb][:, 512 * cc:512 * (cc + 1)],
                            in_=xntok_d[512 * nb:512 * (nb + 1), P * cc:P * (cc + 1)],
                            transpose=True))
                    stg_spec[nb] = ch_stgB.issue_group("sync", builders, waits=stw)

                def emit_router(nb):
                    nbb = nb % 2
                    lgw = [stg_spec[nb], g_consts, sp_wr, sp_br]
                    if nb >= 1:
                        lgw.append((V, v_lgs[nb - 1]))
                    for ns in range(4):
                        for cc in range(NCC):
                            bd.at("tensor", lambda e, nbb=nbb, ns=ns, cc=cc: e.matmul(
                                lg_ps[:, NE * ns:NE * (ns + 1)],
                                stg[nbb][:, 512 * cc + P * ns:512 * cc + P * (ns + 1)],
                                wr_sb[:, NE * cc:NE * (cc + 1)],
                                start=(cc == 0), stop=False, skip_group_check=True),
                                waits=lgw if (ns == 0 and cc == 0) else ())
                        bd.at("tensor", lambda e, ns=ns: e.matmul(
                            lg_ps[:, NE * ns:NE * (ns + 1)], ones1_b[:], br_bf[:],
                            start=False, stop=(ns == 3), skip_group_check=True),
                            incs=[(T, 1)] if ns == 3 else ())
                    t_lg[nb] = bd.mark(T)
                    bd.at("vector", lambda e: e.tensor_copy(out=lgs[:], in_=lg_ps[:]),
                          waits=[(T, t_lg[nb])], incs=[(V, 1)])
                    v_lgs[nb] = bd.mark(V)
                    for ns in range(4):
                        pw = []
                        if not (nb == 0 and ns == 0):
                            pnb, pns = (nb, ns - 1) if ns > 0 else (nb - 1, 3)
                            pw = [(T, t_aug[pnb][pns])]
                        bd.at("vector", lambda e, ns=ns: e.max(out=t8[:], in_=lgs[:, NE * ns:NE * (ns + 1)]),
                              waits=[(V, v_lgs[nb])], incs=[(V, 1)])
                        v_max = bd.mark(V)
                        bd.at("vector", lambda e, ns=ns: e.tensor_scalar(
                            pack[:, 1:NE + 1], lgs[:, NE * ns:NE * (ns + 1)], t8[:, 4:5], None, ALU.is_le),
                            waits=[(V, v_max)], incs=[(V, 1)])
                        v_isle = bd.mark(V)
                        bd.at("vector", lambda e: e.tensor_copy(out=pk_b[:], in_=pack[:]),
                              waits=pw + [(V, v_isle)], incs=[(V, 1)])
                        v_pk = bd.mark(V)
                        idw = [sp_id] if (nb == 0 and ns == 0) else []
                        bd.at("tensor", lambda e: e.transpose(aug_ps[:], pk_b[:], ident[:]),
                              waits=idw + [(V, v_pk)], incs=[(T, 1)])
                        t_aug[nb][ns] = bd.mark(T)
                        aw = [(T, t_aug[nb][ns])]
                        if nb >= 2 and ns == 0:
                            aw.append((T, t_h1[nb - 2][NKC - 1]))
                        bd.at("vector", lambda e, ns=ns, nbb=nbb: e.tensor_copy(
                            out=xaug[nbb][:, P * ns:P * (ns + 1)], in_=aug_ps[:]),
                            waits=aw, incs=[(V, 1)])
                        v_xaug[nb][ns] = bd.mark(V)

                def emit_h1(nb, kc):
                    nbb = nb % 2
                    kb = kc % 3
                    hw = [g_w1[kc // 8], (V, v_xaug[nb][3])]
                    if nb == 0 and kc == 0:
                        hw += [sp_b1, sp_pat]
                    if kc >= 3:
                        hw.append((SC, sc_relu[nb][kc - 3]))
                    elif nb >= 1:
                        hw.append((SC, sc_relu[nb - 1][{0: 30, 1: 31, 2: 29}[kc % 3]]))
                    for cc in range(NCC):
                        bd.at("tensor", lambda e, nbb=nbb, kc=kc, cc=cc, kb=kb: e.matmul(
                            h1_ps[kb][:],
                            w1_sb[:, H * cc + P * kc:H * cc + P * (kc + 1)],
                            stg[nbb][:, 512 * cc:512 * (cc + 1)],
                            start=(cc == 0), stop=False),
                            waits=hw if cc == 0 else ())
                    bd.at("tensor", lambda e, kc=kc, kb=kb, nbb=nbb: e.matmul(
                        h1_ps[kb][:], w1aug[:, P * kc:P * (kc + 1)], xaug[nbb][:],
                        start=False, stop=True), incs=[(T, 1)])
                    t_h1[nb][kc] = bd.mark(T)
                    rw = [(T, t_h1[nb][kc])]
                    if nb >= 1:
                        rw.append((T, t_y[nb - 1][7]))
                    bd.at("scalar", lambda e, kc=kc, kb=kb: e.activation(
                        h1m[:, 512 * kc:512 * (kc + 1)], h1_ps[kb][:], AF.Relu),
                        waits=rw, incs=[(SC, 1)])
                    sc_relu[nb][kc] = bd.mark(SC)

                W1T = (2, 8, 14, 20)
                sched = {5: [lambda: emit_stageB(0)],
                         6: [lambda: emit_router(0)],
                         9: [lambda: emit_stageB(1)],
                         22: [lambda: emit_stageB(2)],
                         17: [lambda: emit_router(1)]}
                for kc in range(NKC):
                    t = max(7 + kc // 4, W1T[kc // 8] + 1)
                    sched.setdefault(t, []).append(
                        lambda nb=0, kc=kc: emit_h1(nb, kc))

                emit_stats(0)
                emit_stats(1)
                for j in range(2, 32):
                    emit_norm(j - 2)
                    emit_stats(j)
                    for fthunk in sched.get(j, []):
                        fthunk()
                emit_norm(30)
                emit_norm(31)
                vA_last = bd.mark(V)
                scA_last = bd.mark(SC)
                A_DONE = ([(V, vA_last), (SC, scA_last)]
                          + ch_x.done() + ch_tok.done())
                bd.emit(block)

            with ExitStack() as sB2:
                sbB2 = lambda n, s, d: sB2.enter_context(nc.sbuf_tensor(n, s, d))
                gate_sb = [sbB2(f"gate{i}", [P, C], BF16) for i in range(2)]
                b2_bf = sbB2("b2_bf", [1, C], BF16)
                sp_b2 = ch_wB.issue("gpsimd", lambda e: e.dma_start(
                    out=b2_bf[:], in_=b2_d), waits=A_DONE)

                def emit_y(nb):
                    for pi, (nsp, dh) in enumerate(((0, 0), (0, 1), (1, 0), (1, 1))):
                        for kq in range(4):
                            for kk in range(8):
                                kc = 8 * kq + kk
                                for nsl in range(2):
                                    ns = 2 * nsp + nsl
                                    yw = []
                                    if kk == 0 and nsl == 0:
                                        yw += [g_w2q[kq], (SC, sc_relu[nb][8 * kq + 7])]
                                    if kc == 0:
                                        if pi >= 1:
                                            pns = 2 * ((pi - 1) // 2) + nsl
                                            pdh = (pi - 1) % 2
                                            yw.append((SC, sc_silu[nb][pns * 2 + pdh]))
                                        elif nb >= 1:
                                            yw.append((SC, sc_silu[nb - 1][(2 + nsl) * 2 + 1]))
                                    bd.at("tensor", lambda e, ns=ns, dh=dh, kc=kc, nsl=nsl: e.matmul(
                                        y_ps[nsl][:],
                                        h1m[:, 512 * kc + P * ns:512 * kc + P * (ns + 1)],
                                        w2_sb[:, C * kc + 512 * dh:C * kc + 512 * (dh + 1)],
                                        start=(kc == 0), stop=False),
                                        waits=yw)
                        for nsl in range(2):
                            ns = 2 * nsp + nsl
                            gidx = ns * 2 + dh
                            b2w = [sp_b2] if (nb == 0 and pi == 0) else []
                            bd.at("tensor", lambda e, dh=dh, nsl=nsl: e.matmul(
                                y_ps[nsl][:], ones1_b[:], b2_bf[0:1, 512 * dh:512 * (dh + 1)],
                                start=False, stop=True), waits=b2w, incs=[(T, 1)])
                            t_y[nb][gidx] = bd.mark(T)
                            sw = [(T, t_y[nb][gidx])]
                            if nb == 0 and gidx in (0, 2):
                                sw += A_DONE
                            if ns >= 2:
                                sw.append(gate_spec[nb][ns - 2])
                            elif nb >= 1:
                                sw.append(gate_spec[nb - 1][2 + ns])
                            bd.at("scalar", lambda e, ns=ns, dh=dh, nsl=nsl: e.activation(
                                gate_sb[ns % 2][:, 512 * dh:512 * (dh + 1)], y_ps[nsl][:], AF.Silu),
                                waits=sw, incs=[(SC, 1)])
                            sc_silu[nb][gidx] = bd.mark(SC)
                        if dh == 1:
                            for nsl in range(2):
                                ns = 2 * nsp + nsl
                                row = 512 * nb + P * ns
                                gate_spec[nb][ns] = ch_gate.issue("scalar", lambda e, ns=ns, row=row: e.dma_start(
                                    out=gate_d[row:row + P, :], in_=gate_sb[ns % 2][:]),
                                    waits=[(SC, sc_silu[nb][ns * 2 + 1])])

                emit_y(0)
                emit_router(2)
                for kc in range(NKC):
                    emit_h1(1, kc)
                emit_stageB(3)
                emit_y(1)
                emit_router(3)
                for kc in range(NKC):
                    emit_h1(2, kc)
                emit_y(2)
                for kc in range(NKC):
                    emit_h1(3, kc)
                emit_y(3)

                if dbg:
                    ch_wB.issue("gpsimd", lambda e: e.dma_start(out=dbg["dbg_lgs"], in_=lgs[:]),
                               waits=[(V, v_lgs[NBB - 1])])
                    ch_wB.issue("gpsimd", lambda e: e.dma_start(out=dbg["dbg_m"], in_=ms_store[:]))
                    ch_wB.issue("gpsimd", lambda e: e.dma_start(out=dbg["dbg_rstd"], in_=rstd_store[:]))
                tB_last = bd.mark(T)
                vB_last = bd.mark(V)
                scB_last = bd.mark(SC)
                B_DONE = ([(T, tB_last), (V, vB_last), (SC, scB_last)]
                          + ch_stgB.done() + ch_gate.done())
                bd.emit(block)


        with ExitStack() as sCD:
            sbCD = lambda n, s, d: sCD.enter_context(nc.sbuf_tensor(n, s, d))
            crow_bf = sbCD("crow_bf", [1, R], BF16)
            bmax_f = sbCD("bmax_f", [1, 8], F32)
            bneg_bf = sbCD("bneg_bf", [1, 1], BF16)
            bcol_sb = sbCD("bcol_sb", [P, 1], F32)
            f_sb = sbCD("f_sb", [P, NCC * S], BF16)
            g_sb = sbCD("g_sb", [P, NCC * R], BF16)
            wv_sb = sbCD("wv_sb", [P, NCC * C], BF16)

            t_gmm = [[0] * NCC for _ in range(4)]
            v_gev = [[0] * NCC for _ in range(4)]
            sc_gsq = [[0] * NCC for _ in range(4)]
            t_nsq = [0] * 4
            sc_crow = [0] * 4
            t_fmm = [[0] * NCC for _ in range(8)]
            v_fev = [[0] * NCC for _ in range(8)]

            def stage_dmaC(stgc, st_i, readers_mark, first_waits):
                w = list(first_waits)
                if readers_mark:
                    w.append((T, readers_mark))
                builders = []
                for cc in range(NCC):
                    builders.append(lambda e, st_i=st_i, cc=cc, stgc=stgc: e.dma_start(
                        out=stgc[st_i % 2][:, 512 * cc:512 * (cc + 1)],
                        in_=xntok_d[512 * st_i:512 * (st_i + 1), P * cc:P * (cc + 1)],
                        transpose=True))
                return ch_stgC.issue_group("sync", builders, waits=w)

            with ExitStack() as sC1:
                sbC1 = lambda n, s, d: sC1.enter_context(nc.sbuf_tensor(n, s, d))
                psC1 = lambda n, s, d: sC1.enter_context(nc.psum_tensor(n, s, d))
                wk_sb = sbC1("wk_sb", [P, NCC * C], BF16)
                wq_sb = sbC1("wq_sb", [P, NCC * C], BF16)
                stgc = [sbC1(f"stgC{i}", [P, NCC * 512], BF16) for i in range(2)]
                gsq = sbC1("gsq", [P, 1024], F32)
                cps = [psC1(f"cps{i}", [P, 512], F32) for i in range(3)]
                nsq_ps = [psC1(f"nsq_ps{i}", [1, 512], F32) for i in range(2)]

                g_wk = ch_wB.issue("gpsimd", lambda e: e.dma_start(
                    out=wk_sb[:].rearrange("p (cc c) -> p cc c", c=C),
                    in_=wkT_d.rearrange("(cc p) c -> p cc c", p=P)), waits=B_DONE)
                g_wq = ch_wB.issue("gpsimd", lambda e: e.dma_start(
                    out=wq_sb[:].rearrange("p (cc c) -> p cc c", c=C),
                    in_=wqT_d.rearrange("(cc p) c -> p cc c", p=P)), waits=B_DONE)
                g_wv = ch_wB.issue("gpsimd", lambda e: e.dma_start(
                    out=wv_sb[:].rearrange("p (cc c) -> p cc c", c=C),
                    in_=wvT_d.rearrange("(cc p) c -> p cc c", p=P)), waits=B_DONE)

                gsq2 = [gsq[:, 0:512], gsq[:, 512:1024]]

                def emit_nsq(st_i, dc):
                    nw = [(SC, sc_gsq[st_i][dc])]
                    if st_i >= 2 and dc == 0:
                        nw.append((SC, sc_crow[st_i - 2]))
                    bd.at("tensor", lambda e, st_i=st_i, dc=dc: e.matmul(
                        nsq_ps[st_i % 2][:], onescol_f[:], gsq2[(st_i * NCC + dc) % 2],
                        start=(dc == 0), stop=(dc == NCC - 1)),
                        waits=nw, incs=[(T, 1)] if dc == NCC - 1 else ())
                    if dc == NCC - 1:
                        t_nsq[st_i] = bd.mark(T)
                        bd.at("scalar", lambda e, st_i=st_i: e.activation(
                            crow_bf[0:1, 512 * st_i:512 * (st_i + 1)], nsq_ps[st_i % 2][:],
                            AF.Sqrt, scale=A_BOUND * A_BOUND),
                            waits=[(T, t_nsq[st_i])], incs=[(SC, 1)])
                        sc_crow[st_i] = bd.mark(SC)

                for st_i in range(4):
                    rm = t_gmm[st_i - 2][NCC - 1] if st_i >= 2 else 0
                    st_spec = stage_dmaC(stgc, st_i, rm, B_DONE + ch_tok.done())
                    for dc in range(NCC):
                        db = dc % 3
                        k = st_i * NCC + dc
                        gw = [g_wk, st_spec]
                        if k == 0:
                            gw += B_DONE
                        if k >= 3:
                            pk = k - 3
                            gw += [(V, v_gev[pk // NCC][pk % NCC]), (SC, sc_gsq[pk // NCC][pk % NCC])]
                        for cc in range(NCC):
                            bd.at("tensor", lambda e, st_i=st_i, dc=dc, cc=cc, db=db: e.matmul(
                                cps[db][:],
                                wk_sb[:, C * cc + P * dc:C * cc + P * (dc + 1)],
                                stgc[st_i % 2][:, 512 * cc:512 * (cc + 1)],
                                start=(cc == 0), stop=(cc == NCC - 1)),
                                waits=gw if cc == 0 else (),
                                incs=[(T, 1)] if cc == NCC - 1 else ())
                        t_gmm[st_i][dc] = bd.mark(T)
                        if k >= 1:
                            emit_nsq((k - 1) // NCC, (k - 1) % NCC)
                        bd.at("vector", lambda e, st_i=st_i, dc=dc, db=db: e.tensor_copy(
                            out=g_sb[:, R * dc + 512 * st_i:R * dc + 512 * (st_i + 1)], in_=cps[db][:]),
                            waits=[(T, t_gmm[st_i][dc])], incs=[(V, 1)])
                        v_gev[st_i][dc] = bd.mark(V)
                        bd.at("scalar", lambda e, db=db, k=k: e.activation(
                            gsq2[k % 2], cps[db][:], AF.Square),
                            waits=[(V, v_gev[st_i][dc])], incs=[(SC, 1)])
                        sc_gsq[st_i][dc] = bd.mark(SC)
                emit_nsq(3, NCC - 1)

                stgf = stgc
                fps = cps
                for st_i in range(8):
                    if st_i >= 2:
                        rm = t_fmm[st_i - 2][NCC - 1]
                    else:
                        rm = t_gmm[2 + st_i][NCC - 1]
                    st_spec = stage_dmaC(stgf, st_i, rm, [])
                    for dc in range(NCC):
                        db = dc % 3
                        k = st_i * NCC + dc
                        fw = [g_wq, st_spec]
                        if k >= 3:
                            pk = k - 3
                            fw.append((V, v_fev[pk // NCC][pk % NCC]))
                        else:
                            pk = {0: 30, 1: 31, 2: 29}[k % 3]
                            fw += [(V, v_gev[pk // NCC][pk % NCC]),
                                   (SC, sc_gsq[pk // NCC][pk % NCC])]
                        for cc in range(NCC):
                            bd.at("tensor", lambda e, st_i=st_i, dc=dc, cc=cc, db=db: e.matmul(
                                fps[db][:],
                                wq_sb[:, C * cc + P * dc:C * cc + P * (dc + 1)],
                                stgf[st_i % 2][:, 512 * cc:512 * (cc + 1)],
                                start=(cc == 0), stop=(cc == NCC - 1)),
                                waits=fw if cc == 0 else (),
                                incs=[(T, 1)] if cc == NCC - 1 else ())
                        t_fmm[st_i][dc] = bd.mark(T)
                        bd.at("vector", lambda e, st_i=st_i, dc=dc, db=db: e.tensor_copy(
                            out=f_sb[:, S * dc + 512 * st_i:S * dc + 512 * (st_i + 1)], in_=fps[db][:]),
                            waits=[(T, t_fmm[st_i][dc])], incs=[(V, 1)])
                        v_fev[st_i][dc] = bd.mark(V)

                if dbg:
                    ch_wB.issue("gpsimd", lambda e: e.dma_start(out=dbg["dbg_crow"], in_=crow_bf[:]),
                               waits=[(SC, sc_crow[3])])
                    ch_wB.issue("gpsimd", lambda e: e.dma_start(out=dbg["dbg_f"], in_=f_sb[:]),
                               waits=[(V, v_fev[7][NCC - 1])])
                    ch_wB.issue("gpsimd", lambda e: e.dma_start(out=dbg["dbg_g"], in_=g_sb[:]),
                               waits=[(V, v_gev[3][NCC - 1])])
                tC_last = bd.mark(T)
                vC_last = bd.mark(V)
                scC_last = bd.mark(SC)
                C_DONE = ([(T, tC_last), (V, vC_last), (SC, scC_last)]
                          + ch_stgC.done() + ch_gate.done())
                bd.emit(block)

            with ExitStack() as sD:
                sbD = lambda n, s, d: sD.enter_context(nc.sbuf_tensor(n, s, d))
                psD = lambda n, s, d: sD.enter_context(nc.psum_tensor(n, s, d))
                xn_sb = sbD("xn_sb", [P, NSC * C], BF16)
                e_buf = [sbD(f"e_buf{i}", [P, 512], BF16) for i in range(2)]
                E_sb = sbD("E_sb", [P, NCC * TCW], BF16)
                xtile = [sbD(f"xtile{i}", [P, C], F32) for i in range(2)]
                gate_t = [sbD(f"gate_t{i}", [P, C], BF16) for i in range(2)]
                osb = sbD("osb", [P, C], F32)
                acc = sbD("acc", [P, C], F32)
                scale_sb = sbD("scale_sb", [P, 2], F32)
                zrow = acc[0:1, 0:512]
                sc_ps = [psD(f"sc_ps{i}", [P, 512], F32) for i in range(2)]
                E_ps = psD("E_ps", [P, NCC * TCW], F32)
                z_ps = psD("z_ps", [P, 512], F32)
                o_ps = psD("o_ps", [P, 512], F32)

                xn_spec = []
                for q in range(4):
                    xn_spec.append(ch_big.issue("scalar", lambda e, q=q: e.dma_start(
                        out=xn_sb[:].rearrange("p (si c) -> p si c", c=C)[:, 8 * q:8 * (q + 1), :],
                        in_=xntok_d.rearrange("(si p) c -> p si c", p=P)[:, 8 * q:8 * (q + 1), :]),
                        waits=(C_DONE + ch_tok.done()) if q == 0 else ()))

                bd.at("vector", lambda e: e.max(out=bmax_f[:], in_=crow_bf[:]),
                      waits=[(SC, sc_crow[3])], incs=[(V, 1)])
                v_bmax = bd.mark(V)
                bd.at("vector", lambda e: e.tensor_scalar(
                    bneg_bf[:], bmax_f[0:1, 0:1], -0.8, None, ALU.mult),
                    waits=[(V, v_bmax)], incs=[(V, 1)])
                v_bneg = bd.mark(V)
                bd.at("tensor", lambda e: e.matmul(
                    sc_ps[0][:, 0:1], ones1_b[:], bneg_bf[:],
                    start=True, stop=True, skip_group_check=True),
                    waits=[(V, v_bneg), (V, vC_last), (SC, scC_last)],
                    incs=[(T, 1)])
                t_bb = bd.mark(T)
                bd.at("vector", lambda e: e.tensor_copy(out=bcol_sb[:], in_=sc_ps[0][:, 0:1]),
                      waits=[(T, t_bb)], incs=[(V, 1)])
                v_bcol = bd.mark(V)

                t_pair = [[0] * 16 for _ in range(NTC)]
                sc_exp = [[0] * 16 for _ in range(NTC)]
                t_EZ = [[0] * 16 for _ in range(NTC)]
                t_z = [0] * NTC
                v_Eev = [0] * NTC
                v_Eevb = [[0] * 4 for _ in range(NTC)]
                t_scmm = [0] * NTC
                v_scev = [0] * NTC
                v_zrow = [0] * NTC
                t_o = [[0] * 4 for _ in range(NTC)]
                sc_oev = [[0] * 4 for _ in range(NTC)]
                v_comb = [[0] * 2 for _ in range(NTC)]
                xt_spec = [[None] * 2 for _ in range(NTC)]
                gt_spec = [[None] * 2 for _ in range(NTC)]
                y_spec = [[None] * 2 for _ in range(NTC)]

                def emit_scores(tc, p):
                    w = [(V, vC_last), (SC, scC_last)]
                    if tc == 0 and p == 0:
                        w.append((V, v_bcol))
                    if p >= 2:
                        w.append((SC, sc_exp[tc][p - 2]))
                    elif tc >= 1:
                        w.append((SC, sc_exp[tc - 1][14 + p]))
                        if p == 1:
                            w.append((V, v_scev[tc - 1]))
                    for hi in range(2):
                        si = 2 * p + hi
                        for dc in range(NCC):
                            bd.at("tensor", lambda e, tc=tc, p=p, hi=hi, si=si, dc=dc: e.matmul(
                                sc_ps[p % 2][:, 256 * hi:256 * (hi + 1)],
                                f_sb[:, S * dc + P * si:S * dc + P * (si + 1)],
                                g_sb[:, R * dc + TCW * tc:R * dc + TCW * (tc + 1)],
                                start=(hi == 0 and dc == 0), stop=(dc == NCC - 1),
                                skip_group_check=True),
                                waits=w if (hi == 0 and dc == 0) else (),
                                incs=[(T, 1)] if (hi == 1 and dc == NCC - 1) else ())
                    t_pair[tc][p] = bd.mark(T)

                def emit_exp(tc, p):
                    ew = [(T, t_pair[tc][p])]
                    if tc == 0 and p == 0:
                        ew.append((V, v_bcol))
                    bd.at("scalar", lambda e, tc=tc, p=p: e.activation(
                        e_buf[p % 2][:], sc_ps[p % 2][:], AF.Exp, bias=bcol_sb[:]),
                        waits=ew, incs=[(SC, 1)])
                    sc_exp[tc][p] = bd.mark(SC)

                def emit_EZ(tc, p, last=False):
                    w = [(SC, sc_exp[tc][p]), xn_spec[(2 * p) // 8], xn_spec[(2 * p + 1) // 8]]
                    if p == 0 and tc >= 1:
                        w += [(V, v_Eev[tc - 1])]
                    zw = ()
                    if p == 0 and tc >= 1:
                        zw = [(SC, sc_oev[tc - 1][3])]

                    def z_mm():
                        bd.at("tensor", lambda e, p=p: e.matmul(
                            z_ps[0:1, 0:512], onescol_b[:], e_buf[p % 2][:, 0:512],
                            start=(p == 0), stop=(p == 15), skip_group_check=True),
                            waits=zw, incs=[(T, 1)])
                    if last:
                        bd.at("tensor", None, waits=w)
                        z_mm()
                        t_z[tc] = bd.mark(T)
                    for hi in range(2):
                        si = 2 * p + hi
                        for cc in range(NCC):
                            bd.at("tensor", lambda e, si=si, cc=cc, p=p: e.matmul(
                                E_ps[:, TCW * cc:TCW * (cc + 1)],
                                xn_sb[:, C * si + P * cc:C * si + P * (cc + 1)],
                                e_buf[p % 2][:, 256 * hi:256 * (hi + 1)],
                                start=(si == 0 and cc % 2 == 0), stop=(si == NSC - 1),
                                skip_group_check=True),
                                waits=w if (hi == 0 and cc == 0) else (),
                                incs=[(T, 1)] if (last and hi == 1 and cc % 2 == 1) else ())
                            if last and hi == 1 and cc % 2 == 1:
                                b = cc // 2
                                tm = bd.mark(T)
                                bd.at("vector", lambda e, b=b: e.tensor_copy(
                                    out=E_sb[:, 512 * b:512 * (b + 1)],
                                    in_=E_ps[:, 512 * b:512 * (b + 1)]),
                                    waits=[(T, tm)], incs=[(V, 1)])
                                v_Eevb[tc][b] = bd.mark(V)
                    if not last:
                        z_mm()
                    else:
                        v_Eev[tc] = bd.mark(V)
                    t_EZ[tc][p] = bd.mark(T)

                for tc in range(NTC):
                    for tsub in range(2):
                        row = TCW * tc + P * tsub
                        xw = [(V, v_comb[tc - 1][tsub])] if tc >= 1 else list(C_DONE)
                        xt_spec[tc][tsub] = ch_xt.issue("sync", lambda e, row=row, tsub=tsub: e.dma_start(
                            out=xtile[tsub][:], in_=x_d[row:row + P, :]), waits=xw)
                        gw = ch_gate.done()
                        if tc >= 1:
                            gw = gw + [(V, v_comb[tc - 1][tsub])]
                        else:
                            gw = gw + C_DONE
                        gt_spec[tc][tsub] = ch_gt.issue("sync", lambda e, row=row, tsub=tsub: e.dma_start(
                            out=gate_t[tsub][:], in_=gate_d[row:row + P, :]), waits=gw)

                    emit_scores(tc, 0)
                    emit_exp(tc, 0)
                    for p in range(1, 16):
                        emit_scores(tc, p)
                        emit_exp(tc, p)
                        emit_EZ(tc, p - 1)
                    emit_EZ(tc, 15, last=True)

                    bd.at("vector", lambda e: e.tensor_copy(out=zrow[:], in_=z_ps[0:1, 0:512]),
                          waits=[(T, t_z[tc])], incs=[(V, 1)])
                    v_zc = bd.mark(V)
                    bd.at("vector", lambda e: e.tensor_tensor(
                        zrow[0:1, 0:256], zrow[0:1, 0:256], zrow[0:1, 256:512], ALU.add),
                        waits=[(V, v_zc)], incs=[(V, 1)])
                    v_zf = bd.mark(V)
                    bd.at("vector", lambda e: e.reciprocal(zrow[0:1, 0:256], zrow[0:1, 0:256]),
                          waits=[(V, v_zf)], incs=[(V, 1)])
                    v_zr = bd.mark(V)
                    bd.at("vector", lambda e: e.tensor_scalar(
                        zrow[0:1, 0:256], zrow[0:1, 0:256], gam_sb[0:1, 0:1], None, ALU.mult),
                        waits=[(V, v_zr)], incs=[(V, 1)])
                    v_zrow[tc] = bd.mark(V)

                    for tsub in range(2):
                        for dh in range(2):
                            gidx = tsub * 2 + dh
                            ob = o_ps[:, 0:512] if gidx % 2 == 0 else z_ps[:, 0:512]
                            ow = [g_wv]
                            if gidx == 0:
                                ow.append((V, v_Eevb[tc][0]))
                            else:
                                ow.append((V, v_Eev[tc]))
                            if gidx == 1:
                                ow.append((V, v_zc))
                            if gidx >= 2:
                                ow.append((SC, sc_oev[tc][gidx - 2]))
                            elif tc >= 1:
                                ow.append((SC, sc_oev[tc - 1][gidx + 2]))
                            for cc in range(NCC):
                                cw = ow if cc == 0 else (
                                    [(V, v_Eevb[tc][cc // 2])]
                                    if (gidx == 0 and cc % 2 == 0) else ())
                                bd.at("tensor", lambda e, tsub=tsub, dh=dh, cc=cc, ob=ob: e.matmul(
                                    ob,
                                    E_sb[:, TCW * cc + P * tsub:TCW * cc + P * (tsub + 1)],
                                    wv_sb[:, C * cc + 512 * dh:C * cc + 512 * (dh + 1)],
                                    start=(cc == 0), stop=(cc == NCC - 1),
                                    skip_group_check=True),
                                    waits=cw,
                                    incs=[(T, 1)] if cc == NCC - 1 else ())
                            t_o[tc][gidx] = bd.mark(T)
                            if gidx == 0:
                                bd.at("tensor", lambda e: e.matmul(
                                    sc_ps[1][:, 0:1], zrow[0:1, 0:P], one11_f[:],
                                    start=True, stop=True, skip_group_check=True),
                                    waits=[(V, v_zrow[tc]), (SC, sc_exp[tc][15])])
                                bd.at("tensor", lambda e: e.matmul(
                                    sc_ps[1][:, 1:2], zrow[0:1, P:TCW], one11_f[:],
                                    start=True, stop=True, skip_group_check=True),
                                    incs=[(T, 1)])
                                t_scmm[tc] = bd.mark(T)
                                bd.at("vector", lambda e: e.tensor_copy(
                                    out=scale_sb[:], in_=sc_ps[1][:, 0:2]),
                                    waits=[(T, t_scmm[tc])], incs=[(V, 1)])
                                v_scev[tc] = bd.mark(V)
                            oew = [(T, t_o[tc][gidx]), (V, v_scev[tc])]
                            if dh == 0 and tsub == 0 and tc >= 1:
                                oew.append(y_spec[tc - 1][1])
                            if dh == 0 and tsub == 1:
                                oew.append(y_spec[tc][0])
                            bd.at("scalar", lambda e, tsub=tsub, dh=dh, ob=ob: e.activation(
                                osb[:, 512 * dh:512 * (dh + 1)], ob, AF.Identity,
                                scale=scale_sb[:, tsub:tsub + 1]),
                                waits=oew, incs=[(SC, 1)])
                            sc_oev[tc][gidx] = bd.mark(SC)
                        ti = 2 * tc + tsub
                        bd.at("vector", lambda e, ti=ti, tsub=tsub: e.tensor_scalar(
                            acc[:], xtile[tsub][:], ms_store[:, 2 * ti:2 * ti + 1],
                            rstd_store[:, ti:ti + 1],
                            ALU.subtract, ALU.mult),
                            waits=[xt_spec[tc][tsub]], incs=[(V, 1)])
                        v_c1 = bd.mark(V)
                        bd.at("vector", lambda e: e.tensor_tensor(acc[:], acc[:], osb[:], ALU.add),
                              waits=[(SC, sc_oev[tc][tsub * 2 + 1]), (V, v_c1)], incs=[(V, 1)])
                        v_c2 = bd.mark(V)
                        bd.at("vector", lambda e, tsub=tsub: e.tensor_tensor(
                            acc[:], acc[:], gate_t[tsub][:], ALU.mult),
                            waits=[gt_spec[tc][tsub], (V, v_c2)], incs=[(V, 1)])
                        v_c3 = bd.mark(V)
                        yw = [(V, v_c3)]
                        if tc >= 1:
                            yw.append(y_spec[tc - 1][tsub])
                        bd.at("vector", lambda e, tsub=tsub: e.tensor_tensor(
                            osb[:], acc[:], xtile[tsub][:], ALU.add),
                            waits=yw, incs=[(V, 1)])
                        v_comb[tc][tsub] = bd.mark(V)
                        row = TCW * tc + P * tsub
                        y_spec[tc][tsub] = ch_y.issue("sync", lambda e, row=row: e.dma_start(
                            out=y_d[row:row + P, :], in_=osb[:]),
                            waits=[(V, v_comb[tc][tsub])])

                if dbg:
                    ch_wB.issue("gpsimd", lambda e: e.dma_start(out=dbg["dbg_xnsb"], in_=xn_sb[:]),
                               waits=ch_big.done())
                    ch_wB.issue("gpsimd", lambda e: e.dma_start(out=dbg["dbg_E"], in_=E_sb[:]),
                               waits=[(V, v_Eev[NTC - 1])])
                    ch_wB.issue("gpsimd", lambda e: e.dma_start(out=dbg["dbg_bb"], in_=bcol_sb[:]),
                               waits=[(V, v_bcol)])
                    ch_wB.issue("gpsimd", lambda e: e.dma_start(out=dbg["dbg_scale"], in_=scale_sb[:]),
                               waits=[(V, v_scev[NTC - 1])])
                    bd.at("gpsimd", None, waits=ch_wB.done())
                bd.at("sync", None, waits=ch_y.done())
                bd.emit(block)

    return nc


_NC_CACHE = {}


def kernel(L0_attn_x, L0_attn_Wq, L0_attn_Wk, L0_attn_Wv, L0_attn_gamma,
           L0_ffn_w_router, L0_ffn_b_router, L0_ffn_w1, L0_ffn_b1,
           L0_ffn_w2, L0_ffn_b2):
    x = np.ascontiguousarray(np.asarray(L0_attn_x, np.float32))
    wqT = np.ascontiguousarray(np.asarray(L0_attn_Wq, np.float32).T)
    wkT = np.ascontiguousarray(np.asarray(L0_attn_Wk, np.float32).T)
    wvT = np.ascontiguousarray(np.asarray(L0_attn_Wv, np.float32).T)
    w1T = np.ascontiguousarray(np.asarray(L0_ffn_w1, np.float32).T)
    w2T = np.ascontiguousarray(np.asarray(L0_ffn_w2, np.float32).T)
    wr = np.ascontiguousarray(np.asarray(L0_ffn_w_router, np.float32))
    br = np.asarray(L0_ffn_b_router, np.float32).reshape(1, NE)
    b1 = np.asarray(L0_ffn_b1, np.float32).reshape(1, H)
    b2 = np.asarray(L0_ffn_b2, np.float32).reshape(1, C)
    gam = np.asarray(L0_attn_gamma, np.float32).reshape(1, 1)
    pat = np.zeros((NE, H), ml_dtypes.bfloat16)
    for e_i in range(NE):
        pat[e_i, 256 * e_i:256 * (e_i + 1)] = -1e30
    ident = np.eye(P).astype(ml_dtypes.bfloat16)

    if "nc" not in _NC_CACHE:
        _NC_CACHE["nc"] = build_nc()
    nc = _NC_CACHE["nc"]

    in_maps = []
    for core in range(8):
        b, half = core // 2, core % 2
        xb = x[b]
        if half == 0:
            x_perm = xb
        else:
            x_perm = np.ascontiguousarray(
                np.concatenate([xb[R:], xb[:R]], axis=0))
        in_maps.append(dict(
            x_perm=x_perm, wqT=wqT, wkT=wkT, wvT=wvT, w1T=w1T, w2T=w2T,
            wr=wr, br=br, b1=b1, b2=b2, gam=gam, pat=pat, ident=ident))

    res = run_bass_kernel_spmd(nc, in_maps, core_ids=list(range(8)))

    out = np.empty((4, S, C), np.float32)
    for core in range(8):
        b, half = core // 2, core % 2
        out[b, R * half:R * (half + 1)] = res.results[core]["y"]
    return out
